# version 22
# baseline (speedup 1.0000x reference)
import numpy as np
import concourse.bass as bass
import concourse.mybir as mybir
from concourse.bass_utils import run_bass_kernel_spmd

F32 = mybir.dt.float32
I32 = mybir.dt.int32
AF = mybir.ActivationFunctionType
ALU = mybir.AluOpType
AX = mybir.AxisListType

D = 1024
T = 2048
H = 16
DH = 64
NPAIR = 8
NTT = T // 128
NDC = D // 128
VST = 80
EPS = 1e-5
NEG = -1.0e9
AV_BATCH = True

TWO_PI_OVER_D = 2.0 * np.pi / 1024.0
TPD_HI = float(np.float32(TWO_PI_OVER_D))
TPD_LO = float(np.float32(TWO_PI_OVER_D - TPD_HI))
HALF_PI = float(np.float32(np.pi / 2))


class Sem:

    def __init__(self, nc, name):
        self.h = nc.alloc_semaphore(name)
        self.n = 0

    def inc(self, inst, k=1):
        inst.then_inc(self.h, k)
        self.n += k
        return self.n

    def wait(self, eng, n=None):
        eng.wait_ge(self.h, self.n if n is None else n)


def build_nc(debug=False):
    nc = bass.Bass("TRN2", target_bir_lowering=False)

    x_hbm = nc.dram_tensor("x", [T, D], F32, kind="ExternalInput").ap()
    wq_hbm = nc.dram_tensor("wq", [H * DH, D], F32, kind="ExternalInput").ap()
    wk_hbm = nc.dram_tensor("wk", [H * DH, D], F32, kind="ExternalInput").ap()
    wv_hbm = nc.dram_tensor("wv", [H * DH, D], F32, kind="ExternalInput").ap()
    wo_hbm = nc.dram_tensor("wo", [D, D], F32, kind="ExternalInput").ap()
    wfr_hbm = nc.dram_tensor("wfr", [128, D], F32, kind="ExternalInput").ap()
    wfi_hbm = nc.dram_tensor("wfi", [128, D], F32, kind="ExternalInput").ap()
    coff_hbm = nc.dram_tensor("coff", [1, 1], F32, kind="ExternalInput").ap()
    bias_hbm = nc.dram_tensor("bias", [1, D], F32, kind="ExternalInput").ap()
    out_hbm = nc.dram_tensor("out", [T, D], F32, kind="ExternalOutput").ap()
    tu_in = nc.dram_tensor("tu_in", [256, D], F32, kind="Internal")
    tu_full = nc.dram_tensor("tu_full", [2048, D], F32, kind="Internal",
                             addr_space="Shared")
    m_in = nc.dram_tensor("m_in", [128, D], F32, kind="Internal")
    m_full = nc.dram_tensor("m_full", [D, D], F32, kind="Internal",
                            addr_space="Shared")

    BASE = 17408

    def sb(name, shape, dtype, off):
        return nc.alloc_sbuf_tensor_at(name, shape, dtype, offset=off).ap()

    ident = sb("ident", [128, 128], F32, BASE)
    maskt = sb("maskt", [128, 128], F32, BASE + 512)
    BF16 = mybir.dt.bfloat16
    VEC0 = BASE + 1024
    vec = [sb(f"vec{i}", [128, 1], F32, VEC0 + 32 * i) for i in range(16)]
    ones_row = sb("ones_row", [1, 128], F32, VEC0 + 512)
    neg1_row = sb("neg1_row", [1, 128], F32, VEC0 + 512 + 512)
    neg64_row = sb("neg64_row", [1, 128], F32, VEC0 + 2112)
    bmx = sb("bmx", [128, 2], F32, VEC0 + 2624)
    coff_s = sb("coff_s", [1, 1], F32, VEC0 + 2656)
    ones2 = sb("ones2", [128, 2], F32, VEC0 + 2688)
    ones_col = sb("ones_col", [128, 1], F32, VEC0 + 1536)
    ident_bf = sb("ident_bf", [128, 128], BF16, VEC0 + 1600)
    maskt_bf = sb("maskt_bf", [128, 128], BF16, VEC0 + 1856)
    bias_bc = sb("bias_bc", [128, D], F32, BASE + 4096)
    wot_hi = sb("wot_hi", [128, D], BF16, BASE + 4096)
    wot_lo = sb("wot_lo", [128, D], BF16, BASE + 4096 + 2048)

    W0 = BASE + 8192
    w = [sb(f"wtmp{i}", [128, D], F32, W0 + 4096 * i) for i in range(4)]

    RA = W0 + 4 * 4096
    RB = RA + 65536
    RC = RB + 65536
    assert RC + 55808 <= 229376, (RC + 55808)

    c_sb = [sb(f"cdft{i}", [128, D], F32, RA + 4096 * i) for i in range(8)]
    s_sb = [sb(f"sdft{i}", [128, D], F32, RA + 32768 + 4096 * i) for i in range(8)]
    lnfm_hi = [sb(f"lnfmh{i}", [128, T], BF16, RA + 8192 * i) for i in range(8)]
    lnfm_lo = [sb(f"lnfml{i}", [128, T], BF16, RA + 8192 * i + 4096) for i in range(8)]
    x1_tm = [sb(f"x1tm{i}", [128, D], F32, RA + 4096 * i) for i in range(16)]

    wrtS = sb("wrtS", [128, D], F32, RB)
    witS = sb("witS", [128, D], F32, RB + 4096)
    acc_tm = [sb(f"acctm{i}", [128, D], F32, RB + 4096 * i) for i in range(16)]
    ln2fm_hi = [sb(f"ln2fmh{i}", [128, T], BF16, RB + 8192 * i) for i in range(8)]
    ln2fm_lo = [sb(f"ln2fml{i}", [128, T], BF16, RB + 8192 * i + 4096)
                for i in range(8)]

    tp_sh = sb("tp_sh", [128, D], F32, RC)
    up_sh = sb("up_sh", [128, D], F32, RC + 4096)
    cbt = [sb(f"cbt{i}", [128, 128], F32, RC + 8192 + 512 * i) for i in range(8)]
    sbt = [sb(f"sbt{i}", [128, 128], F32, RC + 12288 + 512 * i) for i in range(8)]
    tTs = [sb(f"tTs{i}", [128, D], F32, RC + 16384 + 4096 * i) for i in range(2)]
    tUs = [sb(f"tUs{i}", [128, D], F32, RC + 24576 + 4096 * i) for i in range(2)]
    msh = sb("msh", [128, D], F32, RC + 32768)
    gfj = sb("gfj", [128, 128], F32, RC + 36864)
    gfv = sb("gfv", [128, 128], F32, RC + 37376)
    gwk = sb("gwk", [128, 128], F32, RC + 37888)
    gtk = sb("gtk", [128, 128], F32, RC + 38400)
    mvs = sb("mvs", [128, 8], F32, RC + 38912)
    q_hi = sb("q_hi", [128, T], BF16, RC)
    q_lo = sb("q_lo", [128, T], BF16, RC + 4096)
    k_hi = sb("k_hi", [128, T], BF16, RC + 8192)
    k_lo = sb("k_lo", [128, T], BF16, RC + 12288)
    v_hi = sb("v_hi", [128, 16 * 196], BF16, RC + 16384)
    v_lo = sb("v_lo", [128, 16 * 196], BF16, RC + 22656)
    slotA = sb("slotA", [128, 512], F32, RC + 28928)
    hof32 = sb("hof32", [128, 512], F32, RC + 30976)
    pt3 = [sb(f"pt3_{i}", [128, 512], F32, RC + 33024 + 2048 * i)
           for i in range(3)]
    pth = [sb(f"pth{i}", [128, 512], BF16, RC + 39168 + 1024 * i)
           for i in range(3)]
    ptl = [sb(f"ptl{i}", [128, 512], BF16, RC + 42240 + 1024 * i)
           for i in range(3)]
    hofm_hi = sb("hofm_hi", [128, T], BF16, RC + 45312)
    hofm_lo = sb("hofm_lo", [128, T], BF16, RC + 49408)
    sqscr = sb("sqscr", [128, T], F32, RC + 45312)
    wt_hi = sb("wt_hi", [128, D], BF16, RC + 33024)
    wt_lo = sb("wt_lo", [128, D], BF16, RC + 35072)
    ones_pbf = sb("ones_pbf", [128, 128], BF16, RC + 53504)
    rh = sb("rh", [128, 512], BF16, RC + 53760)
    rl = sb("rl", [128, 512], BF16, RC + 54784)
    m_hi = [sb(f"mhi{i}", [128, D], BF16, RC + 2048 * i) for i in range(8)]
    m_lo = [sb(f"mlo{i}", [128, D], BF16, RC + 16384 + 2048 * i)
            for i in range(8)]
    fy = sb("fy", [128, D], F32, RC + 32768)
    fln = sb("fln", [128, D], F32, RC + 36864)
    fout = [sb(f"fout{i}", [128, D], F32, RC + 40960 + 4096 * i) for i in range(2)]
    fsq = sb("fsq", [128, D], F32, RC + 49152)

    ps_sc = nc.alloc_psum_tensor("ps_sc", [128, 2048], F32).ap()
    ps_t0 = nc.alloc_psum_tensor("ps_t0", [128, 512], F32).ap()
    ps_t1 = nc.alloc_psum_tensor("ps_t1", [128, 512], F32).ap()
    ps_av = nc.alloc_psum_tensor("ps_av", [128, 512], F32).ap()
    ps_x = nc.alloc_psum_tensor("ps_x", [128, 512], F32).ap()
    ps_t = [ps_t0, ps_t1]

    ld = Sem(nc, "ld")
    sg = Sem(nc, "sg")
    tm = Sem(nc, "tm")
    ve = Sem(nc, "ve")
    sc = Sem(nc, "sc")
    gp = Sem(nc, "gp")

    tens, vect, scal, gps, sync = nc.tensor, nc.vector, nc.scalar, nc.gpsimd, nc.sync

    def tap(name, ap):
        if not debug:
            return
        d = nc.dram_tensor("tap_" + name, list(ap.shape), F32,
                           kind="ExternalOutput").ap()
        gps.dma_start(out=d, in_=ap).then_inc(sg.h, 16)
        sg.n += 16
        gps.wait_ge(sg.h, sg.n)

    gp.inc(gps.memset(vec[7][:, :], HALF_PI))
    gp.inc(gps.memset(vec[8][:, :], EPS))
    gp.inc(gps.memset(ones_row[:, :], 1.0))
    gp.inc(gps.memset(neg1_row[:, :], -1.0))
    gp.inc(gps.memset(neg64_row[:, :], -64.0))
    gp.inc(gps.memset(ones_col[:, :], 1.0))
    gp.inc(gps.memset(ones2[:, :], 0.0))
    gp.inc(gps.memset(ones_pbf[:, :], 1.0))
    gp.inc(gps.iota(w[2][:, 0:128], pattern=[[-1, 128]], base=0,
                    channel_multiplier=1,
                    allow_small_or_imprecise_dtypes=True))
    gp.wait(vect)
    ve.inc(vect.tensor_scalar(ident[:, :], w[2][:, 0:128], 0.0, None,
                              op0=ALU.is_equal))
    vect.wait_ge(ve.h, ve.n)
    gp.inc(gps.memset(maskt[:, :], 0.0))
    gps.wait_ge(gp.h, gp.n)
    gp.inc(gps.affine_select(maskt[:, :], maskt[:, :], pattern=[[1, 128]],
                             compare_op=ALU.is_ge, fill=NEG, base=0,
                             channel_multiplier=-1))
    gp.inc(gps.memset(ones2[0:64, 0:1], 1.0))
    gp.inc(gps.memset(ones2[64:128, 1:2], 1.0))
    gp.wait(vect)
    ve.inc(vect.tensor_copy(ident_bf[:, :], ident[:, :]))
    ve.inc(vect.tensor_copy(maskt_bf[:, :], maskt[:, :]))
    gps.wait_ge(ve.h, ve.n)
    gp.inc(gps.iota(w[2][:, :], pattern=[[1, D]], base=0, channel_multiplier=0,
                    allow_small_or_imprecise_dtypes=True))
    for dc in range(NDC):
        if dc > 0:
            gps.wait_ge(ve.h, ve.n)
        gp.inc(gps.iota(vec[0][:, :], pattern=[[0, 1]], base=dc * 128,
                        channel_multiplier=1,
                        allow_small_or_imprecise_dtypes=True))
        gp.wait(vect)
        if dc > 0:
            vect.wait_ge(sc.h, sc.n)
        def _step(inst):
            ve.inc(inst)
            vect.wait_ge(ve.h, ve.n)
        _step(vect.tensor_scalar(w[3][:, :], w[2][:, :], vec[0][:, :], None,
                                 op0=ALU.mult))
        _step(vect.tensor_copy(c_sb[dc][:, :].bitcast(I32), w[3][:, :]))
        _step(vect.tensor_scalar(c_sb[dc][:, :].bitcast(I32),
                                 c_sb[dc][:, :].bitcast(I32), 1023, None,
                                 op0=ALU.bitwise_and))
        _step(vect.tensor_copy(w[3][:, :], c_sb[dc][:, :].bitcast(I32)))
        _step(vect.tensor_scalar(c_sb[dc][:, :], w[3][:, :], 511.5, None,
                                 op0=ALU.is_gt))
        _step(vect.scalar_tensor_tensor(w[3][:, :], c_sb[dc][:, :], -1024.0,
                                        w[3][:, :], op0=ALU.mult, op1=ALU.add))
        _step(vect.tensor_scalar(s_sb[dc][:, :], w[3][:, :], TPD_HI, None,
                                 op0=ALU.mult))
        ve.inc(vect.scalar_tensor_tensor(w[3][:, :], w[3][:, :], TPD_LO,
                                         s_sb[dc][:, :], op0=ALU.mult, op1=ALU.add))
        ve.wait(scal)
        sc.inc(scal.activation(s_sb[dc][:, :], w[3][:, :], AF.Sin))
        sc.inc(scal.activation(w[3][:, :], w[3][:, :], AF.Abs))
        scal.wait_ge(sc.h, sc.n)
        sc.inc(scal.activation(c_sb[dc][:, :], w[3][:, :], AF.Sin,
                               bias=vec[7][:, :], scale=-1.0))
    nc.all_engine_barrier()
    tap("c0", c_sb[0][:, :]); tap("s0", s_sb[0][:, :])
    tap("ident", ident[:, :]); tap("maskt", maskt[:, :])
    nc.all_engine_barrier()

    sync.dma_start(out=w[2][:, :], in_=wfr_hbm[:, :]).then_inc(ld.h, 16)
    sync.dma_start(out=w[3][:, :], in_=wfi_hbm[:, :]).then_inc(ld.h, 16)
    sync.dma_start(out=coff_s[:, :], in_=coff_hbm[:, :]).then_inc(ld.h, 16)
    ld.n += 48
    tens.wait_ge(ld.h, ld.n)
    p1rec = {}
    for a in range(8):
        pp = ps_t[a % 2]
        if a >= 2:
            tens.wait_ge(ve.h, p1rec[a - 2])
        tm.inc(tens.transpose(pp[:, 0:128], w[2][:, a * 128:(a + 1) * 128],
                              ident[:, :]))
        tm.inc(tens.transpose(pp[:, 128:256], w[3][:, a * 128:(a + 1) * 128],
                              ident[:, :]))
        tm.wait(vect)
        ve.inc(vect.tensor_copy(wrtS[:, a * 128:(a + 1) * 128], pp[:, 0:128]))
        ve.inc(vect.tensor_copy(witS[:, a * 128:(a + 1) * 128],
                                pp[:, 128:256]))
        p1rec[a] = ve.n
    tm.inc(tens.matmul(ps_x[:, 0:1], ones_row[:, :], coff_s[0:1, 0:1],
                       start=True, stop=True))
    tm.wait(vect)
    ve.inc(vect.tensor_copy(vec[1][:, :], ps_x[:, 0:1]))
    nc.all_engine_barrier()
    tap("wrt0", wrtS[:, :]); tap("wit0", witS[:, :])
    nc.all_engine_barrier()

    gp.inc(gps.iota(gfj[:, :], pattern=[[1, 128]], base=0,
                    channel_multiplier=0, allow_small_or_imprecise_dtypes=True))
    gp.inc(gps.iota(mvs[:, :], pattern=[[128, 8]], base=0,
                    channel_multiplier=1, allow_small_or_imprecise_dtypes=True))
    iota_ms = gp.n
    gp.wait(vect)
    vect.wait_ge(ve.h, ve.n)
    ve.inc(vect.tensor_scalar(gfv[:, :], gfj[:, :], vec[1][:, :], None,
                              op0=ALU.add))
    vect.wait_ge(ve.h, ve.n)
    for mc in range(8):
        if mc > 0:
            vect.wait_ge(sc.h, sc.n)

        def _gstep(inst):
            ve.inc(inst)
            vect.wait_ge(ve.h, ve.n)
        _gstep(vect.tensor_scalar(gwk[:, :], gfv[:, :], mvs[:, mc:mc + 1],
                                  None, op0=ALU.mult))
        _gstep(vect.tensor_copy(gtk[:, :].bitcast(I32), gwk[:, :]))
        _gstep(vect.tensor_scalar(gtk[:, :].bitcast(I32),
                                  gtk[:, :].bitcast(I32), 1023, None,
                                  op0=ALU.bitwise_and))
        _gstep(vect.tensor_copy(gwk[:, :], gtk[:, :].bitcast(I32)))
        _gstep(vect.tensor_scalar(gtk[:, :], gwk[:, :], 511.5, None,
                                  op0=ALU.is_gt))
        _gstep(vect.scalar_tensor_tensor(gwk[:, :], gtk[:, :], -1024.0,
                                         gwk[:, :], op0=ALU.mult, op1=ALU.add))
        _gstep(vect.tensor_scalar(gtk[:, :], gwk[:, :], TPD_HI, None,
                                  op0=ALU.mult))
        ve.inc(vect.scalar_tensor_tensor(gwk[:, :], gwk[:, :], TPD_LO,
                                         gtk[:, :], op0=ALU.mult, op1=ALU.add))
        ve.wait(scal)
        sc.inc(scal.activation(sbt[mc][:, :], gwk[:, :], AF.Sin))
        sc.inc(scal.activation(gwk[:, :], gwk[:, :], AF.Abs))
        scal.wait_ge(sc.h, sc.n)
        sc.inc(scal.activation(cbt[mc][:, :], gwk[:, :], AF.Sin,
                               bias=vec[7][:, :], scale=-1.0))
    gen_sc = sc.n

    def stage12s(dst, rhs_a, rhs_b, sign_b):
        tens.wait_ge(ve.h, p1rec[7])
        for jh in range(2):
            pa = ps_sc[:, jh * 1024:jh * 1024 + 512]
            pb = ps_sc[:, jh * 1024 + 512:jh * 1024 + 1024]
            for oc in range(8):
                tens.matmul(pa, wrtS[:, oc * 128:(oc + 1) * 128],
                            rhs_a[oc][:, jh * 512:(jh + 1) * 512],
                            start=(oc == 0), stop=(oc == 7))
            for oc in range(8):
                st = tens.matmul(pb, witS[:, oc * 128:(oc + 1) * 128],
                                 rhs_b[oc][:, jh * 512:(jh + 1) * 512],
                                 start=(oc == 0), stop=(oc == 7))
            tm.inc(st)
            tm.wait(vect)
            ve.inc(vect.tensor_scalar(w[2][:, jh * 512:(jh + 1) * 512], pb,
                                      float(sign_b), None, op0=ALU.mult))
            vect.wait_ge(ve.h, ve.n)
            ve.inc(vect.tensor_tensor(dst[:, jh * 512:(jh + 1) * 512], pa,
                                      w[2][:, jh * 512:(jh + 1) * 512],
                                      op=ALU.add))
            tens.wait_ge(ve.h, ve.n)

    stage12s(tp_sh, c_sb, s_sb, -1)
    stage12s(up_sh, s_sb, c_sb, +1)
    ve.wait(gps)
    gps.wait_ge(sg.h, sg.n)
    gps.dma_start(out=tu_in.ap()[0:128, :], in_=tp_sh[:, :]).then_inc(sg.h, 16)
    gps.dma_start(out=tu_in.ap()[128:256, :], in_=up_sh[:, :]).then_inc(sg.h, 16)
    sg.n += 32
    gps.wait_ge(sg.h, sg.n)
    gps.wait_ge(gp.h, iota_ms)
    gp.inc(gps.collective_compute(
        "AllGather", ALU.bypass, replica_groups=[list(range(8))],
        ins=[tu_in.ap().opt()], outs=[tu_full.ap().opt()]))
    cc1 = gp.n
    nc.all_engine_barrier()
    tap("tp0", tp_sh[:, :]); tap("up0", up_sh[:, :])
    nc.all_engine_barrier()

    sync.wait_ge(ld.h, ld.n)
    sync.wait_ge(gp.h, cc1)
    gps.wait_ge(gp.h, cc1)
    ld_ms, sg_ms, mm_ms = {}, {}, {}

    def _tu_load(mc):
        eng, sem, ms = (sync, ld, ld_ms) if mc % 2 == 0 else (gps, sg, sg_ms)
        if mc >= 2:
            eng.wait_ge(tm.h, mm_ms[mc - 2])
        eng.dma_start(out=tTs[mc % 2][:, :],
                      in_=tu_full.ap()[mc * 256:mc * 256 + 128, :]
                      ).then_inc(sem.h, 16)
        eng.dma_start(out=tUs[mc % 2][:, :],
                      in_=tu_full.ap()[mc * 256 + 128:mc * 256 + 256, :]
                      ).then_inc(sem.h, 16)
        sem.n += 32
        ms[mc] = sem.n

    _tu_load(0)
    _tu_load(1)
    tens.wait_ge(sc.h, gen_sc)
    tens.wait_ge(ve.h, ve.n)
    for mc in range(8):
        sem, ms = (ld, ld_ms) if mc % 2 == 0 else (sg, sg_ms)
        tens.wait_ge(sem.h, ms[mc])
        for jh in range(2):
            tens.matmul(ps_t[jh][:, :], cbt[mc][:, :],
                        tTs[mc % 2][:, jh * 512:(jh + 1) * 512],
                        start=(mc == 0), stop=False)
            st = tens.matmul(ps_t[jh][:, :], sbt[mc][:, :],
                             tUs[mc % 2][:, jh * 512:(jh + 1) * 512],
                             start=False, stop=(mc == 7))
        tm.inc(st)
        mm_ms[mc] = tm.n
        if mc + 2 < 8:
            _tu_load(mc + 2)
    tm.wait(scal)
    for jh in range(2):
        sc.inc(scal.mul(msh[:, jh * 512:(jh + 1) * 512], ps_t[jh][:, :],
                        1.0 / 1024.0))
    sc.wait(gps)
    gps.wait_ge(sg.h, sg.n)
    gps.dma_start(out=m_in.ap()[:, :], in_=msh[:, :]).then_inc(sg.h, 16)
    sg.n += 16
    gps.wait_ge(sg.h, sg.n)
    gp.inc(gps.collective_compute(
        "AllGather", ALU.bypass, replica_groups=[list(range(8))],
        ins=[m_in.ap().opt()], outs=[m_full.ap().opt()]))
    cc2 = gp.n
    gps.wait_ge(gp.h, cc2)
    nc.all_engine_barrier()
    gp.inc(gps.memset(v_hi[:, :], 1.0))
    gp.inc(gps.memset(v_lo[:, :], 0.0))
    vall_ms = gp.n
    gps.wait_ge(gp.h, vall_ms)

    VS_EVEN = (0, 1, 2, 3, 4)
    VS_ODD = (5, 6, 9, 10, 11)

    def layernorm(src, dst, sq, vs=VS_EVEN, pre=None):
        rsum, negmean, sumsq, stdv, rstd = (vec[i] for i in vs)
        if pre is None:
            vect.wait_ge(sc.h, sc.n)
            vect.wait_ge(ve.h, ve.n)
            scal.wait_ge(ve.h, ve.n)
        else:
            for (h_, v_) in pre:
                vect.wait_ge(h_, v_)
                scal.wait_ge(h_, v_)
        ve.inc(vect.tensor_reduce(rsum[:, :], src, axis=AX.X, op=ALU.add))
        rsum_done = ve.n
        scal.wait_ge(ve.h, rsum_done)
        sc.inc(scal.mul(negmean[:, :], rsum[:, :], -1.0 / D))
        scal.wait_ge(sc.h, sc.n)
        sc.inc(scal.add(dst, src, negmean[:, :]))
        scal.wait_ge(sc.h, sc.n)
        sc.inc(scal.activation(sq, dst, AF.Square, accum_out=sumsq[:, :]))
        scal.wait_ge(sc.h, sc.n)
        sc.inc(scal.activation(stdv[:, :], sumsq[:, :], AF.Sqrt,
                               bias=vec[8][:, :], scale=1.0 / D))
        stdv_done = sc.n
        vect.wait_ge(sc.h, stdv_done)
        ve.inc(vect.reciprocal(rstd[:, :], stdv[:, :]))
        vect.wait_ge(ve.h, ve.n)
        ve.inc(vect.tensor_scalar(dst, dst, rstd[:, :], None, op0=ALU.mult))
        return ve.n

    for tt in range(2):
        sync.dma_start(out=w[tt % 2][:, :],
                       in_=x_hbm[tt * 128:(tt + 1) * 128, :]).then_inc(ld.h, 16)
        ld.n += 16
    lnA = [sb("lnA0", [128, D], F32, RC), sb("lnA1", [128, D], F32, RC + 4096)]
    ln_done = {}
    tr_done = {}
    ev_A = {}

    def a_stage2(tt):
        tens.wait_ge(ve.h, ln_done[tt])
        base = (tt % 2) * 1024
        if tt - 2 in ev_A:
            tens.wait_ge(ve.h, ev_A[tt - 2])
        for dc in range(NDC):
            tm.inc(tens.transpose(
                ps_sc[:, base + dc * 128:base + (dc + 1) * 128],
                lnA[tt % 2][:, dc * 128:(dc + 1) * 128], ident[:, :]))
        tm.wait(scal)
        sl = slice(tt * 128, (tt + 1) * 128)
        for dc in range(NDC):
            sc.inc(scal.copy(lnfm_hi[dc][:, sl],
                             ps_sc[:, base + dc * 128:base + (dc + 1) * 128]))
        sc.wait(vect)
        for dc in range(NDC):
            ve.inc(vect.tensor_tensor(
                lnfm_lo[dc][:, sl],
                ps_sc[:, base + dc * 128:base + (dc + 1) * 128],
                lnfm_hi[dc][:, sl], op=ALU.subtract))
        ev_A[tt] = ve.n
        tr_done[tt] = tm.n

    for tt in range(NTT):
        xs = w[tt % 2]
        pre = [(ld.h, ld.n)]
        if tt >= 2:
            pre.append((tm.h, tr_done[tt - 2]))
        if tt >= 1:
            a_stage2(tt - 1)
        ln_done[tt] = layernorm(xs[:, :], lnA[tt % 2][:, :], w[3][:, :],
                                VS_EVEN if tt % 2 == 0 else VS_ODD, pre=pre)
        if tt + 2 < NTT:
            sync.wait_ge(ve.h, ln_done[tt])
            sync.dma_start(out=xs[:, :],
                           in_=x_hbm[(tt + 2) * 128:(tt + 3) * 128, :]
                           ).then_inc(ld.h, 16)
            ld.n += 16
    a_stage2(NTT - 1)
    nc.all_engine_barrier()
    tap("ln1fmh0", lnfm_hi[0][:, :])
    nc.all_engine_barrier()

    def load_pair_weights(p):
        sync.dma_start(out=w[0][:, :],
                       in_=wq_hbm[p * 128:(p + 1) * 128, :]).then_inc(ld.h, 16)
        sync.dma_start(out=w[1][:, :],
                       in_=wk_hbm[p * 128:(p + 1) * 128, :]).then_inc(ld.h, 16)
        sync.dma_start(out=w[2][:, :],
                       in_=wv_hbm[p * 128:(p + 1) * 128, :]).then_inc(ld.h, 16)
        sync.dma_start(
            out=w[3][:, :].rearrange("p (a c) -> p a c", a=8),
            in_=wo_hbm[:, p * 128:(p + 1) * 128].rearrange(
                "(a p) c -> p a c", p=128)).then_inc(ld.h, 16)
        ld.n += 64
        return ld.n

    SB4 = [ps_t0, ps_t1, ps_av, ps_x]
    AVS = {"n": 0, "meta": {}, "sctm": {}, "exp": {}, "msk": {}, "avtm": {},
           "spl": {}, "bfree": {}, "pend": [], "pend2": [], "eva": {},
           "ev": {}, "r": 0, "slotrec": {}, "sbslot": {}}

    def sb4_take():
        slot = AVS["r"] % 4
        AVS["r"] += 1
        rec = AVS["slotrec"].get(slot)
        if rec is not None:
            kind, val = rec
            if kind == "exp":
                tens.wait_ge(sc.h, val)
            else:
                tens.wait_ge(ve.h, val)
        return slot

    def av_scores(i):
        c, qg, hh = AVS["meta"][i]
        hsl = slice(hh * 64, (hh + 1) * 64)
        qsl = slice(qg * 512, (qg + 1) * 512)
        csl = slice(c * 128, (c + 1) * 128)
        slot = sb4_take()
        AVS["sbslot"][i] = slot
        sb_ = SB4[slot][:, 0:512]
        tens.matmul(sb_, k_hi[hsl, csl], q_hi[hsl, qsl], start=True,
                    stop=False)
        tens.matmul(sb_, k_hi[hsl, csl], q_lo[hsl, qsl], start=False,
                    stop=False)
        st = tens.matmul(sb_, k_lo[hsl, csl], q_hi[hsl, qsl], start=False,
                         stop=True)
        tm.inc(st)
        AVS["sctm"][i] = tm.n

    def av_exp(i):
        c, qg, hh = AVS["meta"][i]
        scal.wait_ge(tm.h, AVS["sctm"][i])
        if i - 3 in AVS["spl"]:
            scal.wait_ge(ve.h, AVS["spl"][i - 3])
        sc.inc(scal.activation(pt3[i % 3][:, 0:512],
                               SB4[AVS["sbslot"][i]][:, 0:512],
                               AF.Exp, bias=vec[10 + hh][:, :], scale=1.0))
        AVS["exp"][i] = sc.n
        AVS["slotrec"][AVS["sbslot"][i]] = ("exp", sc.n)
        if c // 4 == qg:
            gps.wait_ge(sc.h, AVS["exp"][i])
            gps.wait_ge(gp.h, gp.n)
            gp.inc(gps.affine_select(
                pt3[i % 3][:, 0:512], pt3[i % 3][:, 0:512],
                pattern=[[1, 512]], compare_op=ALU.is_ge, fill=0.0,
                base=-128 * (c - 4 * qg), channel_multiplier=-1))
            AVS["msk"][i] = gp.n

    def av_split(i):
        scal.wait_ge(sc.h, AVS["exp"][i])
        if i in AVS["msk"]:
            scal.wait_ge(gp.h, AVS["msk"][i])
        if i - 3 in AVS["avtm"]:
            scal.wait_ge(tm.h, AVS["avtm"][i - 3])
        sc.inc(scal.copy(pth[i % 3][:, 0:512], pt3[i % 3][:, 0:512]))
        hi_sc = sc.n
        vect.wait_ge(sc.h, hi_sc)
        if i - 3 in AVS["avtm"]:
            vect.wait_ge(tm.h, AVS["avtm"][i - 3])
        ve.inc(vect.tensor_tensor(ptl[i % 3][:, 0:512], pt3[i % 3][:, 0:512],
                                  pth[i % 3][:, 0:512], op=ALU.subtract))
        AVS["spl"][i] = ve.n

    def av_av(i):
        c, qg, hh = AVS["meta"][i]
        tens.wait_ge(ve.h, AVS["spl"][i])
        if c == 0 and qg in AVS["bfree"]:
            bf_sc, bf_ve = AVS["bfree"][qg]
            tens.wait_ge(sc.h, bf_sc)
            tens.wait_ge(ve.h, bf_ve)
        if hh == 0:
            vh = v_hi[:, c * 196:c * 196 + 65]
            vl = v_lo[:, c * 196:c * 196 + 65]
            np_out = 65
        else:
            vh = v_hi[:, c * 196 + 66:c * 196 + 194]
            vl = v_lo[:, c * 196 + 66:c * 196 + 194]
            np_out = 128
        bank = ps_sc[0:np_out, qg * 512:(qg + 1) * 512]
        pth_, ptl_ = pth[i % 3][:, 0:512], ptl[i % 3][:, 0:512]
        tens.matmul(bank, vh, pth_, start=(c == 0), stop=False)
        tens.matmul(bank, vh, ptl_, start=False, stop=False)
        st = tens.matmul(bank, vl, pth_, start=False, stop=(c == 4 * qg + 3))
        tm.inc(st)
        AVS["avtm"][i] = tm.n
        if c == 4 * qg + 3:
            AVS["pend"].append((i, AVS["n"]))

    def av_evac_a(i):
        c, qg, hh = AVS["meta"][i]
        den_p = 64 if hh == 0 else 32
        fsl = slice(hh * 64, (hh + 1) * 64)
        prev = AVS["ev"]
        qpsl = slice(qg * 512, (qg + 1) * 512)
        scal.wait_ge(tm.h, AVS["avtm"][i])
        if "mult" in prev:
            scal.wait_ge(ve.h, prev["mult"])
        if "rhl" in prev:
            scal.wait_ge(ve.h, prev["rhl"])
        sc.inc(scal.copy(slotA[fsl, 0:512], ps_sc[fsl, qpsl]))
        copy_sc = sc.n
        vect.wait_ge(tm.h, AVS["avtm"][i])
        if "bcast" in prev:
            vect.wait_ge(tm.h, prev["bcast"])
        if "mult" in prev:
            vect.wait_ge(ve.h, prev["mult"])
        ve.inc(vect.reciprocal(slotA[den_p:den_p + 1, 0:512],
                               ps_sc[den_p:den_p + 1, qpsl]))
        vect.wait_ge(ve.h, ve.n)
        ve.inc(vect.tensor_copy(rh[den_p:den_p + 1, 0:512],
                                slotA[den_p:den_p + 1, 0:512]))
        vect.wait_ge(ve.h, ve.n)
        ve.inc(vect.tensor_tensor(rl[den_p:den_p + 1, 0:512],
                                  slotA[den_p:den_p + 1, 0:512],
                                  rh[den_p:den_p + 1, 0:512],
                                  op=ALU.subtract))
        rhl_ve = ve.n
        AVS["bfree"][qg] = (copy_sc, rhl_ve)
        AVS["eva"][i] = (rhl_ve, copy_sc)
        prev["rhl"] = rhl_ve

    def av_evac_b(i):
        c, qg, hh = AVS["meta"][i]
        den_p = 64 if hh == 0 else 32
        fsl = slice(hh * 64, (hh + 1) * 64)
        prev = AVS["ev"]
        qpsl = slice(qg * 512, (qg + 1) * 512)
        rhl_ve, copy_sc = AVS["eva"][i]
        slot = sb4_take()
        tens.wait_ge(ve.h, rhl_ve)
        tens.matmul(SB4[slot][:, 0:512], ones_pbf[den_p:den_p + 1, 0:128],
                    rh[den_p:den_p + 1, 0:512], start=True, stop=False)
        st = tens.matmul(SB4[slot][:, 0:512],
                         ones_pbf[den_p:den_p + 1, 0:128],
                         rl[den_p:den_p + 1, 0:512], start=False, stop=True)
        tm.inc(st)
        bcast_tm = tm.n
        vect.wait_ge(tm.h, bcast_tm)
        vect.wait_ge(sc.h, copy_sc)
        if "hi" in prev:
            vect.wait_ge(sc.h, prev["hi"])
        if "lo" in prev:
            vect.wait_ge(ve.h, prev["lo"])
        ve.inc(vect.tensor_tensor(hof32[fsl, 0:512], slotA[fsl, 0:512],
                                  SB4[slot][fsl, 0:512], op=ALU.mult))
        mult_ve = ve.n
        AVS["slotrec"][slot] = ("mult", mult_ve)
        scal.wait_ge(ve.h, mult_ve)
        sc.inc(scal.copy(hofm_hi[fsl, qpsl], hof32[fsl, 0:512]))
        hi_sc = sc.n
        vect.wait_ge(sc.h, hi_sc)
        ve.inc(vect.tensor_tensor(hofm_lo[fsl, qpsl], hof32[fsl, 0:512],
                                  hofm_hi[fsl, qpsl], op=ALU.subtract))
        prev.update({"bcast": bcast_tm, "mult": mult_ve,
                     "hi": hi_sc, "lo": ve.n})

    def av_drain(force=False):
        if force:
            while AVS["pend2"]:
                i, _ = AVS["pend2"].pop(0)
                av_evac_b(i)
            while AVS["pend"]:
                i, _ = AVS["pend"].pop(0)
                av_evac_a(i)
                av_evac_b(i)
            return
        while AVS["pend"] and AVS["pend"][0][1] <= AVS["n"] - 1:
            i, _ = AVS["pend"].pop(0)
            av_evac_a(i)
            AVS["pend2"].append((i, AVS["n"]))
        while AVS["pend2"] and AVS["pend2"][0][1] <= AVS["n"] - 2:
            i, _ = AVS["pend2"].pop(0)
            av_evac_b(i)

    wrec = load_pair_weights(0)
    accrec = {}
    for p in range(NPAIR):
        tens.wait_ge(ld.h, wrec)

        def transpose_split(nat, dst_hi, dst_lo):
            tens.wait_ge(ve.h, ve.n)
            tens.wait_ge(sc.h, sc.n)
            for half in range(2):
                pp = ps_t[half]
                for a4 in range(4):
                    a = half * 4 + a4
                    tm.inc(tens.transpose(pp[:, a4 * 128:(a4 + 1) * 128],
                                          nat[:, a * 128:(a + 1) * 128],
                                          ident[:, :]))
                tm.wait(scal)
                sl = slice(half * 512, (half + 1) * 512)
                sc.inc(scal.copy(dst_hi[:, sl], pp[:, 0:512]))
                sc.wait(vect)
                ve.inc(vect.tensor_tensor(dst_lo[:, sl], pp[:, 0:512],
                                          dst_hi[:, sl], op=ALU.subtract))

        evq = {}
        for side, nat in enumerate((w[0], w[1])):
            transpose_split(nat, wt_hi, wt_lo)
            ve.wait(tens)
            dst_hi, dst_lo = (q_hi, q_lo) if side == 0 else (k_hi, k_lo)
            for tc in range(4):
                g = side * 4 + tc
                pp = ps_sc[:, (g % 2) * 512:(g % 2) * 512 + 512]
                tsl = slice(tc * 512, (tc + 1) * 512)
                if g >= 2:
                    tens.wait_ge(ve.h, evq[g - 2])
                for dc in range(NDC):
                    dsl = slice(dc * 128, (dc + 1) * 128)
                    tens.matmul(pp, wt_hi[:, dsl], lnfm_hi[dc][:, tsl],
                                start=(dc == 0), stop=False)
                    tens.matmul(pp, wt_hi[:, dsl], lnfm_lo[dc][:, tsl],
                                start=False, stop=False)
                    st = tens.matmul(pp, wt_lo[:, dsl], lnfm_hi[dc][:, tsl],
                                     start=False, stop=(dc == NDC - 1))
                tm.inc(st)
                tm.wait(scal)
                if side == 0:
                    sc.inc(scal.mul(dst_hi[:, tsl], pp, 0.125))
                    sc.wait(vect)
                    ve.inc(vect.scalar_tensor_tensor(
                        dst_lo[:, tsl], pp, 0.125, dst_hi[:, tsl],
                        op0=ALU.mult, op1=ALU.subtract))
                else:
                    sc.inc(scal.copy(dst_hi[:, tsl], pp))
                    sc.wait(vect)
                    ve.inc(vect.tensor_tensor(dst_lo[:, tsl], pp,
                                              dst_hi[:, tsl], op=ALU.subtract))
                evq[g] = ve.n

        transpose_split(w[2], wt_hi, wt_lo)
        ve.wait(tens)
        evv = {}
        evs = {}
        for tt in range(NTT):
            pp = ps_t[tt % 2]
            if tt >= 2:
                tens.wait_ge(ve.h, evv[tt - 2])
                tens.wait_ge(sc.h, evs[tt - 2])
            for dc in range(NDC):
                dsl = slice(dc * 128, (dc + 1) * 128)
                tsl = slice(tt * 128, (tt + 1) * 128)
                tens.matmul(pp[:, 0:128], lnfm_hi[dc][:, tsl], wt_hi[:, dsl],
                            start=(dc == 0), stop=False)
                tens.matmul(pp[:, 0:128], lnfm_hi[dc][:, tsl], wt_lo[:, dsl],
                            start=False, stop=False)
                st = tens.matmul(pp[:, 0:128], lnfm_lo[dc][:, tsl],
                                 wt_hi[:, dsl], start=False,
                                 stop=(dc == NDC - 1))
            tm.inc(st)
            tm.wait(scal)
            tm.wait(vect)
            if p == 0 and tt == 0:
                scal.wait_ge(gp.h, vall_ms)
                vect.wait_ge(gp.h, vall_ms)
            sc.inc(scal.copy(v_hi[:, tt * 196:tt * 196 + 64], pp[:, 0:64]))
            sc.inc(scal.copy(v_hi[:, tt * 196 + 130:tt * 196 + 194],
                             pp[:, 64:128]))
            evs[tt] = sc.n
            sc.wait(vect)
            ve.inc(vect.tensor_tensor(v_lo[:, tt * 196:tt * 196 + 64],
                                      pp[:, 0:64],
                                      v_hi[:, tt * 196:tt * 196 + 64],
                                      op=ALU.subtract))
            ve.inc(vect.tensor_tensor(v_lo[:, tt * 196 + 130:tt * 196 + 194],
                                      pp[:, 64:128],
                                      v_hi[:, tt * 196 + 130:tt * 196 + 194],
                                      op=ALU.subtract))
            evv[tt] = ve.n

        if p == 0:
            scal.wait_ge(sg.h, sg.n)
            vect.wait_ge(sg.h, sg.n)
        transpose_split(w[3], wot_hi, wot_lo)
        if p + 1 < NPAIR:
            sync.wait_ge(tm.h, tm.n)
            wrec = load_pair_weights(p + 1)

        vect.wait_ge(sc.h, sc.n)
        for side, srcb in enumerate((q_hi, k_hi)):
            scal.wait_ge(tm.h, tm.n)
            scal.wait_ge(ve.h, ve.n)
            sc.inc(scal.activation(sqscr[:, :], srcb[:, :], AF.Square))
            tens.wait_ge(sc.h, sc.n)
            tens.wait_ge(ve.h, ve.n)
            last = None
            for g in range(4):
                last = tens.matmul(ps_sc[0:2, g * 512:(g + 1) * 512],
                                   ones2[:, :], sqscr[:, g * 512:(g + 1) * 512],
                                   start=True, stop=True)
            tm.inc(last)
            tm.wait(vect)
            ve.inc(vect.tensor_reduce(bmx[0:2, side:side + 1], ps_sc[0:2, :],
                                      axis=AX.X, op=ALU.max))
            vect.wait_ge(ve.h, ve.n)
        ve.inc(vect.tensor_tensor(bmx[0:2, 0:1], bmx[0:2, 0:1], bmx[0:2, 1:2],
                                  op=ALU.mult))
        ve.wait(scal)
        sc.inc(scal.activation(bmx[0:2, 1:2], bmx[0:2, 0:1], AF.Sqrt))
        sc.wait(tens)
        tm.inc(tens.transpose(ps_t1[0:1, 0:2], bmx[0:2, 1:2], ident[0:2, 0:2]))
        tm.wait(vect)
        ve.inc(vect.tensor_copy(bmx[0:1, 0:2], ps_t1[0:1, 0:2]))
        ve.wait(tens)
        for hh2 in range(2):
            tm.inc(tens.matmul(ps_t1[:, 0:1], neg1_row[:, :],
                               bmx[0:1, hh2:hh2 + 1], start=True, stop=True))
            tm.wait(vect)
            ve.inc(vect.tensor_copy(vec[10 + hh2][:, :], ps_t1[:, 0:1]))
            tens.wait_ge(ve.h, ve.n)
        nc.all_engine_barrier()
        if debug and p == 0:
            tap("qh0", q_hi[:, :]); tap("ql0", q_lo[:, :])
            tap("kh0", k_hi[:, :]); tap("v00", v_hi[:, :])
            tap("bv0", vec[10][:, :])
            nc.all_engine_barrier()

        pb = AVS["n"]
        for hh in range(2):
            for c in range(16):
                for qg in range(c // 4, 4):
                    i = AVS["n"]
                    AVS["n"] += 1
                    AVS["meta"][i] = (c, qg, hh)
                    if AV_BATCH:
                        if i % 2 == 1:
                            for k in (i - 7, i - 6):
                                if k >= pb:
                                    av_av(k)
                            for k in (i - 5, i - 4):
                                if k >= pb:
                                    av_split(k)
                            for k in (i - 3, i - 2):
                                if k >= pb:
                                    av_exp(k)
                            av_scores(i - 1)
                            av_scores(i)
                            av_drain()
                    else:
                        av_scores(i)
                        if i - 3 >= pb:
                            av_exp(i - 3)
                        if i - 7 >= pb:
                            av_av(i - 7)
                        if i - 5 >= pb:
                            av_split(i - 5)
                        av_drain()
        last = AVS["n"] - 1
        if AV_BATCH:
            av_av(last - 5)
            av_av(last - 4)
            av_split(last - 3)
            av_split(last - 2)
            av_exp(last - 1)
            av_exp(last)
            av_av(last - 3)
            av_av(last - 2)
            av_split(last - 1)
            av_split(last)
            av_av(last - 1)
            av_av(last)
        else:
            av_av(last - 6)
            av_av(last - 5)
            av_split(last - 4)
            av_split(last - 3)
            av_exp(last - 2)
            av_exp(last - 1)
            av_exp(last)
            av_av(last - 4)
            av_av(last - 3)
            av_split(last - 2)
            av_split(last - 1)
            av_av(last - 2)
            av_split(last)
            av_av(last - 1)
            av_av(last)
        av_drain(force=True)
        nc.all_engine_barrier()
        if debug and p == 0:
            tap("hofmh0", hofm_hi[:, :]); tap("hofml0", hofm_lo[:, :])
            nc.all_engine_barrier()

        evrec = {}
        for tt in range(NTT):
            for oc in range(2):
                g = tt * 2 + oc
                pp = (ps_av, ps_x)[oc]
                tsl = slice(tt * 128, (tt + 1) * 128)
                osl = slice(oc * 512, (oc + 1) * 512)
                if g >= 2:
                    tens.wait_ge(ve.h, evrec[g - 2])
                tens.matmul(pp[:, :], hofm_hi[:, tsl], wot_hi[:, osl],
                            start=True, stop=False)
                tens.matmul(pp[:, :], hofm_hi[:, tsl], wot_lo[:, osl],
                            start=False, stop=False)
                st = tens.matmul(pp[:, :], hofm_lo[:, tsl], wot_hi[:, osl],
                                 start=False, stop=True)
                tm.inc(st)
                tm.wait(vect)
                if p > 0:
                    vect.wait_ge(ve.h, accrec[g])
                dst = acc_tm[tt][:, oc * 512:(oc + 1) * 512]
                if p == 0:
                    ve.inc(vect.tensor_copy(dst, pp[:, :]))
                else:
                    ve.inc(vect.tensor_tensor(dst, dst, pp[:, :], op=ALU.add))
                evrec[g] = ve.n
                accrec[g] = ve.n
        nc.all_engine_barrier()

    tap("acc0", acc_tm[0][:, :]); tap("acc15", acc_tm[15][:, :])
    nc.all_engine_barrier()
    dD = [sb("dD0", [128, D], F32, RC + 32768), sb("dD1", [128, D], F32, RC + 36864)]
    dE = [sb("dE0", [128, D], F32, RC + 40960), w[2]]
    lnE_hi = [sb("lnEh0", [128, 1024], BF16, RC + 45056),
              sb("lnEh1", [128, 1024], BF16, RC + 49152)]
    lnE_lo = [sb("lnEl0", [128, 1024], BF16, RC + 47104),
              sb("lnEl1", [128, 1024], BF16, RC + 51200)]
    obuf = [sb(f"obuf{i}", [128, D], F32, RC + 45056 + 4096 * i) for i in range(2)]
    mstage = dD[0]

    sync.wait_ge(sg.h, sg.n)
    sync.wait_ge(gp.h, cc2)
    sync.wait_ge(ld.h, ld.n)
    gps.wait_ge(sg.h, sg.n)
    gps.dma_start(out=bias_bc[0:1, :], in_=bias_hbm[:, :]).then_inc(sg.h, 16)
    sg.n += 16
    bias_ld = sg.n
    mstages = [dD[0], dD[1]]
    gps.wait_ge(sg.h, bias_ld)
    m_ms, spl_rec = {}, {}

    def _m_split(dc):
        eng_sem = ld if dc % 2 == 0 else sg
        scal.wait_ge(eng_sem.h, m_ms[dc])
        sc.inc(scal.copy(m_hi[dc][:, :], mstages[dc % 2][:, :]))
        sc.wait(vect)
        ve.inc(vect.tensor_tensor(m_lo[dc][:, :], mstages[dc % 2][:, :],
                                  m_hi[dc][:, :], op=ALU.subtract))
        spl_rec[dc] = ve.n

    for dc in range(NDC):
        eng, sem = (sync, ld) if dc % 2 == 0 else (gps, sg)
        if dc >= 2:
            eng.wait_ge(ve.h, spl_rec[dc - 2])
        eng.dma_start(out=mstages[dc % 2][:, :],
                      in_=m_full.ap()[dc * 128:(dc + 1) * 128, :]
                      ).then_inc(sem.h, 16)
        sem.n += 16
        m_ms[dc] = sem.n
        if dc >= 1:
            _m_split(dc - 1)
    _m_split(NDC - 1)
    tens.wait_ge(sg.h, bias_ld)
    for oc in range(2):
        tm.inc(tens.matmul(ps_x[:, :], ones_row[:, :],
                           bias_bc[0:1, oc * 512:(oc + 1) * 512],
                           start=True, stop=True))
        tm.wait(vect)
        ve.inc(vect.tensor_copy(bias_bc[:, oc * 512:(oc + 1) * 512], ps_x[:, :]))
        tens.wait_ge(ve.h, ve.n)
    sync.wait_ge(ld.h, ld.n)
    for tt in range(2):
        sync.dma_start(out=w[tt % 2][:, :],
                       in_=x_hbm[tt * 128:(tt + 1) * 128, :]).then_inc(ld.h, 16)
        ld.n += 16

    x1_done = {}
    lnE_done = {}
    out_dma = {}
    fmm_done = {}
    ev_done = {}
    ob_done = {}

    def def_s12(tt):
        r = tt % 2
        pre = [(ld.h, ld.n)]
        if tt >= 2:
            pre.append((tm.h, fmm_done.get(tt - 2, 0)))
        layernorm(acc_tm[tt][:, :], dD[r][:, :], w[3][:, :],
                  VS_EVEN if r == 0 else VS_ODD, pre=pre)
        vect.wait_ge(ve.h, ve.n)
        ve.inc(vect.tensor_tensor(x1_tm[tt][:, :], dD[r][:, :],
                                  w[tt % 2][:, :], op=ALU.add))
        x1_done[tt] = ve.n
        if tt + 2 < NTT:
            sync.wait_ge(ve.h, x1_done[tt])
            sync.dma_start(out=w[tt % 2][:, :],
                           in_=x_hbm[(tt + 2) * 128:(tt + 3) * 128, :]
                           ).then_inc(ld.h, 16)
            ld.n += 16
        layernorm(x1_tm[tt][:, :], dE[r][:, :], w[3][:, :],
                  VS_EVEN if r == 0 else VS_ODD,
                  pre=[(ve.h, x1_done[tt])])
        lnE_done[tt] = ve.n

    def def_s3a(tt):
        r = tt % 2
        tens.wait_ge(ve.h, lnE_done[tt])
        if tt >= 2:
            scal.wait_ge(sg.h, out_dma[tt - 2])
            vect.wait_ge(sg.h, out_dma[tt - 2])
            tens.wait_ge(ve.h, ob_done[tt - 2])
        for half in range(2):
            pp = ps_t[half]
            if (tt - 1, half) in ev_done:
                tens.wait_ge(ve.h, ev_done[(tt - 1, half)])
            for dc4 in range(4):
                dc = half * 4 + dc4
                tm.inc(tens.transpose(pp[:, dc4 * 128:(dc4 + 1) * 128],
                                      dE[r][:, dc * 128:(dc + 1) * 128],
                                      ident[:, :]))
            tm.wait(scal)
            hsl = slice(half * 512, (half + 1) * 512)
            sc.inc(scal.copy(lnE_hi[r][:, hsl], pp[:, 0:512]))
            sc.wait(vect)
            ve.inc(vect.tensor_tensor(lnE_lo[r][:, hsl], pp[:, 0:512],
                                      lnE_hi[r][:, hsl], op=ALU.subtract))
            ev_done[(tt, half)] = ve.n

    def def_s3b(tt):
        r = tt % 2
        fy_t = dD[r]
        for oc in range(2):
            pp = ps_sc[:, (oc + 2 * (tt % 2)) * 512:
                       (oc + 2 * (tt % 2)) * 512 + 512]
            osl = slice(oc * 512, (oc + 1) * 512)
            tens.wait_ge(ve.h, ev_done[(tt, 1)])
            for dc in range(NDC):
                dsl = slice(dc * 128, (dc + 1) * 128)
                tens.matmul(pp, lnE_hi[r][:, dsl], m_hi[dc][:, osl],
                            start=(dc == 0), stop=False)
                tens.matmul(pp, lnE_hi[r][:, dsl], m_lo[dc][:, osl],
                            start=False, stop=False)
                st = tens.matmul(pp, lnE_lo[r][:, dsl], m_hi[dc][:, osl],
                                 start=False, stop=(dc == 7))
            tm.inc(st)
            tm.wait(vect)
            ve.inc(vect.tensor_tensor(fy_t[:, osl], pp,
                                      bias_bc[:, osl], op=ALU.add))
        fmm_done[tt] = tm.n

    def def_s3b2(tt):
        r = tt % 2
        fy_t = dD[r]
        layernorm(fy_t[:, :], dE[r][:, :], w[3][:, :],
                  VS_EVEN if r == 0 else VS_ODD, pre=[(ve.h, ve.n)])
        ob = obuf[r]
        if tt >= 2:
            vect.wait_ge(sg.h, sg.n)
        vect.wait_ge(ve.h, ve.n)
        ve.inc(vect.tensor_tensor(ob[:, :], dE[r][:, :], x1_tm[tt][:, :],
                                  op=ALU.add))
        ob_done[tt] = ve.n
        ve.wait(gps)
        gps.wait_ge(sg.h, sg.n)
        gps.dma_start(out=out_hbm[tt * 128:(tt + 1) * 128, :],
                      in_=ob[:, :]).then_inc(sg.h, 16)
        sg.n += 16
        out_dma[tt] = sg.n

    for tt in range(NTT):
        if tt >= 1:
            def_s3a(tt - 1)
            def_s3b(tt - 1)
        def_s12(tt)
        if tt >= 1:
            def_s3b2(tt - 1)
    def_s3a(NTT - 1)
    def_s3b(NTT - 1)
    def_s3b2(NTT - 1)
    gps.wait_ge(sg.h, sg.n)
    return nc


_NC_CACHE = None


def kernel(L0_attn_x, L0_attn_wq, L0_attn_wk, L0_attn_wv, L0_attn_wo,
           L0_ffn_w_freq_real, L0_ffn_w_freq_imag, L0_ffn_bias):
    global _NC_CACHE
    x = np.ascontiguousarray(np.asarray(L0_attn_x, dtype=np.float32))
    wq = np.ascontiguousarray(np.asarray(L0_attn_wq, np.float32).reshape(H * DH, D))
    wk = np.ascontiguousarray(np.asarray(L0_attn_wk, np.float32).reshape(H * DH, D))
    wv = np.ascontiguousarray(np.asarray(L0_attn_wv, np.float32).reshape(H * DH, D))
    wo = np.ascontiguousarray(np.asarray(L0_attn_wo, np.float32))
    wfr = np.ascontiguousarray(np.asarray(L0_ffn_w_freq_real, np.float32))
    wfi = np.ascontiguousarray(np.asarray(L0_ffn_w_freq_imag, np.float32))
    bias = np.ascontiguousarray(np.asarray(L0_ffn_bias, np.float32).reshape(1, D))
    if _NC_CACHE is None:
        _NC_CACHE = build_nc()
    nc = _NC_CACHE
    in_maps = [{"x": np.ascontiguousarray(x[b]), "wq": wq, "wk": wk, "wv": wv,
                "wo": wo,
                "wfr": np.ascontiguousarray(wfr[b * 128:(b + 1) * 128]),
                "wfi": np.ascontiguousarray(wfi[b * 128:(b + 1) * 128]),
                "coff": np.full((1, 1), b * 128, np.float32),
                "bias": bias}
               for b in range(8)]
    res = run_bass_kernel_spmd(nc, in_maps, core_ids=list(range(8)))
    return np.stack([res.results[b]["out"] for b in range(8)], axis=0)


# version 23
# speedup vs baseline: 1.0185x; 1.0185x over previous
import numpy as np
import concourse.bass as bass
import concourse.mybir as mybir
from concourse.bass_utils import run_bass_kernel_spmd

F32 = mybir.dt.float32
I32 = mybir.dt.int32
AF = mybir.ActivationFunctionType
ALU = mybir.AluOpType
AX = mybir.AxisListType

D = 1024
T = 2048
H = 16
DH = 64
NPAIR = 8
NTT = T // 128
NDC = D // 128
VST = 80
EPS = 1e-5
NEG = -1.0e9
AV_BATCH = True

TWO_PI_OVER_D = 2.0 * np.pi / 1024.0
TPD_HI = float(np.float32(TWO_PI_OVER_D))
TPD_LO = float(np.float32(TWO_PI_OVER_D - TPD_HI))
HALF_PI = float(np.float32(np.pi / 2))


class Sem:

    def __init__(self, nc, name):
        self.h = nc.alloc_semaphore(name)
        self.n = 0

    def inc(self, inst, k=1):
        inst.then_inc(self.h, k)
        self.n += k
        return self.n

    def wait(self, eng, n=None):
        eng.wait_ge(self.h, self.n if n is None else n)


def build_nc(debug=False):
    nc = bass.Bass("TRN2", target_bir_lowering=False)

    x_hbm = nc.dram_tensor("x", [T, D], F32, kind="ExternalInput").ap()
    wq_hbm = nc.dram_tensor("wq", [H * DH, D], F32, kind="ExternalInput").ap()
    wk_hbm = nc.dram_tensor("wk", [H * DH, D], F32, kind="ExternalInput").ap()
    wv_hbm = nc.dram_tensor("wv", [H * DH, D], F32, kind="ExternalInput").ap()
    wo_hbm = nc.dram_tensor("wo", [D, D], F32, kind="ExternalInput").ap()
    wfr_hbm = nc.dram_tensor("wfr", [128, D], F32, kind="ExternalInput").ap()
    wfi_hbm = nc.dram_tensor("wfi", [128, D], F32, kind="ExternalInput").ap()
    coff_hbm = nc.dram_tensor("coff", [1, 1], F32, kind="ExternalInput").ap()
    bias_hbm = nc.dram_tensor("bias", [1, D], F32, kind="ExternalInput").ap()
    out_hbm = nc.dram_tensor("out", [T, D], F32, kind="ExternalOutput").ap()
    tu_in = nc.dram_tensor("tu_in", [256, D], F32, kind="Internal")
    tu_full = nc.dram_tensor("tu_full", [2048, D], F32, kind="Internal",
                             addr_space="Shared")
    m_in = nc.dram_tensor("m_in", [128, D], F32, kind="Internal")
    m_full = nc.dram_tensor("m_full", [D, D], F32, kind="Internal",
                            addr_space="Shared")

    BASE = 17408

    def sb(name, shape, dtype, off):
        return nc.alloc_sbuf_tensor_at(name, shape, dtype, offset=off).ap()

    ident = sb("ident", [128, 128], F32, BASE)
    maskt = sb("maskt", [128, 128], F32, BASE + 512)
    BF16 = mybir.dt.bfloat16
    VEC0 = BASE + 1024
    vec = [sb(f"vec{i}", [128, 1], F32, VEC0 + 32 * i) for i in range(16)]
    ones_row = sb("ones_row", [1, 128], F32, VEC0 + 512)
    neg1_row = sb("neg1_row", [1, 128], F32, VEC0 + 512 + 512)
    neg64_row = sb("neg64_row", [1, 128], F32, VEC0 + 2112)
    bmx = sb("bmx", [128, 2], F32, VEC0 + 2624)
    coff_s = sb("coff_s", [1, 1], F32, VEC0 + 2656)
    ones2 = sb("ones2", [128, 2], F32, VEC0 + 2688)
    ones_col = sb("ones_col", [128, 1], F32, VEC0 + 1536)
    ident_bf = sb("ident_bf", [128, 128], BF16, VEC0 + 1600)
    maskt_bf = sb("maskt_bf", [128, 128], BF16, VEC0 + 1856)
    bias_bc = sb("bias_bc", [128, D], F32, BASE + 4096)
    wot_hi = sb("wot_hi", [128, D], BF16, BASE + 4096)
    wot_lo = sb("wot_lo", [128, D], BF16, BASE + 4096 + 2048)

    W0 = BASE + 8192
    w = [sb(f"wtmp{i}", [128, D], F32, W0 + 4096 * i) for i in range(4)]

    RA = W0 + 4 * 4096
    RB = RA + 65536
    RC = RB + 65536
    assert RC + 55808 <= 229376, (RC + 55808)

    c_sb = [sb(f"cdft{i}", [128, D], F32, RA + 4096 * i) for i in range(8)]
    s_sb = [sb(f"sdft{i}", [128, D], F32, RA + 32768 + 4096 * i) for i in range(8)]
    lnfm_hi = [sb(f"lnfmh{i}", [128, T], BF16, RA + 8192 * i) for i in range(8)]
    lnfm_lo = [sb(f"lnfml{i}", [128, T], BF16, RA + 8192 * i + 4096) for i in range(8)]
    x1_tm = [sb(f"x1tm{i}", [128, D], F32, RA + 4096 * i) for i in range(16)]

    wrtS = sb("wrtS", [128, D], F32, RB)
    witS = sb("witS", [128, D], F32, RB + 4096)
    acc_tm = [sb(f"acctm{i}", [128, D], F32, RB + 4096 * i) for i in range(16)]
    ln2fm_hi = [sb(f"ln2fmh{i}", [128, T], BF16, RB + 8192 * i) for i in range(8)]
    ln2fm_lo = [sb(f"ln2fml{i}", [128, T], BF16, RB + 8192 * i + 4096)
                for i in range(8)]

    tp_sh = sb("tp_sh", [128, D], F32, RC)
    up_sh = sb("up_sh", [128, D], F32, RC + 4096)
    cbt = [sb(f"cbt{i}", [128, 128], F32, RC + 8192 + 512 * i) for i in range(8)]
    sbt = [sb(f"sbt{i}", [128, 128], F32, RC + 12288 + 512 * i) for i in range(8)]
    tTs = [sb(f"tTs{i}", [128, D], F32, RC + 16384 + 4096 * i) for i in range(2)]
    tUs = [sb(f"tUs{i}", [128, D], F32, RC + 24576 + 4096 * i) for i in range(2)]
    msh = sb("msh", [128, D], F32, RC + 32768)
    gfj = sb("gfj", [128, 128], F32, RC + 36864)
    gfv = sb("gfv", [128, 128], F32, RC + 37376)
    gwk = sb("gwk", [128, 128], F32, RC + 37888)
    gtk = sb("gtk", [128, 128], F32, RC + 38400)
    mvs = sb("mvs", [128, 8], F32, RC + 38912)
    q_hi = sb("q_hi", [128, T], BF16, RC)
    q_lo = sb("q_lo", [128, T], BF16, RC + 4096)
    k_hi = sb("k_hi", [128, T], BF16, RC + 8192)
    k_lo = sb("k_lo", [128, T], BF16, RC + 12288)
    v_hi = sb("v_hi", [128, 16 * 196], BF16, RC + 16384)
    v_lo = sb("v_lo", [128, 16 * 196], BF16, RC + 22656)
    slotA = sb("slotA", [128, 512], F32, RC + 28928)
    hof32 = sb("hof32", [128, 512], F32, RC + 30976)
    pt3 = [sb(f"pt3_{i}", [128, 512], F32, RC + 33024 + 2048 * i)
           for i in range(3)]
    pth = [sb(f"pth{i}", [128, 512], BF16, RC + 39168 + 1024 * i)
           for i in range(3)]
    ptl = [sb(f"ptl{i}", [128, 512], BF16, RC + 42240 + 1024 * i)
           for i in range(3)]
    hofm_hi = sb("hofm_hi", [128, T], BF16, RC + 45312)
    hofm_lo = sb("hofm_lo", [128, T], BF16, RC + 49408)
    sqscr = sb("sqscr", [128, T], F32, RC + 45312)
    wt_hi = sb("wt_hi", [128, D], BF16, RC + 33024)
    wt_lo = sb("wt_lo", [128, D], BF16, RC + 35072)
    ones_pbf = sb("ones_pbf", [128, 128], BF16, RC + 53504)
    rh = sb("rh", [128, 512], BF16, RC + 53760)
    rl = sb("rl", [128, 512], BF16, RC + 54784)
    m_hi = [sb(f"mhi{i}", [128, D], BF16, RC + 2048 * i) for i in range(8)]
    m_lo = [sb(f"mlo{i}", [128, D], BF16, RC + 16384 + 2048 * i)
            for i in range(8)]
    fy = sb("fy", [128, D], F32, RC + 32768)
    fln = sb("fln", [128, D], F32, RC + 36864)
    fout = [sb(f"fout{i}", [128, D], F32, RC + 40960 + 4096 * i) for i in range(2)]
    fsq = sb("fsq", [128, D], F32, RC + 49152)

    ps_sc = nc.alloc_psum_tensor("ps_sc", [128, 2048], F32).ap()
    ps_t0 = nc.alloc_psum_tensor("ps_t0", [128, 512], F32).ap()
    ps_t1 = nc.alloc_psum_tensor("ps_t1", [128, 512], F32).ap()
    ps_av = nc.alloc_psum_tensor("ps_av", [128, 512], F32).ap()
    ps_x = nc.alloc_psum_tensor("ps_x", [128, 512], F32).ap()
    ps_t = [ps_t0, ps_t1]

    ld = Sem(nc, "ld")
    sg = Sem(nc, "sg")
    tm = Sem(nc, "tm")
    ve = Sem(nc, "ve")
    sc = Sem(nc, "sc")
    gp = Sem(nc, "gp")

    tens, vect, scal, gps, sync = nc.tensor, nc.vector, nc.scalar, nc.gpsimd, nc.sync

    def tap(name, ap):
        if not debug:
            return
        d = nc.dram_tensor("tap_" + name, list(ap.shape), F32,
                           kind="ExternalOutput").ap()
        gps.dma_start(out=d, in_=ap).then_inc(sg.h, 16)
        sg.n += 16
        gps.wait_ge(sg.h, sg.n)

    gp.inc(gps.memset(vec[7][:, :], HALF_PI))
    gp.inc(gps.memset(vec[8][:, :], EPS))
    gp.inc(gps.memset(ones_row[:, :], 1.0))
    gp.inc(gps.memset(neg1_row[:, :], -1.0))
    gp.inc(gps.memset(neg64_row[:, :], -64.0))
    gp.inc(gps.memset(ones_col[:, :], 1.0))
    gp.inc(gps.memset(ones2[:, :], 0.0))
    gp.inc(gps.memset(ones_pbf[:, :], 1.0))
    gp.inc(gps.iota(w[2][:, 0:128], pattern=[[-1, 128]], base=0,
                    channel_multiplier=1,
                    allow_small_or_imprecise_dtypes=True))
    gp.wait(vect)
    ve.inc(vect.tensor_scalar(ident[:, :], w[2][:, 0:128], 0.0, None,
                              op0=ALU.is_equal))
    vect.wait_ge(ve.h, ve.n)
    gp.inc(gps.memset(maskt[:, :], 0.0))
    gps.wait_ge(gp.h, gp.n)
    gp.inc(gps.affine_select(maskt[:, :], maskt[:, :], pattern=[[1, 128]],
                             compare_op=ALU.is_ge, fill=NEG, base=0,
                             channel_multiplier=-1))
    gp.inc(gps.memset(ones2[0:64, 0:1], 1.0))
    gp.inc(gps.memset(ones2[64:128, 1:2], 1.0))
    gp.wait(vect)
    ve.inc(vect.tensor_copy(ident_bf[:, :], ident[:, :]))
    ve.inc(vect.tensor_copy(maskt_bf[:, :], maskt[:, :]))
    gps.wait_ge(ve.h, ve.n)
    gp.inc(gps.iota(w[2][:, :], pattern=[[1, D]], base=0, channel_multiplier=0,
                    allow_small_or_imprecise_dtypes=True))
    for dc in range(NDC):
        if dc > 0:
            gps.wait_ge(ve.h, ve.n)
        gp.inc(gps.iota(vec[0][:, :], pattern=[[0, 1]], base=dc * 128,
                        channel_multiplier=1,
                        allow_small_or_imprecise_dtypes=True))
        gp.wait(vect)
        if dc > 0:
            vect.wait_ge(sc.h, sc.n)
        def _step(inst):
            ve.inc(inst)
            vect.wait_ge(ve.h, ve.n)
        _step(vect.tensor_scalar(w[3][:, :], w[2][:, :], vec[0][:, :], None,
                                 op0=ALU.mult))
        _step(vect.tensor_copy(c_sb[dc][:, :].bitcast(I32), w[3][:, :]))
        _step(vect.tensor_scalar(c_sb[dc][:, :].bitcast(I32),
                                 c_sb[dc][:, :].bitcast(I32), 1023, None,
                                 op0=ALU.bitwise_and))
        _step(vect.tensor_copy(w[3][:, :], c_sb[dc][:, :].bitcast(I32)))
        _step(vect.tensor_scalar(c_sb[dc][:, :], w[3][:, :], 511.5, None,
                                 op0=ALU.is_gt))
        _step(vect.scalar_tensor_tensor(w[3][:, :], c_sb[dc][:, :], -1024.0,
                                        w[3][:, :], op0=ALU.mult, op1=ALU.add))
        _step(vect.tensor_scalar(s_sb[dc][:, :], w[3][:, :], TPD_HI, None,
                                 op0=ALU.mult))
        ve.inc(vect.scalar_tensor_tensor(w[3][:, :], w[3][:, :], TPD_LO,
                                         s_sb[dc][:, :], op0=ALU.mult, op1=ALU.add))
        ve.wait(scal)
        sc.inc(scal.activation(s_sb[dc][:, :], w[3][:, :], AF.Sin))
        sc.inc(scal.activation(w[3][:, :], w[3][:, :], AF.Abs))
        scal.wait_ge(sc.h, sc.n)
        sc.inc(scal.activation(c_sb[dc][:, :], w[3][:, :], AF.Sin,
                               bias=vec[7][:, :], scale=-1.0))
    nc.all_engine_barrier()
    tap("c0", c_sb[0][:, :]); tap("s0", s_sb[0][:, :])
    tap("ident", ident[:, :]); tap("maskt", maskt[:, :])
    nc.all_engine_barrier()

    sync.dma_start(out=w[2][:, :], in_=wfr_hbm[:, :]).then_inc(ld.h, 16)
    sync.dma_start(out=w[3][:, :], in_=wfi_hbm[:, :]).then_inc(ld.h, 16)
    sync.dma_start(out=coff_s[:, :], in_=coff_hbm[:, :]).then_inc(ld.h, 16)
    ld.n += 48
    tens.wait_ge(ld.h, ld.n)
    p1rec = {}
    for a in range(8):
        pp = ps_t[a % 2]
        if a >= 2:
            tens.wait_ge(ve.h, p1rec[a - 2])
        tm.inc(tens.transpose(pp[:, 0:128], w[2][:, a * 128:(a + 1) * 128],
                              ident[:, :]))
        tm.inc(tens.transpose(pp[:, 128:256], w[3][:, a * 128:(a + 1) * 128],
                              ident[:, :]))
        tm.wait(vect)
        ve.inc(vect.tensor_copy(wrtS[:, a * 128:(a + 1) * 128], pp[:, 0:128]))
        ve.inc(vect.tensor_copy(witS[:, a * 128:(a + 1) * 128],
                                pp[:, 128:256]))
        p1rec[a] = ve.n
    tm.inc(tens.matmul(ps_x[:, 0:1], ones_row[:, :], coff_s[0:1, 0:1],
                       start=True, stop=True))
    tm.wait(vect)
    ve.inc(vect.tensor_copy(vec[1][:, :], ps_x[:, 0:1]))
    nc.all_engine_barrier()
    tap("wrt0", wrtS[:, :]); tap("wit0", witS[:, :])
    nc.all_engine_barrier()

    gp.inc(gps.iota(gfj[:, :], pattern=[[1, 128]], base=0,
                    channel_multiplier=0, allow_small_or_imprecise_dtypes=True))
    gp.inc(gps.iota(mvs[:, :], pattern=[[128, 8]], base=0,
                    channel_multiplier=1, allow_small_or_imprecise_dtypes=True))
    iota_ms = gp.n
    gp.wait(vect)
    vect.wait_ge(ve.h, ve.n)
    ve.inc(vect.tensor_scalar(gfv[:, :], gfj[:, :], vec[1][:, :], None,
                              op0=ALU.add))
    vect.wait_ge(ve.h, ve.n)
    for mc in range(8):
        if mc > 0:
            vect.wait_ge(sc.h, sc.n)

        def _gstep(inst):
            ve.inc(inst)
            vect.wait_ge(ve.h, ve.n)
        _gstep(vect.tensor_scalar(gwk[:, :], gfv[:, :], mvs[:, mc:mc + 1],
                                  None, op0=ALU.mult))
        _gstep(vect.tensor_copy(gtk[:, :].bitcast(I32), gwk[:, :]))
        _gstep(vect.tensor_scalar(gtk[:, :].bitcast(I32),
                                  gtk[:, :].bitcast(I32), 1023, None,
                                  op0=ALU.bitwise_and))
        _gstep(vect.tensor_copy(gwk[:, :], gtk[:, :].bitcast(I32)))
        _gstep(vect.tensor_scalar(gtk[:, :], gwk[:, :], 511.5, None,
                                  op0=ALU.is_gt))
        _gstep(vect.scalar_tensor_tensor(gwk[:, :], gtk[:, :], -1024.0,
                                         gwk[:, :], op0=ALU.mult, op1=ALU.add))
        _gstep(vect.tensor_scalar(gtk[:, :], gwk[:, :], TPD_HI, None,
                                  op0=ALU.mult))
        ve.inc(vect.scalar_tensor_tensor(gwk[:, :], gwk[:, :], TPD_LO,
                                         gtk[:, :], op0=ALU.mult, op1=ALU.add))
        ve.wait(scal)
        sc.inc(scal.activation(sbt[mc][:, :], gwk[:, :], AF.Sin))
        sc.inc(scal.activation(gwk[:, :], gwk[:, :], AF.Abs))
        scal.wait_ge(sc.h, sc.n)
        sc.inc(scal.activation(cbt[mc][:, :], gwk[:, :], AF.Sin,
                               bias=vec[7][:, :], scale=-1.0))
    gen_sc = sc.n

    def stage12s(dst, rhs_a, rhs_b, sign_b):
        tens.wait_ge(ve.h, p1rec[7])
        for jh in range(2):
            pa = ps_sc[:, jh * 1024:jh * 1024 + 512]
            pb = ps_sc[:, jh * 1024 + 512:jh * 1024 + 1024]
            for oc in range(8):
                tens.matmul(pa, wrtS[:, oc * 128:(oc + 1) * 128],
                            rhs_a[oc][:, jh * 512:(jh + 1) * 512],
                            start=(oc == 0), stop=(oc == 7))
            for oc in range(8):
                st = tens.matmul(pb, witS[:, oc * 128:(oc + 1) * 128],
                                 rhs_b[oc][:, jh * 512:(jh + 1) * 512],
                                 start=(oc == 0), stop=(oc == 7))
            tm.inc(st)
            tm.wait(vect)
            ve.inc(vect.tensor_scalar(w[2][:, jh * 512:(jh + 1) * 512], pb,
                                      float(sign_b), None, op0=ALU.mult))
            vect.wait_ge(ve.h, ve.n)
            ve.inc(vect.tensor_tensor(dst[:, jh * 512:(jh + 1) * 512], pa,
                                      w[2][:, jh * 512:(jh + 1) * 512],
                                      op=ALU.add))
            tens.wait_ge(ve.h, ve.n)

    stage12s(tp_sh, c_sb, s_sb, -1)
    stage12s(up_sh, s_sb, c_sb, +1)
    ve.wait(gps)
    gps.wait_ge(sg.h, sg.n)
    gps.dma_start(out=tu_in.ap()[0:128, :], in_=tp_sh[:, :]).then_inc(sg.h, 16)
    gps.dma_start(out=tu_in.ap()[128:256, :], in_=up_sh[:, :]).then_inc(sg.h, 16)
    sg.n += 32
    gps.wait_ge(sg.h, sg.n)
    gps.wait_ge(gp.h, iota_ms)
    gp.inc(gps.collective_compute(
        "AllGather", ALU.bypass, replica_groups=[list(range(8))],
        ins=[tu_in.ap().opt()], outs=[tu_full.ap().opt()]))
    cc1 = gp.n
    nc.all_engine_barrier()
    tap("tp0", tp_sh[:, :]); tap("up0", up_sh[:, :])
    nc.all_engine_barrier()

    sync.wait_ge(ld.h, ld.n)
    sync.wait_ge(gp.h, cc1)
    gps.wait_ge(gp.h, cc1)
    ld_ms, sg_ms, mm_ms = {}, {}, {}

    def _tu_load(mc):
        eng, sem, ms = (sync, ld, ld_ms) if mc % 2 == 0 else (gps, sg, sg_ms)
        if mc >= 2:
            eng.wait_ge(tm.h, mm_ms[mc - 2])
        eng.dma_start(out=tTs[mc % 2][:, :],
                      in_=tu_full.ap()[mc * 256:mc * 256 + 128, :]
                      ).then_inc(sem.h, 16)
        eng.dma_start(out=tUs[mc % 2][:, :],
                      in_=tu_full.ap()[mc * 256 + 128:mc * 256 + 256, :]
                      ).then_inc(sem.h, 16)
        sem.n += 32
        ms[mc] = sem.n

    _tu_load(0)
    _tu_load(1)
    tens.wait_ge(sc.h, gen_sc)
    tens.wait_ge(ve.h, ve.n)
    for mc in range(8):
        sem, ms = (ld, ld_ms) if mc % 2 == 0 else (sg, sg_ms)
        tens.wait_ge(sem.h, ms[mc])
        for jh in range(2):
            tens.matmul(ps_t[jh][:, :], cbt[mc][:, :],
                        tTs[mc % 2][:, jh * 512:(jh + 1) * 512],
                        start=(mc == 0), stop=False)
            st = tens.matmul(ps_t[jh][:, :], sbt[mc][:, :],
                             tUs[mc % 2][:, jh * 512:(jh + 1) * 512],
                             start=False, stop=(mc == 7))
        tm.inc(st)
        mm_ms[mc] = tm.n
        if mc + 2 < 8:
            _tu_load(mc + 2)
    tm.wait(scal)
    for jh in range(2):
        sc.inc(scal.mul(msh[:, jh * 512:(jh + 1) * 512], ps_t[jh][:, :],
                        1.0 / 1024.0))
    sc.wait(gps)
    gps.wait_ge(sg.h, sg.n)
    gps.dma_start(out=m_in.ap()[:, :], in_=msh[:, :]).then_inc(sg.h, 16)
    sg.n += 16
    gps.wait_ge(sg.h, sg.n)
    gp.inc(gps.collective_compute(
        "AllGather", ALU.bypass, replica_groups=[list(range(8))],
        ins=[m_in.ap().opt()], outs=[m_full.ap().opt()]))
    cc2 = gp.n
    gps.wait_ge(gp.h, cc2)
    nc.all_engine_barrier()
    gp.inc(gps.memset(v_hi[:, :], 1.0))
    gp.inc(gps.memset(v_lo[:, :], 0.0))
    vall_ms = gp.n
    gps.wait_ge(gp.h, vall_ms)

    VS_EVEN = (0, 1, 2, 3, 4)
    VS_ODD = (5, 6, 9, 10, 11)

    def layernorm(src, dst, sq, vs=VS_EVEN, pre=None):
        rsum, negmean, sumsq, stdv, rstd = (vec[i] for i in vs)
        if pre is None:
            vect.wait_ge(sc.h, sc.n)
            vect.wait_ge(ve.h, ve.n)
            scal.wait_ge(ve.h, ve.n)
        else:
            for (h_, v_) in pre:
                vect.wait_ge(h_, v_)
                scal.wait_ge(h_, v_)
        ve.inc(vect.tensor_reduce(rsum[:, :], src, axis=AX.X, op=ALU.add))
        rsum_done = ve.n
        scal.wait_ge(ve.h, rsum_done)
        sc.inc(scal.mul(negmean[:, :], rsum[:, :], -1.0 / D))
        scal.wait_ge(sc.h, sc.n)
        sc.inc(scal.add(dst, src, negmean[:, :]))
        scal.wait_ge(sc.h, sc.n)
        sc.inc(scal.activation(sq, dst, AF.Square, accum_out=sumsq[:, :]))
        scal.wait_ge(sc.h, sc.n)
        sc.inc(scal.activation(stdv[:, :], sumsq[:, :], AF.Sqrt,
                               bias=vec[8][:, :], scale=1.0 / D))
        stdv_done = sc.n
        vect.wait_ge(sc.h, stdv_done)
        ve.inc(vect.reciprocal(rstd[:, :], stdv[:, :]))
        vect.wait_ge(ve.h, ve.n)
        ve.inc(vect.tensor_scalar(dst, dst, rstd[:, :], None, op0=ALU.mult))
        return ve.n

    for tt in range(2):
        sync.dma_start(out=w[tt % 2][:, :],
                       in_=x_hbm[tt * 128:(tt + 1) * 128, :]).then_inc(ld.h, 16)
        ld.n += 16
    lnA = [sb("lnA0", [128, D], F32, RC), sb("lnA1", [128, D], F32, RC + 4096)]
    ln_done = {}
    tr_done = {}
    ev_A = {}

    def a_stage2(tt):
        tens.wait_ge(ve.h, ln_done[tt])
        base = (tt % 2) * 1024
        if tt - 2 in ev_A:
            tens.wait_ge(ve.h, ev_A[tt - 2])
        for dc in range(NDC):
            tm.inc(tens.transpose(
                ps_sc[:, base + dc * 128:base + (dc + 1) * 128],
                lnA[tt % 2][:, dc * 128:(dc + 1) * 128], ident[:, :]))
        tm.wait(scal)
        sl = slice(tt * 128, (tt + 1) * 128)
        for dc in range(NDC):
            sc.inc(scal.copy(lnfm_hi[dc][:, sl],
                             ps_sc[:, base + dc * 128:base + (dc + 1) * 128]))
        sc.wait(vect)
        for dc in range(NDC):
            ve.inc(vect.tensor_tensor(
                lnfm_lo[dc][:, sl],
                ps_sc[:, base + dc * 128:base + (dc + 1) * 128],
                lnfm_hi[dc][:, sl], op=ALU.subtract))
        ev_A[tt] = ve.n
        tr_done[tt] = tm.n

    for tt in range(NTT):
        xs = w[tt % 2]
        pre = [(ld.h, ld.n)]
        if tt >= 2:
            pre.append((tm.h, tr_done[tt - 2]))
        if tt >= 1:
            a_stage2(tt - 1)
        ln_done[tt] = layernorm(xs[:, :], lnA[tt % 2][:, :], w[3][:, :],
                                VS_EVEN if tt % 2 == 0 else VS_ODD, pre=pre)
        if tt + 2 < NTT:
            sync.wait_ge(ve.h, ln_done[tt])
            sync.dma_start(out=xs[:, :],
                           in_=x_hbm[(tt + 2) * 128:(tt + 3) * 128, :]
                           ).then_inc(ld.h, 16)
            ld.n += 16
    a_stage2(NTT - 1)
    nc.all_engine_barrier()
    tap("ln1fmh0", lnfm_hi[0][:, :])
    nc.all_engine_barrier()

    def load_pair_weights(p):
        sync.dma_start(out=w[0][:, :],
                       in_=wq_hbm[p * 128:(p + 1) * 128, :]).then_inc(ld.h, 16)
        sync.dma_start(out=w[1][:, :],
                       in_=wk_hbm[p * 128:(p + 1) * 128, :]).then_inc(ld.h, 16)
        sync.dma_start(out=w[2][:, :],
                       in_=wv_hbm[p * 128:(p + 1) * 128, :]).then_inc(ld.h, 16)
        sync.dma_start(
            out=w[3][:, :].rearrange("p (a c) -> p a c", a=8),
            in_=wo_hbm[:, p * 128:(p + 1) * 128].rearrange(
                "(a p) c -> p a c", p=128)).then_inc(ld.h, 16)
        ld.n += 64
        return ld.n

    SB4 = [ps_t0, ps_t1, ps_av, ps_x]
    AVS = {"n": 0, "meta": {}, "sctm": {}, "exp": {}, "msk": {}, "avtm": {},
           "spl": {}, "bfree": {}, "pend": [], "pend2": [], "eva": {},
           "ev": {}, "r": 0, "slotrec": {}, "sbslot": {}}

    def sb4_take():
        slot = AVS["r"] % 4
        AVS["r"] += 1
        rec = AVS["slotrec"].get(slot)
        if rec is not None:
            kind, val = rec
            if kind == "exp":
                tens.wait_ge(sc.h, val)
            else:
                tens.wait_ge(ve.h, val)
        return slot

    def av_scores(i):
        c, qg, hh = AVS["meta"][i]
        hsl = slice(hh * 64, (hh + 1) * 64)
        qsl = slice(qg * 512, (qg + 1) * 512)
        csl = slice(c * 128, (c + 1) * 128)
        slot = sb4_take()
        AVS["sbslot"][i] = slot
        sb_ = SB4[slot][:, 0:512]
        tens.matmul(sb_, k_hi[hsl, csl], q_hi[hsl, qsl], start=True,
                    stop=False)
        tens.matmul(sb_, k_hi[hsl, csl], q_lo[hsl, qsl], start=False,
                    stop=False)
        st = tens.matmul(sb_, k_lo[hsl, csl], q_hi[hsl, qsl], start=False,
                         stop=True)
        tm.inc(st)
        AVS["sctm"][i] = tm.n

    def av_exp(i):
        c, qg, hh = AVS["meta"][i]
        scal.wait_ge(tm.h, AVS["sctm"][i])
        if i - 3 in AVS["spl"]:
            scal.wait_ge(ve.h, AVS["spl"][i - 3])
        sc.inc(scal.activation(pt3[i % 3][:, 0:512],
                               SB4[AVS["sbslot"][i]][:, 0:512],
                               AF.Exp, bias=vec[10 + hh][:, :], scale=1.0))
        AVS["exp"][i] = sc.n
        AVS["slotrec"][AVS["sbslot"][i]] = ("exp", sc.n)
        if c // 4 == qg:
            gps.wait_ge(sc.h, AVS["exp"][i])
            gps.wait_ge(gp.h, gp.n)
            gp.inc(gps.affine_select(
                pt3[i % 3][:, 0:512], pt3[i % 3][:, 0:512],
                pattern=[[1, 512]], compare_op=ALU.is_ge, fill=0.0,
                base=-128 * (c - 4 * qg), channel_multiplier=-1))
            AVS["msk"][i] = gp.n

    def av_split(i):
        scal.wait_ge(sc.h, AVS["exp"][i])
        if i in AVS["msk"]:
            scal.wait_ge(gp.h, AVS["msk"][i])
        if i - 3 in AVS["avtm"]:
            scal.wait_ge(tm.h, AVS["avtm"][i - 3])
        sc.inc(scal.copy(pth[i % 3][:, 0:512], pt3[i % 3][:, 0:512]))
        hi_sc = sc.n
        vect.wait_ge(sc.h, hi_sc)
        if i - 3 in AVS["avtm"]:
            vect.wait_ge(tm.h, AVS["avtm"][i - 3])
        ve.inc(vect.tensor_tensor(ptl[i % 3][:, 0:512], pt3[i % 3][:, 0:512],
                                  pth[i % 3][:, 0:512], op=ALU.subtract))
        AVS["spl"][i] = ve.n

    def av_av(i):
        c, qg, hh = AVS["meta"][i]
        tens.wait_ge(ve.h, AVS["spl"][i])
        if c == 0 and qg in AVS["bfree"]:
            bf_sc, bf_ve = AVS["bfree"][qg]
            tens.wait_ge(sc.h, bf_sc)
            tens.wait_ge(ve.h, bf_ve)
        if hh == 0:
            vh = v_hi[:, c * 196:c * 196 + 65]
            vl = v_lo[:, c * 196:c * 196 + 65]
            np_out = 65
        else:
            vh = v_hi[:, c * 196 + 66:c * 196 + 194]
            vl = v_lo[:, c * 196 + 66:c * 196 + 194]
            np_out = 128
        bank = ps_sc[0:np_out, qg * 512:(qg + 1) * 512]
        pth_, ptl_ = pth[i % 3][:, 0:512], ptl[i % 3][:, 0:512]
        tens.matmul(bank, vh, pth_, start=(c == 0), stop=False)
        tens.matmul(bank, vh, ptl_, start=False, stop=False)
        st = tens.matmul(bank, vl, pth_, start=False, stop=(c == 4 * qg + 3))
        tm.inc(st)
        AVS["avtm"][i] = tm.n
        if c == 4 * qg + 3:
            AVS["pend"].append((i, AVS["n"]))

    def av_evac_a(i):
        c, qg, hh = AVS["meta"][i]
        den_p = 64 if hh == 0 else 32
        fsl = slice(hh * 64, (hh + 1) * 64)
        prev = AVS["ev"]
        qpsl = slice(qg * 512, (qg + 1) * 512)
        scal.wait_ge(tm.h, AVS["avtm"][i])
        if "mult" in prev:
            scal.wait_ge(ve.h, prev["mult"])
        if "rhl" in prev:
            scal.wait_ge(ve.h, prev["rhl"])
        sc.inc(scal.copy(slotA[fsl, 0:512], ps_sc[fsl, qpsl]))
        copy_sc = sc.n
        vect.wait_ge(tm.h, AVS["avtm"][i])
        if "bcast" in prev:
            vect.wait_ge(tm.h, prev["bcast"])
        if "mult" in prev:
            vect.wait_ge(ve.h, prev["mult"])
        ve.inc(vect.reciprocal(slotA[den_p:den_p + 1, 0:512],
                               ps_sc[den_p:den_p + 1, qpsl]))
        vect.wait_ge(ve.h, ve.n)
        ve.inc(vect.tensor_copy(rh[den_p:den_p + 1, 0:512],
                                slotA[den_p:den_p + 1, 0:512]))
        vect.wait_ge(ve.h, ve.n)
        ve.inc(vect.tensor_tensor(rl[den_p:den_p + 1, 0:512],
                                  slotA[den_p:den_p + 1, 0:512],
                                  rh[den_p:den_p + 1, 0:512],
                                  op=ALU.subtract))
        rhl_ve = ve.n
        AVS["bfree"][qg] = (copy_sc, rhl_ve)
        AVS["eva"][i] = (rhl_ve, copy_sc)
        prev["rhl"] = rhl_ve

    def av_evac_b(i):
        c, qg, hh = AVS["meta"][i]
        den_p = 64 if hh == 0 else 32
        fsl = slice(hh * 64, (hh + 1) * 64)
        prev = AVS["ev"]
        qpsl = slice(qg * 512, (qg + 1) * 512)
        rhl_ve, copy_sc = AVS["eva"][i]
        slot = sb4_take()
        tens.wait_ge(ve.h, rhl_ve)
        tens.matmul(SB4[slot][:, 0:512], ones_pbf[den_p:den_p + 1, 0:128],
                    rh[den_p:den_p + 1, 0:512], start=True, stop=False)
        st = tens.matmul(SB4[slot][:, 0:512],
                         ones_pbf[den_p:den_p + 1, 0:128],
                         rl[den_p:den_p + 1, 0:512], start=False, stop=True)
        tm.inc(st)
        bcast_tm = tm.n
        vect.wait_ge(tm.h, bcast_tm)
        vect.wait_ge(sc.h, copy_sc)
        if "hi" in prev:
            vect.wait_ge(sc.h, prev["hi"])
        if "lo" in prev:
            vect.wait_ge(ve.h, prev["lo"])
        ve.inc(vect.tensor_tensor(hof32[fsl, 0:512], slotA[fsl, 0:512],
                                  SB4[slot][fsl, 0:512], op=ALU.mult))
        mult_ve = ve.n
        AVS["slotrec"][slot] = ("mult", mult_ve)
        scal.wait_ge(ve.h, mult_ve)
        sc.inc(scal.copy(hofm_hi[fsl, qpsl], hof32[fsl, 0:512]))
        hi_sc = sc.n
        vect.wait_ge(sc.h, hi_sc)
        ve.inc(vect.tensor_tensor(hofm_lo[fsl, qpsl], hof32[fsl, 0:512],
                                  hofm_hi[fsl, qpsl], op=ALU.subtract))
        prev.update({"bcast": bcast_tm, "mult": mult_ve,
                     "hi": hi_sc, "lo": ve.n})

    def av_drain(force=False):
        if force:
            while AVS["pend2"]:
                i, _ = AVS["pend2"].pop(0)
                av_evac_b(i)
            while AVS["pend"]:
                i, _ = AVS["pend"].pop(0)
                av_evac_a(i)
                av_evac_b(i)
            return
        while AVS["pend"] and AVS["pend"][0][1] <= AVS["n"] - 1:
            i, _ = AVS["pend"].pop(0)
            av_evac_a(i)
            AVS["pend2"].append((i, AVS["n"]))
        while AVS["pend2"] and AVS["pend2"][0][1] <= AVS["n"] - 2:
            i, _ = AVS["pend2"].pop(0)
            av_evac_b(i)

    wrec = load_pair_weights(0)
    accrec = {}
    for p in range(NPAIR):
        tens.wait_ge(ld.h, wrec)

        def transpose_split(nat, dst_hi, dst_lo):
            tens.wait_ge(ve.h, ve.n)
            tens.wait_ge(sc.h, sc.n)
            for half in range(2):
                pp = ps_t[half]
                for a4 in range(4):
                    a = half * 4 + a4
                    tm.inc(tens.transpose(pp[:, a4 * 128:(a4 + 1) * 128],
                                          nat[:, a * 128:(a + 1) * 128],
                                          ident[:, :]))
                tm.wait(scal)
                sl = slice(half * 512, (half + 1) * 512)
                sc.inc(scal.copy(dst_hi[:, sl], pp[:, 0:512]))
                sc.wait(vect)
                ve.inc(vect.tensor_tensor(dst_lo[:, sl], pp[:, 0:512],
                                          dst_hi[:, sl], op=ALU.subtract))

        evq = {}
        for side, nat in enumerate((w[0], w[1])):
            transpose_split(nat, wt_hi, wt_lo)
            ve.wait(tens)
            dst_hi, dst_lo = (q_hi, q_lo) if side == 0 else (k_hi, k_lo)
            for tc in range(4):
                g = side * 4 + tc
                pp = ps_sc[:, (g % 2) * 512:(g % 2) * 512 + 512]
                tsl = slice(tc * 512, (tc + 1) * 512)
                if g >= 2:
                    tens.wait_ge(ve.h, evq[g - 2])
                for dc in range(NDC):
                    dsl = slice(dc * 128, (dc + 1) * 128)
                    tens.matmul(pp, wt_hi[:, dsl], lnfm_hi[dc][:, tsl],
                                start=(dc == 0), stop=False)
                    tens.matmul(pp, wt_hi[:, dsl], lnfm_lo[dc][:, tsl],
                                start=False, stop=False)
                    st = tens.matmul(pp, wt_lo[:, dsl], lnfm_hi[dc][:, tsl],
                                     start=False, stop=(dc == NDC - 1))
                tm.inc(st)
                tm.wait(scal)
                if side == 0:
                    sc.inc(scal.mul(dst_hi[:, tsl], pp, 0.125))
                    sc.wait(vect)
                    ve.inc(vect.scalar_tensor_tensor(
                        dst_lo[:, tsl], pp, 0.125, dst_hi[:, tsl],
                        op0=ALU.mult, op1=ALU.subtract))
                else:
                    sc.inc(scal.copy(dst_hi[:, tsl], pp))
                    sc.wait(vect)
                    ve.inc(vect.tensor_tensor(dst_lo[:, tsl], pp,
                                              dst_hi[:, tsl], op=ALU.subtract))
                evq[g] = ve.n

        transpose_split(w[2], wt_hi, wt_lo)
        ve.wait(tens)
        evv = {}
        evs = {}
        for tt in range(NTT):
            pp = ps_t[tt % 2]
            if tt >= 2:
                tens.wait_ge(ve.h, evv[tt - 2])
                tens.wait_ge(sc.h, evs[tt - 2])
            for dc in range(NDC):
                dsl = slice(dc * 128, (dc + 1) * 128)
                tsl = slice(tt * 128, (tt + 1) * 128)
                tens.matmul(pp[:, 0:128], lnfm_hi[dc][:, tsl], wt_hi[:, dsl],
                            start=(dc == 0), stop=False)
                tens.matmul(pp[:, 0:128], lnfm_hi[dc][:, tsl], wt_lo[:, dsl],
                            start=False, stop=False)
                st = tens.matmul(pp[:, 0:128], lnfm_lo[dc][:, tsl],
                                 wt_hi[:, dsl], start=False,
                                 stop=(dc == NDC - 1))
            tm.inc(st)
            tm.wait(scal)
            tm.wait(vect)
            if p == 0 and tt == 0:
                scal.wait_ge(gp.h, vall_ms)
                vect.wait_ge(gp.h, vall_ms)
            sc.inc(scal.copy(v_hi[:, tt * 196:tt * 196 + 64], pp[:, 0:64]))
            sc.inc(scal.copy(v_hi[:, tt * 196 + 130:tt * 196 + 194],
                             pp[:, 64:128]))
            evs[tt] = sc.n
            sc.wait(vect)
            ve.inc(vect.tensor_tensor(v_lo[:, tt * 196:tt * 196 + 64],
                                      pp[:, 0:64],
                                      v_hi[:, tt * 196:tt * 196 + 64],
                                      op=ALU.subtract))
            ve.inc(vect.tensor_tensor(v_lo[:, tt * 196 + 130:tt * 196 + 194],
                                      pp[:, 64:128],
                                      v_hi[:, tt * 196 + 130:tt * 196 + 194],
                                      op=ALU.subtract))
            evv[tt] = ve.n

        if p == 0:
            scal.wait_ge(sg.h, sg.n)
            vect.wait_ge(sg.h, sg.n)
        transpose_split(w[3], wot_hi, wot_lo)
        if p + 1 < NPAIR:
            sync.wait_ge(tm.h, tm.n)
            wrec = load_pair_weights(p + 1)

        vect.wait_ge(sc.h, sc.n)
        for side, srcb in enumerate((q_hi, k_hi)):
            scal.wait_ge(tm.h, tm.n)
            scal.wait_ge(ve.h, ve.n)
            sc.inc(scal.activation(sqscr[:, :], srcb[:, :], AF.Square))
            tens.wait_ge(sc.h, sc.n)
            tens.wait_ge(ve.h, ve.n)
            last = None
            for g in range(4):
                last = tens.matmul(ps_sc[0:2, g * 512:(g + 1) * 512],
                                   ones2[:, :], sqscr[:, g * 512:(g + 1) * 512],
                                   start=True, stop=True)
            tm.inc(last)
            tm.wait(vect)
            ve.inc(vect.tensor_reduce(bmx[0:2, side:side + 1], ps_sc[0:2, :],
                                      axis=AX.X, op=ALU.max))
            vect.wait_ge(ve.h, ve.n)
        ve.inc(vect.tensor_tensor(bmx[0:2, 0:1], bmx[0:2, 0:1], bmx[0:2, 1:2],
                                  op=ALU.mult))
        ve.wait(scal)
        sc.inc(scal.activation(bmx[0:2, 1:2], bmx[0:2, 0:1], AF.Sqrt))
        sc.wait(tens)
        tm.inc(tens.transpose(ps_t1[0:1, 0:2], bmx[0:2, 1:2], ident[0:2, 0:2]))
        tm.wait(vect)
        ve.inc(vect.tensor_copy(bmx[0:1, 0:2], ps_t1[0:1, 0:2]))
        ve.wait(tens)
        for hh2 in range(2):
            tm.inc(tens.matmul(ps_t1[:, 0:1], neg1_row[:, :],
                               bmx[0:1, hh2:hh2 + 1], start=True, stop=True))
            tm.wait(vect)
            ve.inc(vect.tensor_copy(vec[10 + hh2][:, :], ps_t1[:, 0:1]))
            tens.wait_ge(ve.h, ve.n)
        nc.all_engine_barrier()
        if debug and p == 0:
            tap("qh0", q_hi[:, :]); tap("ql0", q_lo[:, :])
            tap("kh0", k_hi[:, :]); tap("v00", v_hi[:, :])
            tap("bv0", vec[10][:, :])
            nc.all_engine_barrier()

        pb = AVS["n"]
        for hh in range(2):
            for c in range(16):
                for qg in range(c // 4, 4):
                    i = AVS["n"]
                    AVS["n"] += 1
                    AVS["meta"][i] = (c, qg, hh)
                    if AV_BATCH:
                        if i % 2 == 1:
                            for k in (i - 7, i - 6):
                                if k >= pb:
                                    av_av(k)
                            for k in (i - 5, i - 4):
                                if k >= pb:
                                    av_split(k)
                            for k in (i - 3, i - 2):
                                if k >= pb:
                                    av_exp(k)
                            av_scores(i - 1)
                            av_scores(i)
                            av_drain()
                    else:
                        av_scores(i)
                        if i - 3 >= pb:
                            av_exp(i - 3)
                        if i - 7 >= pb:
                            av_av(i - 7)
                        if i - 5 >= pb:
                            av_split(i - 5)
                        av_drain()
        last = AVS["n"] - 1
        if AV_BATCH:
            av_av(last - 5)
            av_av(last - 4)
            av_split(last - 3)
            av_split(last - 2)
            av_exp(last - 1)
            av_exp(last)
            av_av(last - 3)
            av_av(last - 2)
            av_split(last - 1)
            av_split(last)
            av_av(last - 1)
            av_av(last)
        else:
            av_av(last - 6)
            av_av(last - 5)
            av_split(last - 4)
            av_split(last - 3)
            av_exp(last - 2)
            av_exp(last - 1)
            av_exp(last)
            av_av(last - 4)
            av_av(last - 3)
            av_split(last - 2)
            av_split(last - 1)
            av_av(last - 2)
            av_split(last)
            av_av(last - 1)
            av_av(last)
        av_drain(force=True)
        nc.all_engine_barrier()
        if debug and p == 0:
            tap("hofmh0", hofm_hi[:, :]); tap("hofml0", hofm_lo[:, :])
            nc.all_engine_barrier()

        evrec = {}
        for tt in range(NTT):
            for oc in range(2):
                g = tt * 2 + oc
                pp = (ps_av, ps_x)[oc]
                tsl = slice(tt * 128, (tt + 1) * 128)
                osl = slice(oc * 512, (oc + 1) * 512)
                if g >= 2:
                    tens.wait_ge(ve.h, evrec[g - 2])
                tens.matmul(pp[:, :], hofm_hi[:, tsl], wot_hi[:, osl],
                            start=True, stop=False)
                tens.matmul(pp[:, :], hofm_hi[:, tsl], wot_lo[:, osl],
                            start=False, stop=False)
                st = tens.matmul(pp[:, :], hofm_lo[:, tsl], wot_hi[:, osl],
                                 start=False, stop=True)
                tm.inc(st)
                tm.wait(vect)
                if p > 0:
                    vect.wait_ge(ve.h, accrec[g])
                dst = acc_tm[tt][:, oc * 512:(oc + 1) * 512]
                if p == 0:
                    ve.inc(vect.tensor_copy(dst, pp[:, :]))
                else:
                    ve.inc(vect.tensor_tensor(dst, dst, pp[:, :], op=ALU.add))
                evrec[g] = ve.n
                accrec[g] = ve.n
        nc.all_engine_barrier()

    tap("acc0", acc_tm[0][:, :]); tap("acc15", acc_tm[15][:, :])
    nc.all_engine_barrier()
    dD = [sb("dD0", [128, D], F32, RC + 32768), sb("dD1", [128, D], F32, RC + 36864)]
    dE = [sb("dE0", [128, D], F32, RC + 40960), w[2]]
    lnE_hi = [sb("lnEh0", [128, 1024], BF16, RC + 45056),
              sb("lnEh1", [128, 1024], BF16, RC + 49152)]
    lnE_lo = [sb("lnEl0", [128, 1024], BF16, RC + 47104),
              sb("lnEl1", [128, 1024], BF16, RC + 51200)]
    obuf = [sb(f"obuf{i}", [128, D], F32, RC + 45056 + 4096 * i) for i in range(2)]
    mstage = dD[0]

    sync.wait_ge(sg.h, sg.n)
    sync.wait_ge(gp.h, cc2)
    sync.wait_ge(ld.h, ld.n)
    gps.wait_ge(sg.h, sg.n)
    gps.dma_start(out=bias_bc[0:1, :], in_=bias_hbm[:, :]).then_inc(sg.h, 16)
    sg.n += 16
    bias_ld = sg.n
    mstages = [dD[0], dD[1]]
    gps.wait_ge(sg.h, bias_ld)
    m_ms, spl_rec = {}, {}

    def _m_split(dc):
        eng_sem = ld if dc % 2 == 0 else sg
        scal.wait_ge(eng_sem.h, m_ms[dc])
        sc.inc(scal.copy(m_hi[dc][:, :], mstages[dc % 2][:, :]))
        sc.wait(vect)
        ve.inc(vect.tensor_tensor(m_lo[dc][:, :], mstages[dc % 2][:, :],
                                  m_hi[dc][:, :], op=ALU.subtract))
        spl_rec[dc] = ve.n

    for dc in range(NDC):
        eng, sem = (sync, ld) if dc % 2 == 0 else (gps, sg)
        if dc >= 2:
            eng.wait_ge(ve.h, spl_rec[dc - 2])
        eng.dma_start(out=mstages[dc % 2][:, :],
                      in_=m_full.ap()[dc * 128:(dc + 1) * 128, :]
                      ).then_inc(sem.h, 16)
        sem.n += 16
        m_ms[dc] = sem.n
        if dc >= 1:
            _m_split(dc - 1)
    _m_split(NDC - 1)
    tens.wait_ge(sg.h, bias_ld)
    for oc in range(2):
        tm.inc(tens.matmul(ps_x[:, :], ones_row[:, :],
                           bias_bc[0:1, oc * 512:(oc + 1) * 512],
                           start=True, stop=True))
        tm.wait(vect)
        ve.inc(vect.tensor_copy(bias_bc[:, oc * 512:(oc + 1) * 512], ps_x[:, :]))
        tens.wait_ge(ve.h, ve.n)
    sync.wait_ge(ld.h, ld.n)
    for tt in range(2):
        sync.dma_start(out=w[tt % 2][:, :],
                       in_=x_hbm[tt * 128:(tt + 1) * 128, :]).then_inc(ld.h, 16)
        ld.n += 16

    x1_done = {}
    lnE_done = {}
    out_dma = {}
    fmm_done = {}
    fy_ve = {}
    ev_done = {}
    ob_done = {}

    def def_s12(tt):
        r = tt % 2
        pre = [(ld.h, ld.n)]
        if tt >= 2:
            pre.append((tm.h, fmm_done.get(tt - 2, 0)))
        layernorm(acc_tm[tt][:, :], dD[r][:, :], w[3][:, :],
                  VS_EVEN if r == 0 else VS_ODD, pre=pre)
        vect.wait_ge(ve.h, ve.n)
        ve.inc(vect.tensor_tensor(x1_tm[tt][:, :], dD[r][:, :],
                                  w[tt % 2][:, :], op=ALU.add))
        x1_done[tt] = ve.n
        if tt + 2 < NTT:
            sync.wait_ge(ve.h, x1_done[tt])
            sync.dma_start(out=w[tt % 2][:, :],
                           in_=x_hbm[(tt + 2) * 128:(tt + 3) * 128, :]
                           ).then_inc(ld.h, 16)
            ld.n += 16
        layernorm(x1_tm[tt][:, :], dE[r][:, :], w[3][:, :],
                  VS_EVEN if r == 0 else VS_ODD,
                  pre=[(ve.h, x1_done[tt])])
        lnE_done[tt] = ve.n

    def def_s3a(tt):
        r = tt % 2
        tens.wait_ge(ve.h, lnE_done[tt])
        if tt >= 2:
            scal.wait_ge(sg.h, out_dma[tt - 2])
            vect.wait_ge(sg.h, out_dma[tt - 2])
            tens.wait_ge(ve.h, ob_done[tt - 2])
        for half in range(2):
            pp = ps_t[half]
            if (tt - 1, half) in ev_done:
                tens.wait_ge(ve.h, ev_done[(tt - 1, half)])
            for dc4 in range(4):
                dc = half * 4 + dc4
                tm.inc(tens.transpose(pp[:, dc4 * 128:(dc4 + 1) * 128],
                                      dE[r][:, dc * 128:(dc + 1) * 128],
                                      ident[:, :]))
            tm.wait(scal)
            hsl = slice(half * 512, (half + 1) * 512)
            sc.inc(scal.copy(lnE_hi[r][:, hsl], pp[:, 0:512]))
            sc.wait(vect)
            ve.inc(vect.tensor_tensor(lnE_lo[r][:, hsl], pp[:, 0:512],
                                      lnE_hi[r][:, hsl], op=ALU.subtract))
            ev_done[(tt, half)] = ve.n

    def def_s3b(tt):
        r = tt % 2
        fy_t = dD[r]
        for oc in range(2):
            pp = ps_sc[:, (oc + 2 * (tt % 2)) * 512:
                       (oc + 2 * (tt % 2)) * 512 + 512]
            osl = slice(oc * 512, (oc + 1) * 512)
            tens.wait_ge(ve.h, ev_done[(tt, 1)])
            for dc in range(NDC):
                dsl = slice(dc * 128, (dc + 1) * 128)
                tens.matmul(pp, lnE_hi[r][:, dsl], m_hi[dc][:, osl],
                            start=(dc == 0), stop=False)
                tens.matmul(pp, lnE_hi[r][:, dsl], m_lo[dc][:, osl],
                            start=False, stop=False)
                st = tens.matmul(pp, lnE_lo[r][:, dsl], m_hi[dc][:, osl],
                                 start=False, stop=(dc == 7))
            tm.inc(st)
            tm.wait(vect)
            ve.inc(vect.tensor_tensor(fy_t[:, osl], pp,
                                      bias_bc[:, osl], op=ALU.add))
        fmm_done[tt] = tm.n
        fy_ve[tt] = ve.n

    def def_s3b2(tt):
        r = tt % 2
        fy_t = dD[r]
        layernorm(fy_t[:, :], dE[r][:, :], w[3][:, :],
                  VS_EVEN if r == 0 else VS_ODD, pre=[(ve.h, fy_ve[tt])])
        ob = obuf[r]
        if tt >= 2:
            vect.wait_ge(sg.h, sg.n)
        vect.wait_ge(ve.h, ve.n)
        ve.inc(vect.tensor_tensor(ob[:, :], dE[r][:, :], x1_tm[tt][:, :],
                                  op=ALU.add))
        ob_done[tt] = ve.n
        ve.wait(gps)
        gps.wait_ge(sg.h, sg.n)
        gps.dma_start(out=out_hbm[tt * 128:(tt + 1) * 128, :],
                      in_=ob[:, :]).then_inc(sg.h, 16)
        sg.n += 16
        out_dma[tt] = sg.n

    for tt in range(NTT):
        if tt >= 1:
            def_s3a(tt - 1)
            def_s3b(tt - 1)
        def_s12(tt)
        if tt >= 1:
            def_s3b2(tt - 1)
    def_s3a(NTT - 1)
    def_s3b(NTT - 1)
    def_s3b2(NTT - 1)
    gps.wait_ge(sg.h, sg.n)
    return nc


_NC_CACHE = None


def kernel(L0_attn_x, L0_attn_wq, L0_attn_wk, L0_attn_wv, L0_attn_wo,
           L0_ffn_w_freq_real, L0_ffn_w_freq_imag, L0_ffn_bias):
    global _NC_CACHE
    x = np.ascontiguousarray(np.asarray(L0_attn_x, dtype=np.float32))
    wq = np.ascontiguousarray(np.asarray(L0_attn_wq, np.float32).reshape(H * DH, D))
    wk = np.ascontiguousarray(np.asarray(L0_attn_wk, np.float32).reshape(H * DH, D))
    wv = np.ascontiguousarray(np.asarray(L0_attn_wv, np.float32).reshape(H * DH, D))
    wo = np.ascontiguousarray(np.asarray(L0_attn_wo, np.float32))
    wfr = np.ascontiguousarray(np.asarray(L0_ffn_w_freq_real, np.float32))
    wfi = np.ascontiguousarray(np.asarray(L0_ffn_w_freq_imag, np.float32))
    bias = np.ascontiguousarray(np.asarray(L0_ffn_bias, np.float32).reshape(1, D))
    if _NC_CACHE is None:
        _NC_CACHE = build_nc()
    nc = _NC_CACHE
    in_maps = [{"x": np.ascontiguousarray(x[b]), "wq": wq, "wk": wk, "wv": wv,
                "wo": wo,
                "wfr": np.ascontiguousarray(wfr[b * 128:(b + 1) * 128]),
                "wfi": np.ascontiguousarray(wfi[b * 128:(b + 1) * 128]),
                "coff": np.full((1, 1), b * 128, np.float32),
                "bias": bias}
               for b in range(8)]
    res = run_bass_kernel_spmd(nc, in_maps, core_ids=list(range(8)))
    return np.stack([res.results[b]["out"] for b in range(8)], axis=0)


# version 25 (final)
# speedup vs baseline: 1.0654x; 1.0460x over previous
import numpy as np
import concourse.bass as bass
import concourse.mybir as mybir
from concourse.bass_utils import run_bass_kernel_spmd

F32 = mybir.dt.float32
I32 = mybir.dt.int32
AF = mybir.ActivationFunctionType
ALU = mybir.AluOpType
AX = mybir.AxisListType

D = 1024
T = 2048
H = 16
DH = 64
NPAIR = 8
NTT = T // 128
NDC = D // 128
VST = 80
EPS = 1e-5
NEG = -1.0e9
AV_BATCH = True

TWO_PI_OVER_D = 2.0 * np.pi / 1024.0
TPD_HI = float(np.float32(TWO_PI_OVER_D))
TPD_LO = float(np.float32(TWO_PI_OVER_D - TPD_HI))
HALF_PI = float(np.float32(np.pi / 2))


class Sem:

    def __init__(self, nc, name):
        self.h = nc.alloc_semaphore(name)
        self.n = 0

    def inc(self, inst, k=1):
        inst.then_inc(self.h, k)
        self.n += k
        return self.n

    def wait(self, eng, n=None):
        eng.wait_ge(self.h, self.n if n is None else n)


def build_nc(debug=False):
    nc = bass.Bass("TRN2", target_bir_lowering=False)

    x_hbm = nc.dram_tensor("x", [T, D], F32, kind="ExternalInput").ap()
    wq_hbm = nc.dram_tensor("wq", [H * DH, D], F32, kind="ExternalInput").ap()
    wk_hbm = nc.dram_tensor("wk", [H * DH, D], F32, kind="ExternalInput").ap()
    wv_hbm = nc.dram_tensor("wv", [H * DH, D], F32, kind="ExternalInput").ap()
    wo_hbm = nc.dram_tensor("wo", [D, D], F32, kind="ExternalInput").ap()
    wfr_hbm = nc.dram_tensor("wfr", [128, D], F32, kind="ExternalInput").ap()
    wfi_hbm = nc.dram_tensor("wfi", [128, D], F32, kind="ExternalInput").ap()
    coff_hbm = nc.dram_tensor("coff", [1, 1], F32, kind="ExternalInput").ap()
    bias_hbm = nc.dram_tensor("bias", [1, D], F32, kind="ExternalInput").ap()
    out_hbm = nc.dram_tensor("out", [T, D], F32, kind="ExternalOutput").ap()
    tu_in = nc.dram_tensor("tu_in", [256, D], F32, kind="Internal")
    tu_full = nc.dram_tensor("tu_full", [2048, D], F32, kind="Internal",
                             addr_space="Shared")
    m_in = nc.dram_tensor("m_in", [128, D], F32, kind="Internal")
    m_full = nc.dram_tensor("m_full", [D, D], F32, kind="Internal",
                            addr_space="Shared")

    BASE = 17408

    def sb(name, shape, dtype, off):
        return nc.alloc_sbuf_tensor_at(name, shape, dtype, offset=off).ap()

    ident = sb("ident", [128, 128], F32, BASE)
    maskt = sb("maskt", [128, 128], F32, BASE + 512)
    BF16 = mybir.dt.bfloat16
    VEC0 = BASE + 1024
    vec = [sb(f"vec{i}", [128, 1], F32, VEC0 + 32 * i) for i in range(16)]
    ones_row = sb("ones_row", [1, 128], F32, VEC0 + 512)
    neg1_row = sb("neg1_row", [1, 128], F32, VEC0 + 512 + 512)
    neg64_row = sb("neg64_row", [1, 128], F32, VEC0 + 2112)
    bmx = sb("bmx", [128, 2], F32, VEC0 + 2624)
    coff_s = sb("coff_s", [1, 1], F32, VEC0 + 2656)
    ones2 = sb("ones2", [128, 2], F32, VEC0 + 2688)
    ones_col = sb("ones_col", [128, 1], F32, VEC0 + 1536)
    ident_bf = sb("ident_bf", [128, 128], BF16, VEC0 + 1600)
    maskt_bf = sb("maskt_bf", [128, 128], BF16, VEC0 + 1856)
    bias_bc = sb("bias_bc", [128, D], F32, BASE + 4096)
    wot_hi = sb("wot_hi", [128, D], BF16, BASE + 4096)
    wot_lo = sb("wot_lo", [128, D], BF16, BASE + 4096 + 2048)

    W0 = BASE + 8192
    w = [sb(f"wtmp{i}", [128, D], F32, W0 + 4096 * i) for i in range(4)]

    RA = W0 + 4 * 4096
    RB = RA + 65536
    RC = RB + 65536
    assert RC + 55808 <= 229376, (RC + 55808)

    c_sb = [sb(f"cdft{i}", [128, D], F32, RA + 4096 * i) for i in range(8)]
    s_sb = [sb(f"sdft{i}", [128, D], F32, RA + 32768 + 4096 * i) for i in range(8)]
    lnfm_hi = [sb(f"lnfmh{i}", [128, T], BF16, RA + 8192 * i) for i in range(8)]
    lnfm_lo = [sb(f"lnfml{i}", [128, T], BF16, RA + 8192 * i + 4096) for i in range(8)]
    x1_tm = [sb(f"x1tm{i}", [128, D], F32, RA + 4096 * i) for i in range(16)]

    wrtS = sb("wrtS", [128, D], F32, RB)
    witS = sb("witS", [128, D], F32, RB + 4096)
    acc_tm = [sb(f"acctm{i}", [128, D], F32, RB + 4096 * i) for i in range(16)]
    ln2fm_hi = [sb(f"ln2fmh{i}", [128, T], BF16, RB + 8192 * i) for i in range(8)]
    ln2fm_lo = [sb(f"ln2fml{i}", [128, T], BF16, RB + 8192 * i + 4096)
                for i in range(8)]

    tp_sh = sb("tp_sh", [128, D], F32, RC)
    up_sh = sb("up_sh", [128, D], F32, RC + 4096)
    cbt = [sb(f"cbt{i}", [128, 128], F32, RC + 8192 + 512 * i) for i in range(8)]
    sbt = [sb(f"sbt{i}", [128, 128], F32, RC + 12288 + 512 * i) for i in range(8)]
    tTs = [sb(f"tTs{i}", [128, D], F32, RC + 16384 + 4096 * i) for i in range(2)]
    tUs = [sb(f"tUs{i}", [128, D], F32, RC + 24576 + 4096 * i) for i in range(2)]
    msh = sb("msh", [128, D], F32, RC + 32768)
    gfj = sb("gfj", [128, 128], F32, RC + 36864)
    gfv = sb("gfv", [128, 128], F32, RC + 37376)
    gwk = sb("gwk", [128, 128], F32, RC + 37888)
    gtk = sb("gtk", [128, 128], F32, RC + 38400)
    mvs = sb("mvs", [128, 8], F32, RC + 38912)
    q_hi = sb("q_hi", [128, T], BF16, RC)
    q_lo = sb("q_lo", [128, T], BF16, RC + 4096)
    k_hi = sb("k_hi", [128, T], BF16, RC + 8192)
    k_lo = sb("k_lo", [128, T], BF16, RC + 12288)
    v_hi = sb("v_hi", [128, 16 * 196], BF16, RC + 16384)
    v_lo = sb("v_lo", [128, 16 * 196], BF16, RC + 22656)
    slotA = sb("slotA", [128, 512], F32, RC + 28928)
    hof32 = sb("hof32", [128, 512], F32, RC + 30976)
    pt3 = [sb(f"pt3_{i}", [128, 512], F32, RC + 33024 + 2048 * i)
           for i in range(3)]
    pth = [sb(f"pth{i}", [128, 512], BF16, RC + 39168 + 1024 * i)
           for i in range(3)]
    ptl = [sb(f"ptl{i}", [128, 512], BF16, RC + 42240 + 1024 * i)
           for i in range(3)]
    hofm_hi = sb("hofm_hi", [128, T], BF16, RC + 45312)
    hofm_lo = sb("hofm_lo", [128, T], BF16, RC + 49408)
    sqscr = sb("sqscr", [128, T], F32, RC + 45312)
    wt_hi = sb("wt_hi", [128, D], BF16, RC + 33024)
    wt_lo = sb("wt_lo", [128, D], BF16, RC + 35072)
    ones_pbf = sb("ones_pbf", [128, 128], BF16, RC + 53504)
    rh = sb("rh", [128, 512], BF16, RC + 53760)
    rl = sb("rl", [128, 512], BF16, RC + 54784)
    m_hi = [sb(f"mhi{i}", [128, D], BF16, RC + 2048 * i) for i in range(8)]
    m_lo = [sb(f"mlo{i}", [128, D], BF16, RC + 16384 + 2048 * i)
            for i in range(8)]
    fy = sb("fy", [128, D], F32, RC + 32768)
    fln = sb("fln", [128, D], F32, RC + 36864)
    fout = [sb(f"fout{i}", [128, D], F32, RC + 40960 + 4096 * i) for i in range(2)]
    fsq = sb("fsq", [128, D], F32, RC + 49152)

    ps_sc = nc.alloc_psum_tensor("ps_sc", [128, 2048], F32).ap()
    ps_t0 = nc.alloc_psum_tensor("ps_t0", [128, 512], F32).ap()
    ps_t1 = nc.alloc_psum_tensor("ps_t1", [128, 512], F32).ap()
    ps_av = nc.alloc_psum_tensor("ps_av", [128, 512], F32).ap()
    ps_x = nc.alloc_psum_tensor("ps_x", [128, 512], F32).ap()
    ps_t = [ps_t0, ps_t1]

    ld = Sem(nc, "ld")
    sg = Sem(nc, "sg")
    tm = Sem(nc, "tm")
    ve = Sem(nc, "ve")
    sc = Sem(nc, "sc")
    gp = Sem(nc, "gp")

    tens, vect, scal, gps, sync = nc.tensor, nc.vector, nc.scalar, nc.gpsimd, nc.sync

    def tap(name, ap):
        if not debug:
            return
        d = nc.dram_tensor("tap_" + name, list(ap.shape), F32,
                           kind="ExternalOutput").ap()
        gps.dma_start(out=d, in_=ap).then_inc(sg.h, 16)
        sg.n += 16
        gps.wait_ge(sg.h, sg.n)

    gp.inc(gps.memset(vec[7][:, :], HALF_PI))
    gp.inc(gps.memset(vec[8][:, :], EPS))
    gp.inc(gps.memset(ones_row[:, :], 1.0))
    gp.inc(gps.memset(neg1_row[:, :], -1.0))
    gp.inc(gps.memset(neg64_row[:, :], -64.0))
    gp.inc(gps.memset(ones_col[:, :], 1.0))
    gp.inc(gps.memset(ones2[:, :], 0.0))
    gp.inc(gps.memset(ones_pbf[:, :], 1.0))
    gp.inc(gps.iota(w[2][:, 0:128], pattern=[[-1, 128]], base=0,
                    channel_multiplier=1,
                    allow_small_or_imprecise_dtypes=True))
    gp.wait(vect)
    ve.inc(vect.tensor_scalar(ident[:, :], w[2][:, 0:128], 0.0, None,
                              op0=ALU.is_equal))
    vect.wait_ge(ve.h, ve.n)
    gp.inc(gps.memset(maskt[:, :], 0.0))
    gps.wait_ge(gp.h, gp.n)
    gp.inc(gps.affine_select(maskt[:, :], maskt[:, :], pattern=[[1, 128]],
                             compare_op=ALU.is_ge, fill=NEG, base=0,
                             channel_multiplier=-1))
    gp.inc(gps.memset(ones2[0:64, 0:1], 1.0))
    gp.inc(gps.memset(ones2[64:128, 1:2], 1.0))
    gp.wait(vect)
    ve.inc(vect.tensor_copy(ident_bf[:, :], ident[:, :]))
    ve.inc(vect.tensor_copy(maskt_bf[:, :], maskt[:, :]))
    gps.wait_ge(ve.h, ve.n)
    gp.inc(gps.iota(w[2][:, :], pattern=[[1, D]], base=0, channel_multiplier=0,
                    allow_small_or_imprecise_dtypes=True))
    for dc in range(NDC):
        if dc > 0:
            gps.wait_ge(ve.h, ve.n)
        gp.inc(gps.iota(vec[0][:, :], pattern=[[0, 1]], base=dc * 128,
                        channel_multiplier=1,
                        allow_small_or_imprecise_dtypes=True))
        gp.wait(vect)
        if dc > 0:
            vect.wait_ge(sc.h, sc.n)
        def _step(inst):
            ve.inc(inst)
            vect.wait_ge(ve.h, ve.n)
        _step(vect.tensor_scalar(w[3][:, :], w[2][:, :], vec[0][:, :], None,
                                 op0=ALU.mult))
        _step(vect.tensor_copy(c_sb[dc][:, :].bitcast(I32), w[3][:, :]))
        _step(vect.tensor_scalar(c_sb[dc][:, :].bitcast(I32),
                                 c_sb[dc][:, :].bitcast(I32), 1023, None,
                                 op0=ALU.bitwise_and))
        _step(vect.tensor_copy(w[3][:, :], c_sb[dc][:, :].bitcast(I32)))
        _step(vect.tensor_scalar(c_sb[dc][:, :], w[3][:, :], 511.5, None,
                                 op0=ALU.is_gt))
        _step(vect.scalar_tensor_tensor(w[3][:, :], c_sb[dc][:, :], -1024.0,
                                        w[3][:, :], op0=ALU.mult, op1=ALU.add))
        _step(vect.tensor_scalar(s_sb[dc][:, :], w[3][:, :], TPD_HI, None,
                                 op0=ALU.mult))
        ve.inc(vect.scalar_tensor_tensor(w[3][:, :], w[3][:, :], TPD_LO,
                                         s_sb[dc][:, :], op0=ALU.mult, op1=ALU.add))
        ve.wait(scal)
        sc.inc(scal.activation(s_sb[dc][:, :], w[3][:, :], AF.Sin))
        sc.inc(scal.activation(w[3][:, :], w[3][:, :], AF.Abs))
        scal.wait_ge(sc.h, sc.n)
        sc.inc(scal.activation(c_sb[dc][:, :], w[3][:, :], AF.Sin,
                               bias=vec[7][:, :], scale=-1.0))
    nc.all_engine_barrier()
    tap("c0", c_sb[0][:, :]); tap("s0", s_sb[0][:, :])
    tap("ident", ident[:, :]); tap("maskt", maskt[:, :])
    nc.all_engine_barrier()

    sync.dma_start(out=w[2][:, :], in_=wfr_hbm[:, :]).then_inc(ld.h, 16)
    sync.dma_start(out=w[3][:, :], in_=wfi_hbm[:, :]).then_inc(ld.h, 16)
    sync.dma_start(out=coff_s[:, :], in_=coff_hbm[:, :]).then_inc(ld.h, 16)
    ld.n += 48
    tens.wait_ge(ld.h, ld.n)
    p1rec = {}
    for a in range(8):
        pp = ps_t[a % 2]
        if a >= 2:
            tens.wait_ge(ve.h, p1rec[a - 2])
        tm.inc(tens.transpose(pp[:, 0:128], w[2][:, a * 128:(a + 1) * 128],
                              ident[:, :]))
        tm.inc(tens.transpose(pp[:, 128:256], w[3][:, a * 128:(a + 1) * 128],
                              ident[:, :]))
        tm.wait(vect)
        ve.inc(vect.tensor_copy(wrtS[:, a * 128:(a + 1) * 128], pp[:, 0:128]))
        ve.inc(vect.tensor_copy(witS[:, a * 128:(a + 1) * 128],
                                pp[:, 128:256]))
        p1rec[a] = ve.n
    tm.inc(tens.matmul(ps_x[:, 0:1], ones_row[:, :], coff_s[0:1, 0:1],
                       start=True, stop=True))
    tm.wait(vect)
    ve.inc(vect.tensor_copy(vec[1][:, :], ps_x[:, 0:1]))
    nc.all_engine_barrier()
    tap("wrt0", wrtS[:, :]); tap("wit0", witS[:, :])
    nc.all_engine_barrier()

    gp.inc(gps.iota(gfj[:, :], pattern=[[1, 128]], base=0,
                    channel_multiplier=0, allow_small_or_imprecise_dtypes=True))
    gp.inc(gps.iota(mvs[:, :], pattern=[[128, 8]], base=0,
                    channel_multiplier=1, allow_small_or_imprecise_dtypes=True))
    iota_ms = gp.n
    gp.wait(vect)
    vect.wait_ge(ve.h, ve.n)
    ve.inc(vect.tensor_scalar(gfv[:, :], gfj[:, :], vec[1][:, :], None,
                              op0=ALU.add))
    vect.wait_ge(ve.h, ve.n)
    for mc in range(8):
        if mc > 0:
            vect.wait_ge(sc.h, sc.n)

        def _gstep(inst):
            ve.inc(inst)
            vect.wait_ge(ve.h, ve.n)
        _gstep(vect.tensor_scalar(gwk[:, :], gfv[:, :], mvs[:, mc:mc + 1],
                                  None, op0=ALU.mult))
        _gstep(vect.tensor_copy(gtk[:, :].bitcast(I32), gwk[:, :]))
        _gstep(vect.tensor_scalar(gtk[:, :].bitcast(I32),
                                  gtk[:, :].bitcast(I32), 1023, None,
                                  op0=ALU.bitwise_and))
        _gstep(vect.tensor_copy(gwk[:, :], gtk[:, :].bitcast(I32)))
        _gstep(vect.tensor_scalar(gtk[:, :], gwk[:, :], 511.5, None,
                                  op0=ALU.is_gt))
        _gstep(vect.scalar_tensor_tensor(gwk[:, :], gtk[:, :], -1024.0,
                                         gwk[:, :], op0=ALU.mult, op1=ALU.add))
        _gstep(vect.tensor_scalar(gtk[:, :], gwk[:, :], TPD_HI, None,
                                  op0=ALU.mult))
        ve.inc(vect.scalar_tensor_tensor(gwk[:, :], gwk[:, :], TPD_LO,
                                         gtk[:, :], op0=ALU.mult, op1=ALU.add))
        ve.wait(scal)
        sc.inc(scal.activation(sbt[mc][:, :], gwk[:, :], AF.Sin))
        sc.inc(scal.activation(gwk[:, :], gwk[:, :], AF.Abs))
        scal.wait_ge(sc.h, sc.n)
        sc.inc(scal.activation(cbt[mc][:, :], gwk[:, :], AF.Sin,
                               bias=vec[7][:, :], scale=-1.0))
    gen_sc = sc.n

    def stage12s(dst, rhs_a, rhs_b, sign_b):
        tens.wait_ge(ve.h, p1rec[7])
        for jh in range(2):
            pa = ps_sc[:, jh * 1024:jh * 1024 + 512]
            pb = ps_sc[:, jh * 1024 + 512:jh * 1024 + 1024]
            for oc in range(8):
                tens.matmul(pa, wrtS[:, oc * 128:(oc + 1) * 128],
                            rhs_a[oc][:, jh * 512:(jh + 1) * 512],
                            start=(oc == 0), stop=(oc == 7))
            for oc in range(8):
                st = tens.matmul(pb, witS[:, oc * 128:(oc + 1) * 128],
                                 rhs_b[oc][:, jh * 512:(jh + 1) * 512],
                                 start=(oc == 0), stop=(oc == 7))
            tm.inc(st)
            tm.wait(vect)
            ve.inc(vect.tensor_scalar(w[2][:, jh * 512:(jh + 1) * 512], pb,
                                      float(sign_b), None, op0=ALU.mult))
            vect.wait_ge(ve.h, ve.n)
            ve.inc(vect.tensor_tensor(dst[:, jh * 512:(jh + 1) * 512], pa,
                                      w[2][:, jh * 512:(jh + 1) * 512],
                                      op=ALU.add))
            tens.wait_ge(ve.h, ve.n)

    stage12s(tp_sh, c_sb, s_sb, -1)
    stage12s(up_sh, s_sb, c_sb, +1)
    ve.wait(gps)
    gps.wait_ge(sg.h, sg.n)
    gps.dma_start(out=tu_in.ap()[0:128, :], in_=tp_sh[:, :]).then_inc(sg.h, 16)
    gps.dma_start(out=tu_in.ap()[128:256, :], in_=up_sh[:, :]).then_inc(sg.h, 16)
    sg.n += 32
    gps.wait_ge(sg.h, sg.n)
    gps.wait_ge(gp.h, iota_ms)
    gp.inc(gps.collective_compute(
        "AllGather", ALU.bypass, replica_groups=[list(range(8))],
        ins=[tu_in.ap().opt()], outs=[tu_full.ap().opt()]))
    cc1 = gp.n
    nc.all_engine_barrier()
    tap("tp0", tp_sh[:, :]); tap("up0", up_sh[:, :])
    nc.all_engine_barrier()

    sync.wait_ge(ld.h, ld.n)
    sync.wait_ge(gp.h, cc1)
    gps.wait_ge(gp.h, cc1)
    ld_ms, sg_ms, mm_ms = {}, {}, {}

    def _tu_load(mc):
        eng, sem, ms = (sync, ld, ld_ms) if mc % 2 == 0 else (gps, sg, sg_ms)
        if mc >= 2:
            eng.wait_ge(tm.h, mm_ms[mc - 2])
        eng.dma_start(out=tTs[mc % 2][:, :],
                      in_=tu_full.ap()[mc * 256:mc * 256 + 128, :]
                      ).then_inc(sem.h, 16)
        eng.dma_start(out=tUs[mc % 2][:, :],
                      in_=tu_full.ap()[mc * 256 + 128:mc * 256 + 256, :]
                      ).then_inc(sem.h, 16)
        sem.n += 32
        ms[mc] = sem.n

    _tu_load(0)
    _tu_load(1)
    tens.wait_ge(sc.h, gen_sc)
    tens.wait_ge(ve.h, ve.n)
    for mc in range(8):
        sem, ms = (ld, ld_ms) if mc % 2 == 0 else (sg, sg_ms)
        tens.wait_ge(sem.h, ms[mc])
        for jh in range(2):
            tens.matmul(ps_t[jh][:, :], cbt[mc][:, :],
                        tTs[mc % 2][:, jh * 512:(jh + 1) * 512],
                        start=(mc == 0), stop=False)
            st = tens.matmul(ps_t[jh][:, :], sbt[mc][:, :],
                             tUs[mc % 2][:, jh * 512:(jh + 1) * 512],
                             start=False, stop=(mc == 7))
        tm.inc(st)
        mm_ms[mc] = tm.n
        if mc + 2 < 8:
            _tu_load(mc + 2)
    tm.wait(scal)
    for jh in range(2):
        sc.inc(scal.mul(msh[:, jh * 512:(jh + 1) * 512], ps_t[jh][:, :],
                        1.0 / 1024.0))
    sc.wait(gps)
    gps.wait_ge(sg.h, sg.n)
    gps.dma_start(out=m_in.ap()[:, :], in_=msh[:, :]).then_inc(sg.h, 16)
    sg.n += 16
    gps.wait_ge(sg.h, sg.n)
    gp.inc(gps.collective_compute(
        "AllGather", ALU.bypass, replica_groups=[list(range(8))],
        ins=[m_in.ap().opt()], outs=[m_full.ap().opt()]))
    cc2 = gp.n
    gps.wait_ge(gp.h, cc2)
    nc.all_engine_barrier()
    gp.inc(gps.memset(v_hi[:, :], 1.0))
    gp.inc(gps.memset(v_lo[:, :], 0.0))
    vall_ms = gp.n
    gps.wait_ge(gp.h, vall_ms)

    VS_EVEN = (0, 1, 2, 3, 4)
    VS_ODD = (5, 6, 9, 10, 11)

    def layernorm(src, dst, sq, vs=VS_EVEN, pre=None):
        rsum, negmean, sumsq, stdv, rstd = (vec[i] for i in vs)
        if pre is None:
            vect.wait_ge(sc.h, sc.n)
            vect.wait_ge(ve.h, ve.n)
            scal.wait_ge(ve.h, ve.n)
        else:
            for (h_, v_) in pre:
                vect.wait_ge(h_, v_)
                scal.wait_ge(h_, v_)
        ve.inc(vect.tensor_reduce(rsum[:, :], src, axis=AX.X, op=ALU.add))
        rsum_done = ve.n
        scal.wait_ge(ve.h, rsum_done)
        sc.inc(scal.mul(negmean[:, :], rsum[:, :], -1.0 / D))
        scal.wait_ge(sc.h, sc.n)
        sc.inc(scal.add(dst, src, negmean[:, :]))
        scal.wait_ge(sc.h, sc.n)
        sc.inc(scal.activation(sq, dst, AF.Square, accum_out=sumsq[:, :]))
        scal.wait_ge(sc.h, sc.n)
        sc.inc(scal.activation(stdv[:, :], sumsq[:, :], AF.Sqrt,
                               bias=vec[8][:, :], scale=1.0 / D))
        stdv_done = sc.n
        vect.wait_ge(sc.h, stdv_done)
        ve.inc(vect.reciprocal(rstd[:, :], stdv[:, :]))
        vect.wait_ge(ve.h, ve.n)
        ve.inc(vect.tensor_scalar(dst, dst, rstd[:, :], None, op0=ALU.mult))
        return ve.n

    for tt in range(2):
        sync.dma_start(out=w[tt % 2][:, :],
                       in_=x_hbm[tt * 128:(tt + 1) * 128, :]).then_inc(ld.h, 16)
        ld.n += 16
    lnA = [sb("lnA0", [128, D], F32, RC), sb("lnA1", [128, D], F32, RC + 4096)]
    ln_done = {}
    tr_done = {}
    ev_A = {}

    def a_stage2(tt):
        tens.wait_ge(ve.h, ln_done[tt])
        base = (tt % 2) * 1024
        if tt - 2 in ev_A:
            tens.wait_ge(ve.h, ev_A[tt - 2])
        for dc in range(NDC):
            tm.inc(tens.transpose(
                ps_sc[:, base + dc * 128:base + (dc + 1) * 128],
                lnA[tt % 2][:, dc * 128:(dc + 1) * 128], ident[:, :]))
        tm.wait(scal)
        sl = slice(tt * 128, (tt + 1) * 128)
        for dc in range(NDC):
            sc.inc(scal.copy(lnfm_hi[dc][:, sl],
                             ps_sc[:, base + dc * 128:base + (dc + 1) * 128]))
        sc.wait(vect)
        for dc in range(NDC):
            ve.inc(vect.tensor_tensor(
                lnfm_lo[dc][:, sl],
                ps_sc[:, base + dc * 128:base + (dc + 1) * 128],
                lnfm_hi[dc][:, sl], op=ALU.subtract))
        ev_A[tt] = ve.n
        tr_done[tt] = tm.n

    s1_done, s2_first, s2_done, s3_done, x_free = {}, {}, {}, {}, {}

    def ln_s1(t):
        rsum = vec[(VS_EVEN if t % 2 == 0 else VS_ODD)[0]]
        vect.wait_ge(ld.h, ld.n)
        if t - 2 in s2_first:
            vect.wait_ge(sc.h, s2_first[t - 2])
        ve.inc(vect.tensor_reduce(rsum[:, :], w[t % 3][:, :], axis=AX.X,
                                  op=ALU.add))
        s1_done[t] = ve.n

    def ln_s2(t):
        vs = VS_EVEN if t % 2 == 0 else VS_ODD
        rsum, negmean, sumsq, stdv = (vec[vs[i]] for i in range(4))
        scal.wait_ge(ve.h, s1_done[t])
        if t - 2 in s3_done:
            scal.wait_ge(ve.h, s3_done[t - 2])
        if t - 2 in tr_done:
            scal.wait_ge(tm.h, tr_done[t - 2])
        sc.inc(scal.mul(negmean[:, :], rsum[:, :], -1.0 / D))
        s2_first[t] = sc.n
        scal.wait_ge(sc.h, sc.n)
        sc.inc(scal.add(lnA[t % 2][:, :], w[t % 3][:, :], negmean[:, :]))
        x_free[t] = sc.n
        scal.wait_ge(sc.h, sc.n)
        sc.inc(scal.activation(w[3][:, :], lnA[t % 2][:, :], AF.Square,
                               accum_out=sumsq[:, :]))
        scal.wait_ge(sc.h, sc.n)
        sc.inc(scal.activation(stdv[:, :], sumsq[:, :], AF.Sqrt,
                               bias=vec[8][:, :], scale=1.0 / D))
        s2_done[t] = sc.n
        if t + 3 < NTT:
            sync.wait_ge(ld.h, ld.n)
            sync.wait_ge(sc.h, x_free[t])
            sync.wait_ge(ve.h, s1_done[t])
            sync.dma_start(out=w[t % 3][:, :],
                           in_=x_hbm[(t + 3) * 128:(t + 4) * 128, :]
                           ).then_inc(ld.h, 16)
            ld.n += 16

    def ln_s3(t):
        vs = VS_EVEN if t % 2 == 0 else VS_ODD
        stdv, rstd = vec[vs[3]], vec[vs[4]]
        vect.wait_ge(sc.h, s2_done[t])
        ve.inc(vect.reciprocal(rstd[:, :], stdv[:, :]))
        vect.wait_ge(ve.h, ve.n)
        ve.inc(vect.tensor_scalar(lnA[t % 2][:, :], lnA[t % 2][:, :],
                                  rstd[:, :], None, op0=ALU.mult))
        ln_done[t] = ve.n

    sync.dma_start(out=w[2][:, :],
                   in_=x_hbm[2 * 128:3 * 128, :]).then_inc(ld.h, 16)
    ld.n += 16
    for k in range(NTT + 2):
        if k - 2 >= 0:
            ln_s3(k - 2)
        if k < NTT:
            ln_s1(k)
        if k - 1 >= 0 and k - 1 < NTT:
            ln_s2(k - 1)
        if k - 2 >= 0:
            a_stage2(k - 2)
    nc.all_engine_barrier()
    tap("ln1fmh0", lnfm_hi[0][:, :])
    nc.all_engine_barrier()

    def load_pair_weights(p):
        sync.dma_start(out=w[0][:, :],
                       in_=wq_hbm[p * 128:(p + 1) * 128, :]).then_inc(ld.h, 16)
        sync.dma_start(out=w[1][:, :],
                       in_=wk_hbm[p * 128:(p + 1) * 128, :]).then_inc(ld.h, 16)
        sync.dma_start(out=w[2][:, :],
                       in_=wv_hbm[p * 128:(p + 1) * 128, :]).then_inc(ld.h, 16)
        sync.dma_start(
            out=w[3][:, :].rearrange("p (a c) -> p a c", a=8),
            in_=wo_hbm[:, p * 128:(p + 1) * 128].rearrange(
                "(a p) c -> p a c", p=128)).then_inc(ld.h, 16)
        ld.n += 64
        return ld.n

    SB4 = [ps_t0, ps_t1, ps_av, ps_x]
    AVS = {"n": 0, "meta": {}, "sctm": {}, "exp": {}, "msk": {}, "avtm": {},
           "spl": {}, "bfree": {}, "pend": [], "pend2": [], "eva": {},
           "ev": {}, "r": 0, "slotrec": {}, "sbslot": {}}

    def sb4_take():
        slot = AVS["r"] % 4
        AVS["r"] += 1
        rec = AVS["slotrec"].get(slot)
        if rec is not None:
            kind, val = rec
            if kind == "exp":
                tens.wait_ge(sc.h, val)
            else:
                tens.wait_ge(ve.h, val)
        return slot

    def av_scores(i):
        c, qg, hh = AVS["meta"][i]
        hsl = slice(hh * 64, (hh + 1) * 64)
        qsl = slice(qg * 512, (qg + 1) * 512)
        csl = slice(c * 128, (c + 1) * 128)
        slot = sb4_take()
        AVS["sbslot"][i] = slot
        sb_ = SB4[slot][:, 0:512]
        tens.matmul(sb_, k_hi[hsl, csl], q_hi[hsl, qsl], start=True,
                    stop=False)
        tens.matmul(sb_, k_hi[hsl, csl], q_lo[hsl, qsl], start=False,
                    stop=False)
        st = tens.matmul(sb_, k_lo[hsl, csl], q_hi[hsl, qsl], start=False,
                         stop=True)
        tm.inc(st)
        AVS["sctm"][i] = tm.n

    def av_exp(i):
        c, qg, hh = AVS["meta"][i]
        scal.wait_ge(tm.h, AVS["sctm"][i])
        if i - 3 in AVS["spl"]:
            scal.wait_ge(ve.h, AVS["spl"][i - 3])
        sc.inc(scal.activation(pt3[i % 3][:, 0:512],
                               SB4[AVS["sbslot"][i]][:, 0:512],
                               AF.Exp, bias=vec[10 + hh][:, :], scale=1.0))
        AVS["exp"][i] = sc.n
        AVS["slotrec"][AVS["sbslot"][i]] = ("exp", sc.n)
        if c // 4 == qg:
            gps.wait_ge(sc.h, AVS["exp"][i])
            gps.wait_ge(gp.h, gp.n)
            gp.inc(gps.affine_select(
                pt3[i % 3][:, 0:512], pt3[i % 3][:, 0:512],
                pattern=[[1, 512]], compare_op=ALU.is_ge, fill=0.0,
                base=-128 * (c - 4 * qg), channel_multiplier=-1))
            AVS["msk"][i] = gp.n

    def av_split(i):
        scal.wait_ge(sc.h, AVS["exp"][i])
        if i in AVS["msk"]:
            scal.wait_ge(gp.h, AVS["msk"][i])
        if i - 3 in AVS["avtm"]:
            scal.wait_ge(tm.h, AVS["avtm"][i - 3])
        sc.inc(scal.copy(pth[i % 3][:, 0:512], pt3[i % 3][:, 0:512]))
        hi_sc = sc.n
        vect.wait_ge(sc.h, hi_sc)
        if i - 3 in AVS["avtm"]:
            vect.wait_ge(tm.h, AVS["avtm"][i - 3])
        ve.inc(vect.tensor_tensor(ptl[i % 3][:, 0:512], pt3[i % 3][:, 0:512],
                                  pth[i % 3][:, 0:512], op=ALU.subtract))
        AVS["spl"][i] = ve.n

    def av_av(i):
        c, qg, hh = AVS["meta"][i]
        tens.wait_ge(ve.h, AVS["spl"][i])
        if c == 0 and qg in AVS["bfree"]:
            bf_sc, bf_ve = AVS["bfree"][qg]
            tens.wait_ge(sc.h, bf_sc)
            tens.wait_ge(ve.h, bf_ve)
        if hh == 0:
            vh = v_hi[:, c * 196:c * 196 + 65]
            vl = v_lo[:, c * 196:c * 196 + 65]
            np_out = 65
        else:
            vh = v_hi[:, c * 196 + 66:c * 196 + 194]
            vl = v_lo[:, c * 196 + 66:c * 196 + 194]
            np_out = 128
        bank = ps_sc[0:np_out, qg * 512:(qg + 1) * 512]
        pth_, ptl_ = pth[i % 3][:, 0:512], ptl[i % 3][:, 0:512]
        tens.matmul(bank, vh, pth_, start=(c == 0), stop=False)
        tens.matmul(bank, vh, ptl_, start=False, stop=False)
        st = tens.matmul(bank, vl, pth_, start=False, stop=(c == 4 * qg + 3))
        tm.inc(st)
        AVS["avtm"][i] = tm.n
        if c == 4 * qg + 3:
            AVS["pend"].append((i, AVS["n"]))

    def av_evac_a(i):
        c, qg, hh = AVS["meta"][i]
        den_p = 64 if hh == 0 else 32
        fsl = slice(hh * 64, (hh + 1) * 64)
        prev = AVS["ev"]
        qpsl = slice(qg * 512, (qg + 1) * 512)
        scal.wait_ge(tm.h, AVS["avtm"][i])
        if "mult" in prev:
            scal.wait_ge(ve.h, prev["mult"])
        if "rhl" in prev:
            scal.wait_ge(ve.h, prev["rhl"])
        sc.inc(scal.copy(slotA[fsl, 0:512], ps_sc[fsl, qpsl]))
        copy_sc = sc.n
        vect.wait_ge(tm.h, AVS["avtm"][i])
        if "bcast" in prev:
            vect.wait_ge(tm.h, prev["bcast"])
        if "mult" in prev:
            vect.wait_ge(ve.h, prev["mult"])
        ve.inc(vect.reciprocal(slotA[den_p:den_p + 1, 0:512],
                               ps_sc[den_p:den_p + 1, qpsl]))
        vect.wait_ge(ve.h, ve.n)
        ve.inc(vect.tensor_copy(rh[den_p:den_p + 1, 0:512],
                                slotA[den_p:den_p + 1, 0:512]))
        vect.wait_ge(ve.h, ve.n)
        ve.inc(vect.tensor_tensor(rl[den_p:den_p + 1, 0:512],
                                  slotA[den_p:den_p + 1, 0:512],
                                  rh[den_p:den_p + 1, 0:512],
                                  op=ALU.subtract))
        rhl_ve = ve.n
        AVS["bfree"][qg] = (copy_sc, rhl_ve)
        AVS["eva"][i] = (rhl_ve, copy_sc)
        prev["rhl"] = rhl_ve

    def av_evac_b(i):
        c, qg, hh = AVS["meta"][i]
        den_p = 64 if hh == 0 else 32
        fsl = slice(hh * 64, (hh + 1) * 64)
        prev = AVS["ev"]
        qpsl = slice(qg * 512, (qg + 1) * 512)
        rhl_ve, copy_sc = AVS["eva"][i]
        slot = sb4_take()
        tens.wait_ge(ve.h, rhl_ve)
        tens.matmul(SB4[slot][:, 0:512], ones_pbf[den_p:den_p + 1, 0:128],
                    rh[den_p:den_p + 1, 0:512], start=True, stop=False)
        st = tens.matmul(SB4[slot][:, 0:512],
                         ones_pbf[den_p:den_p + 1, 0:128],
                         rl[den_p:den_p + 1, 0:512], start=False, stop=True)
        tm.inc(st)
        bcast_tm = tm.n
        vect.wait_ge(tm.h, bcast_tm)
        vect.wait_ge(sc.h, copy_sc)
        if "hi" in prev:
            vect.wait_ge(sc.h, prev["hi"])
        if "lo" in prev:
            vect.wait_ge(ve.h, prev["lo"])
        ve.inc(vect.tensor_tensor(hof32[fsl, 0:512], slotA[fsl, 0:512],
                                  SB4[slot][fsl, 0:512], op=ALU.mult))
        mult_ve = ve.n
        AVS["slotrec"][slot] = ("mult", mult_ve)
        scal.wait_ge(ve.h, mult_ve)
        sc.inc(scal.copy(hofm_hi[fsl, qpsl], hof32[fsl, 0:512]))
        hi_sc = sc.n
        vect.wait_ge(sc.h, hi_sc)
        ve.inc(vect.tensor_tensor(hofm_lo[fsl, qpsl], hof32[fsl, 0:512],
                                  hofm_hi[fsl, qpsl], op=ALU.subtract))
        prev.update({"bcast": bcast_tm, "mult": mult_ve,
                     "hi": hi_sc, "lo": ve.n})

    def av_drain(force=False):
        if force:
            while AVS["pend2"]:
                i, _ = AVS["pend2"].pop(0)
                av_evac_b(i)
            while AVS["pend"]:
                i, _ = AVS["pend"].pop(0)
                av_evac_a(i)
                av_evac_b(i)
            return
        while AVS["pend"] and AVS["pend"][0][1] <= AVS["n"] - 1:
            i, _ = AVS["pend"].pop(0)
            av_evac_a(i)
            AVS["pend2"].append((i, AVS["n"]))
        while AVS["pend2"] and AVS["pend2"][0][1] <= AVS["n"] - 2:
            i, _ = AVS["pend2"].pop(0)
            av_evac_b(i)

    wrec = load_pair_weights(0)
    accrec = {}
    for p in range(NPAIR):
        tens.wait_ge(ld.h, wrec)

        def transpose_split(nat, dst_hi, dst_lo):
            tens.wait_ge(ve.h, ve.n)
            tens.wait_ge(sc.h, sc.n)
            for half in range(2):
                pp = ps_t[half]
                for a4 in range(4):
                    a = half * 4 + a4
                    tm.inc(tens.transpose(pp[:, a4 * 128:(a4 + 1) * 128],
                                          nat[:, a * 128:(a + 1) * 128],
                                          ident[:, :]))
                tm.wait(scal)
                sl = slice(half * 512, (half + 1) * 512)
                sc.inc(scal.copy(dst_hi[:, sl], pp[:, 0:512]))
                sc.wait(vect)
                ve.inc(vect.tensor_tensor(dst_lo[:, sl], pp[:, 0:512],
                                          dst_hi[:, sl], op=ALU.subtract))

        evq = {}
        for side, nat in enumerate((w[0], w[1])):
            transpose_split(nat, wt_hi, wt_lo)
            ve.wait(tens)
            dst_hi, dst_lo = (q_hi, q_lo) if side == 0 else (k_hi, k_lo)
            for tc in range(4):
                g = side * 4 + tc
                pp = ps_sc[:, (g % 2) * 512:(g % 2) * 512 + 512]
                tsl = slice(tc * 512, (tc + 1) * 512)
                if g >= 2:
                    tens.wait_ge(ve.h, evq[g - 2])
                for dc in range(NDC):
                    dsl = slice(dc * 128, (dc + 1) * 128)
                    tens.matmul(pp, wt_hi[:, dsl], lnfm_hi[dc][:, tsl],
                                start=(dc == 0), stop=False)
                    tens.matmul(pp, wt_hi[:, dsl], lnfm_lo[dc][:, tsl],
                                start=False, stop=False)
                    st = tens.matmul(pp, wt_lo[:, dsl], lnfm_hi[dc][:, tsl],
                                     start=False, stop=(dc == NDC - 1))
                tm.inc(st)
                tm.wait(scal)
                if side == 0:
                    sc.inc(scal.mul(dst_hi[:, tsl], pp, 0.125))
                    sc.wait(vect)
                    ve.inc(vect.scalar_tensor_tensor(
                        dst_lo[:, tsl], pp, 0.125, dst_hi[:, tsl],
                        op0=ALU.mult, op1=ALU.subtract))
                else:
                    sc.inc(scal.copy(dst_hi[:, tsl], pp))
                    sc.wait(vect)
                    ve.inc(vect.tensor_tensor(dst_lo[:, tsl], pp,
                                              dst_hi[:, tsl], op=ALU.subtract))
                evq[g] = ve.n

        transpose_split(w[2], wt_hi, wt_lo)
        ve.wait(tens)
        evv = {}
        evs = {}
        for tt in range(NTT):
            pp = ps_t[tt % 2]
            if tt >= 2:
                tens.wait_ge(ve.h, evv[tt - 2])
                tens.wait_ge(sc.h, evs[tt - 2])
            for dc in range(NDC):
                dsl = slice(dc * 128, (dc + 1) * 128)
                tsl = slice(tt * 128, (tt + 1) * 128)
                tens.matmul(pp[:, 0:128], lnfm_hi[dc][:, tsl], wt_hi[:, dsl],
                            start=(dc == 0), stop=False)
                tens.matmul(pp[:, 0:128], lnfm_hi[dc][:, tsl], wt_lo[:, dsl],
                            start=False, stop=False)
                st = tens.matmul(pp[:, 0:128], lnfm_lo[dc][:, tsl],
                                 wt_hi[:, dsl], start=False,
                                 stop=(dc == NDC - 1))
            tm.inc(st)
            tm.wait(scal)
            tm.wait(vect)
            if p == 0 and tt == 0:
                scal.wait_ge(gp.h, vall_ms)
                vect.wait_ge(gp.h, vall_ms)
            sc.inc(scal.copy(v_hi[:, tt * 196:tt * 196 + 64], pp[:, 0:64]))
            sc.inc(scal.copy(v_hi[:, tt * 196 + 130:tt * 196 + 194],
                             pp[:, 64:128]))
            evs[tt] = sc.n
            sc.wait(vect)
            ve.inc(vect.tensor_tensor(v_lo[:, tt * 196:tt * 196 + 64],
                                      pp[:, 0:64],
                                      v_hi[:, tt * 196:tt * 196 + 64],
                                      op=ALU.subtract))
            ve.inc(vect.tensor_tensor(v_lo[:, tt * 196 + 130:tt * 196 + 194],
                                      pp[:, 64:128],
                                      v_hi[:, tt * 196 + 130:tt * 196 + 194],
                                      op=ALU.subtract))
            evv[tt] = ve.n

        if p == 0:
            scal.wait_ge(sg.h, sg.n)
            vect.wait_ge(sg.h, sg.n)
        transpose_split(w[3], wot_hi, wot_lo)
        if p + 1 < NPAIR:
            sync.wait_ge(tm.h, tm.n)
            wrec = load_pair_weights(p + 1)

        vect.wait_ge(sc.h, sc.n)
        for side, srcb in enumerate((q_hi, k_hi)):
            scal.wait_ge(tm.h, tm.n)
            scal.wait_ge(ve.h, ve.n)
            sc.inc(scal.activation(sqscr[:, :], srcb[:, :], AF.Square))
            tens.wait_ge(sc.h, sc.n)
            tens.wait_ge(ve.h, ve.n)
            last = None
            for g in range(4):
                last = tens.matmul(ps_sc[0:2, g * 512:(g + 1) * 512],
                                   ones2[:, :], sqscr[:, g * 512:(g + 1) * 512],
                                   start=True, stop=True)
            tm.inc(last)
            tm.wait(vect)
            ve.inc(vect.tensor_reduce(bmx[0:2, side:side + 1], ps_sc[0:2, :],
                                      axis=AX.X, op=ALU.max))
            vect.wait_ge(ve.h, ve.n)
        ve.inc(vect.tensor_tensor(bmx[0:2, 0:1], bmx[0:2, 0:1], bmx[0:2, 1:2],
                                  op=ALU.mult))
        ve.wait(scal)
        sc.inc(scal.activation(bmx[0:2, 1:2], bmx[0:2, 0:1], AF.Sqrt))
        sc.wait(tens)
        tm.inc(tens.transpose(ps_t1[0:1, 0:2], bmx[0:2, 1:2], ident[0:2, 0:2]))
        tm.wait(vect)
        ve.inc(vect.tensor_copy(bmx[0:1, 0:2], ps_t1[0:1, 0:2]))
        ve.wait(tens)
        for hh2 in range(2):
            tm.inc(tens.matmul(ps_t1[:, 0:1], neg1_row[:, :],
                               bmx[0:1, hh2:hh2 + 1], start=True, stop=True))
            tm.wait(vect)
            ve.inc(vect.tensor_copy(vec[10 + hh2][:, :], ps_t1[:, 0:1]))
            tens.wait_ge(ve.h, ve.n)
        nc.all_engine_barrier()
        if debug and p == 0:
            tap("qh0", q_hi[:, :]); tap("ql0", q_lo[:, :])
            tap("kh0", k_hi[:, :]); tap("v00", v_hi[:, :])
            tap("bv0", vec[10][:, :])
            nc.all_engine_barrier()

        pb = AVS["n"]
        for hh in range(2):
            for c in range(16):
                for qg in range(c // 4, 4):
                    i = AVS["n"]
                    AVS["n"] += 1
                    AVS["meta"][i] = (c, qg, hh)
                    if AV_BATCH:
                        if i % 2 == 1:
                            for k in (i - 7, i - 6):
                                if k >= pb:
                                    av_av(k)
                            for k in (i - 5, i - 4):
                                if k >= pb:
                                    av_split(k)
                            for k in (i - 3, i - 2):
                                if k >= pb:
                                    av_exp(k)
                            av_scores(i - 1)
                            av_scores(i)
                            av_drain()
                    else:
                        av_scores(i)
                        if i - 3 >= pb:
                            av_exp(i - 3)
                        if i - 7 >= pb:
                            av_av(i - 7)
                        if i - 5 >= pb:
                            av_split(i - 5)
                        av_drain()
        last = AVS["n"] - 1
        if AV_BATCH:
            av_av(last - 5)
            av_av(last - 4)
            av_split(last - 3)
            av_split(last - 2)
            av_exp(last - 1)
            av_exp(last)
            av_av(last - 3)
            av_av(last - 2)
            av_split(last - 1)
            av_split(last)
            av_av(last - 1)
            av_av(last)
        else:
            av_av(last - 6)
            av_av(last - 5)
            av_split(last - 4)
            av_split(last - 3)
            av_exp(last - 2)
            av_exp(last - 1)
            av_exp(last)
            av_av(last - 4)
            av_av(last - 3)
            av_split(last - 2)
            av_split(last - 1)
            av_av(last - 2)
            av_split(last)
            av_av(last - 1)
            av_av(last)
        av_drain(force=True)
        nc.all_engine_barrier()
        if debug and p == 0:
            tap("hofmh0", hofm_hi[:, :]); tap("hofml0", hofm_lo[:, :])
            nc.all_engine_barrier()

        evrec = {}
        for tt in range(NTT):
            for oc in range(2):
                g = tt * 2 + oc
                pp = (ps_av, ps_x)[oc]
                tsl = slice(tt * 128, (tt + 1) * 128)
                osl = slice(oc * 512, (oc + 1) * 512)
                if g >= 2:
                    tens.wait_ge(ve.h, evrec[g - 2])
                tens.matmul(pp[:, :], hofm_hi[:, tsl], wot_hi[:, osl],
                            start=True, stop=False)
                tens.matmul(pp[:, :], hofm_hi[:, tsl], wot_lo[:, osl],
                            start=False, stop=False)
                st = tens.matmul(pp[:, :], hofm_lo[:, tsl], wot_hi[:, osl],
                                 start=False, stop=True)
                tm.inc(st)
                tm.wait(vect)
                if p > 0:
                    vect.wait_ge(ve.h, accrec[g])
                dst = acc_tm[tt][:, oc * 512:(oc + 1) * 512]
                if p == 0:
                    ve.inc(vect.tensor_copy(dst, pp[:, :]))
                else:
                    ve.inc(vect.tensor_tensor(dst, dst, pp[:, :], op=ALU.add))
                evrec[g] = ve.n
                accrec[g] = ve.n
        nc.all_engine_barrier()

    tap("acc0", acc_tm[0][:, :]); tap("acc15", acc_tm[15][:, :])
    nc.all_engine_barrier()
    dD = [sb("dD0", [128, D], F32, RC + 32768), sb("dD1", [128, D], F32, RC + 36864)]
    dE = [sb("dE0", [128, D], F32, RC + 40960), w[2]]
    lnE_hi = [sb("lnEh0", [128, 1024], BF16, RC + 45056),
              sb("lnEh1", [128, 1024], BF16, RC + 49152)]
    lnE_lo = [sb("lnEl0", [128, 1024], BF16, RC + 47104),
              sb("lnEl1", [128, 1024], BF16, RC + 51200)]
    obuf = [sb(f"obuf{i}", [128, D], F32, RC + 45056 + 4096 * i) for i in range(2)]
    mstage = dD[0]

    sync.wait_ge(sg.h, sg.n)
    sync.wait_ge(gp.h, cc2)
    sync.wait_ge(ld.h, ld.n)
    gps.wait_ge(sg.h, sg.n)
    gps.dma_start(out=bias_bc[0:1, :], in_=bias_hbm[:, :]).then_inc(sg.h, 16)
    sg.n += 16
    bias_ld = sg.n
    mstages = [dD[0], dD[1]]
    gps.wait_ge(sg.h, bias_ld)
    m_ms, spl_rec = {}, {}

    def _m_split(dc):
        eng_sem = ld if dc % 2 == 0 else sg
        scal.wait_ge(eng_sem.h, m_ms[dc])
        sc.inc(scal.copy(m_hi[dc][:, :], mstages[dc % 2][:, :]))
        sc.wait(vect)
        ve.inc(vect.tensor_tensor(m_lo[dc][:, :], mstages[dc % 2][:, :],
                                  m_hi[dc][:, :], op=ALU.subtract))
        spl_rec[dc] = ve.n

    for dc in range(NDC):
        eng, sem = (sync, ld) if dc % 2 == 0 else (gps, sg)
        if dc >= 2:
            eng.wait_ge(ve.h, spl_rec[dc - 2])
        eng.dma_start(out=mstages[dc % 2][:, :],
                      in_=m_full.ap()[dc * 128:(dc + 1) * 128, :]
                      ).then_inc(sem.h, 16)
        sem.n += 16
        m_ms[dc] = sem.n
        if dc >= 1:
            _m_split(dc - 1)
    _m_split(NDC - 1)
    tens.wait_ge(sg.h, bias_ld)
    for oc in range(2):
        tm.inc(tens.matmul(ps_x[:, :], ones_row[:, :],
                           bias_bc[0:1, oc * 512:(oc + 1) * 512],
                           start=True, stop=True))
        tm.wait(vect)
        ve.inc(vect.tensor_copy(bias_bc[:, oc * 512:(oc + 1) * 512], ps_x[:, :]))
        tens.wait_ge(ve.h, ve.n)
    sync.wait_ge(ld.h, ld.n)
    for tt in range(2):
        sync.dma_start(out=w[tt % 2][:, :],
                       in_=x_hbm[tt * 128:(tt + 1) * 128, :]).then_inc(ld.h, 16)
        ld.n += 16

    x1_done = {}
    lnE_done = {}
    out_dma = {}
    fmm_done = {}
    fy_ve = {}
    ev_done = {}
    ob_done = {}

    def def_s12(tt):
        r = tt % 2
        pre = [(ld.h, ld.n)]
        if tt >= 2:
            pre.append((tm.h, fmm_done.get(tt - 2, 0)))
        layernorm(acc_tm[tt][:, :], dD[r][:, :], w[3][:, :],
                  VS_EVEN if r == 0 else VS_ODD, pre=pre)
        vect.wait_ge(ve.h, ve.n)
        ve.inc(vect.tensor_tensor(x1_tm[tt][:, :], dD[r][:, :],
                                  w[tt % 2][:, :], op=ALU.add))
        x1_done[tt] = ve.n
        if tt + 2 < NTT:
            sync.wait_ge(ve.h, x1_done[tt])
            sync.dma_start(out=w[tt % 2][:, :],
                           in_=x_hbm[(tt + 2) * 128:(tt + 3) * 128, :]
                           ).then_inc(ld.h, 16)
            ld.n += 16
        layernorm(x1_tm[tt][:, :], dE[r][:, :], w[3][:, :],
                  VS_EVEN if r == 0 else VS_ODD,
                  pre=[(ve.h, x1_done[tt])])
        lnE_done[tt] = ve.n

    def def_s3a(tt):
        r = tt % 2
        tens.wait_ge(ve.h, lnE_done[tt])
        if tt >= 2:
            scal.wait_ge(sg.h, out_dma[tt - 2])
            vect.wait_ge(sg.h, out_dma[tt - 2])
            tens.wait_ge(ve.h, ob_done[tt - 2])
        for half in range(2):
            pp = ps_t[half]
            if (tt - 1, half) in ev_done:
                tens.wait_ge(ve.h, ev_done[(tt - 1, half)])
            for dc4 in range(4):
                dc = half * 4 + dc4
                tm.inc(tens.transpose(pp[:, dc4 * 128:(dc4 + 1) * 128],
                                      dE[r][:, dc * 128:(dc + 1) * 128],
                                      ident[:, :]))
            tm.wait(scal)
            hsl = slice(half * 512, (half + 1) * 512)
            sc.inc(scal.copy(lnE_hi[r][:, hsl], pp[:, 0:512]))
            sc.wait(vect)
            ve.inc(vect.tensor_tensor(lnE_lo[r][:, hsl], pp[:, 0:512],
                                      lnE_hi[r][:, hsl], op=ALU.subtract))
            ev_done[(tt, half)] = ve.n

    def def_s3b(tt):
        r = tt % 2
        fy_t = dD[r]
        for oc in range(2):
            pp = ps_sc[:, (oc + 2 * (tt % 2)) * 512:
                       (oc + 2 * (tt % 2)) * 512 + 512]
            osl = slice(oc * 512, (oc + 1) * 512)
            tens.wait_ge(ve.h, ev_done[(tt, 1)])
            for dc in range(NDC):
                dsl = slice(dc * 128, (dc + 1) * 128)
                tens.matmul(pp, lnE_hi[r][:, dsl], m_hi[dc][:, osl],
                            start=(dc == 0), stop=False)
                tens.matmul(pp, lnE_hi[r][:, dsl], m_lo[dc][:, osl],
                            start=False, stop=False)
                st = tens.matmul(pp, lnE_lo[r][:, dsl], m_hi[dc][:, osl],
                                 start=False, stop=(dc == 7))
            tm.inc(st)
            tm.wait(vect)
            ve.inc(vect.tensor_tensor(fy_t[:, osl], pp,
                                      bias_bc[:, osl], op=ALU.add))
        fmm_done[tt] = tm.n
        fy_ve[tt] = ve.n

    def def_s3b2(tt):
        r = tt % 2
        fy_t = dD[r]
        layernorm(fy_t[:, :], dE[r][:, :], w[3][:, :],
                  VS_EVEN if r == 0 else VS_ODD, pre=[(ve.h, fy_ve[tt])])
        ob = obuf[r]
        if tt >= 2:
            vect.wait_ge(sg.h, sg.n)
        vect.wait_ge(ve.h, ve.n)
        ve.inc(vect.tensor_tensor(ob[:, :], dE[r][:, :], x1_tm[tt][:, :],
                                  op=ALU.add))
        ob_done[tt] = ve.n
        ve.wait(gps)
        gps.wait_ge(sg.h, sg.n)
        gps.dma_start(out=out_hbm[tt * 128:(tt + 1) * 128, :],
                      in_=ob[:, :]).then_inc(sg.h, 16)
        sg.n += 16
        out_dma[tt] = sg.n

    for tt in range(NTT):
        if tt >= 1:
            def_s3a(tt - 1)
        def_s12(tt)
        if tt >= 1:
            def_s3b(tt - 1)
            def_s3b2(tt - 1)
    def_s3a(NTT - 1)
    def_s3b(NTT - 1)
    def_s3b2(NTT - 1)
    gps.wait_ge(sg.h, sg.n)
    return nc


_NC_CACHE = None


def kernel(L0_attn_x, L0_attn_wq, L0_attn_wk, L0_attn_wv, L0_attn_wo,
           L0_ffn_w_freq_real, L0_ffn_w_freq_imag, L0_ffn_bias):
    global _NC_CACHE
    x = np.ascontiguousarray(np.asarray(L0_attn_x, dtype=np.float32))
    wq = np.ascontiguousarray(np.asarray(L0_attn_wq, np.float32).reshape(H * DH, D))
    wk = np.ascontiguousarray(np.asarray(L0_attn_wk, np.float32).reshape(H * DH, D))
    wv = np.ascontiguousarray(np.asarray(L0_attn_wv, np.float32).reshape(H * DH, D))
    wo = np.ascontiguousarray(np.asarray(L0_attn_wo, np.float32))
    wfr = np.ascontiguousarray(np.asarray(L0_ffn_w_freq_real, np.float32))
    wfi = np.ascontiguousarray(np.asarray(L0_ffn_w_freq_imag, np.float32))
    bias = np.ascontiguousarray(np.asarray(L0_ffn_bias, np.float32).reshape(1, D))
    if _NC_CACHE is None:
        _NC_CACHE = build_nc()
    nc = _NC_CACHE
    in_maps = [{"x": np.ascontiguousarray(x[b]), "wq": wq, "wk": wk, "wv": wv,
                "wo": wo,
                "wfr": np.ascontiguousarray(wfr[b * 128:(b + 1) * 128]),
                "wfi": np.ascontiguousarray(wfi[b * 128:(b + 1) * 128]),
                "coff": np.full((1, 1), b * 128, np.float32),
                "bias": bias}
               for b in range(8)]
    res = run_bass_kernel_spmd(nc, in_maps, core_ids=list(range(8)))
    return np.stack([res.results[b]["out"] for b in range(8)], axis=0)


# version 26
# speedup vs baseline: 1.0754x; 1.0095x over previous
import numpy as np
import concourse.bass as bass
import concourse.mybir as mybir
from concourse.bass_utils import run_bass_kernel_spmd

F32 = mybir.dt.float32
I32 = mybir.dt.int32
AF = mybir.ActivationFunctionType
ALU = mybir.AluOpType
AX = mybir.AxisListType

D = 1024
T = 2048
H = 16
DH = 64
NPAIR = 8
NTT = T // 128
NDC = D // 128
VST = 80
EPS = 1e-5
NEG = -1.0e9
AV_BATCH = True

TWO_PI_OVER_D = 2.0 * np.pi / 1024.0
TPD_HI = float(np.float32(TWO_PI_OVER_D))
TPD_LO = float(np.float32(TWO_PI_OVER_D - TPD_HI))
HALF_PI = float(np.float32(np.pi / 2))


class Sem:

    def __init__(self, nc, name):
        self.h = nc.alloc_semaphore(name)
        self.n = 0

    def inc(self, inst, k=1):
        inst.then_inc(self.h, k)
        self.n += k
        return self.n

    def wait(self, eng, n=None):
        eng.wait_ge(self.h, self.n if n is None else n)


def build_nc(debug=False):
    nc = bass.Bass("TRN2", target_bir_lowering=False)

    x_hbm = nc.dram_tensor("x", [T, D], F32, kind="ExternalInput").ap()
    wq_hbm = nc.dram_tensor("wq", [H * DH, D], F32, kind="ExternalInput").ap()
    wk_hbm = nc.dram_tensor("wk", [H * DH, D], F32, kind="ExternalInput").ap()
    wv_hbm = nc.dram_tensor("wv", [H * DH, D], F32, kind="ExternalInput").ap()
    wo_hbm = nc.dram_tensor("wo", [D, D], F32, kind="ExternalInput").ap()
    wfr_hbm = nc.dram_tensor("wfr", [128, D], F32, kind="ExternalInput").ap()
    wfi_hbm = nc.dram_tensor("wfi", [128, D], F32, kind="ExternalInput").ap()
    coff_hbm = nc.dram_tensor("coff", [1, 1], F32, kind="ExternalInput").ap()
    bias_hbm = nc.dram_tensor("bias", [1, D], F32, kind="ExternalInput").ap()
    out_hbm = nc.dram_tensor("out", [T, D], F32, kind="ExternalOutput").ap()
    tu_in = nc.dram_tensor("tu_in", [256, D], F32, kind="Internal")
    tu_full = nc.dram_tensor("tu_full", [2048, D], F32, kind="Internal",
                             addr_space="Shared")
    m_in = nc.dram_tensor("m_in", [128, D], F32, kind="Internal")
    m_full = nc.dram_tensor("m_full", [D, D], F32, kind="Internal",
                            addr_space="Shared")

    BASE = 17408

    def sb(name, shape, dtype, off):
        return nc.alloc_sbuf_tensor_at(name, shape, dtype, offset=off).ap()

    ident = sb("ident", [128, 128], F32, BASE)
    maskt = sb("maskt", [128, 128], F32, BASE + 512)
    BF16 = mybir.dt.bfloat16
    VEC0 = BASE + 1024
    vec = [sb(f"vec{i}", [128, 1], F32, VEC0 + 32 * i) for i in range(16)]
    ones_row = sb("ones_row", [1, 128], F32, VEC0 + 512)
    neg1_row = sb("neg1_row", [1, 128], F32, VEC0 + 512 + 512)
    neg64_row = sb("neg64_row", [1, 128], F32, VEC0 + 2112)
    bmx = sb("bmx", [128, 2], F32, VEC0 + 2624)
    coff_s = sb("coff_s", [1, 1], F32, VEC0 + 2656)
    ones2 = sb("ones2", [128, 33], F32, VEC0 + 2688)
    ones_col = sb("ones_col", [128, 1], F32, VEC0 + 1536)
    ident_bf = sb("ident_bf", [128, 128], BF16, VEC0 + 1600)
    maskt_bf = sb("maskt_bf", [128, 128], BF16, VEC0 + 1856)
    bias_bc = sb("bias_bc", [128, D], F32, BASE + 4096)
    wot_hi = sb("wot_hi", [128, D], BF16, BASE + 4096)
    wot_lo = sb("wot_lo", [128, D], BF16, BASE + 4096 + 2048)

    W0 = BASE + 8192
    w = [sb(f"wtmp{i}", [128, D], F32, W0 + 4096 * i) for i in range(4)]

    RA = W0 + 4 * 4096
    RB = RA + 65536
    RC = RB + 65536
    assert RC + 55808 <= 229376, (RC + 55808)

    c_sb = [sb(f"cdft{i}", [128, D], F32, RA + 4096 * i) for i in range(8)]
    s_sb = [sb(f"sdft{i}", [128, D], F32, RA + 32768 + 4096 * i) for i in range(8)]
    lnfm_hi = [sb(f"lnfmh{i}", [128, T], BF16, RA + 8192 * i) for i in range(8)]
    lnfm_lo = [sb(f"lnfml{i}", [128, T], BF16, RA + 8192 * i + 4096) for i in range(8)]
    x1_tm = [sb(f"x1tm{i}", [128, D], F32, RA + 4096 * i) for i in range(16)]

    wrtS = sb("wrtS", [128, D], F32, RB)
    witS = sb("witS", [128, D], F32, RB + 4096)
    acc_tm = [sb(f"acctm{i}", [128, D], F32, RB + 4096 * i) for i in range(16)]
    ln2fm_hi = [sb(f"ln2fmh{i}", [128, T], BF16, RB + 8192 * i) for i in range(8)]
    ln2fm_lo = [sb(f"ln2fml{i}", [128, T], BF16, RB + 8192 * i + 4096)
                for i in range(8)]

    tp_sh = sb("tp_sh", [128, D], F32, RC)
    up_sh = sb("up_sh", [128, D], F32, RC + 4096)
    cbt = [sb(f"cbt{i}", [128, 128], F32, RC + 8192 + 512 * i) for i in range(8)]
    sbt = [sb(f"sbt{i}", [128, 128], F32, RC + 12288 + 512 * i) for i in range(8)]
    tTs = [sb(f"tTs{i}", [128, D], F32, RC + 16384 + 4096 * i) for i in range(2)]
    tUs = [sb(f"tUs{i}", [128, D], F32, RC + 24576 + 4096 * i) for i in range(2)]
    msh = sb("msh", [128, D], F32, RC + 32768)
    gfj = sb("gfj", [128, 128], F32, RC + 36864)
    gfv = sb("gfv", [128, 128], F32, RC + 37376)
    gwk = sb("gwk", [128, 128], F32, RC + 37888)
    gtk = sb("gtk", [128, 128], F32, RC + 38400)
    mvs = sb("mvs", [128, 8], F32, RC + 38912)
    q_hi = sb("q_hi", [128, T], BF16, RC)
    q_lo = sb("q_lo", [128, T], BF16, RC + 4096)
    k_hi = sb("k_hi", [128, T], BF16, RC + 8192)
    k_lo = sb("k_lo", [128, T], BF16, RC + 12288)
    v_hi = sb("v_hi", [128, 16 * 196], BF16, RC + 16384)
    v_lo = sb("v_lo", [128, 16 * 196], BF16, RC + 22656)
    slotA = sb("slotA", [128, 512], F32, RC + 28928)
    hof32 = sb("hof32", [128, 512], F32, RC + 30976)
    pt3 = [sb(f"pt3_{i}", [128, 512], F32, RC + 33024 + 2048 * i)
           for i in range(3)]
    pth = [sb(f"pth{i}", [128, 512], BF16, RC + 39168 + 1024 * i)
           for i in range(3)]
    ptl = [sb(f"ptl{i}", [128, 512], BF16, RC + 42240 + 1024 * i)
           for i in range(3)]
    hofm_hi = sb("hofm_hi", [128, T], BF16, RC + 45312)
    hofm_lo = sb("hofm_lo", [128, T], BF16, RC + 49408)
    sqscr = sb("sqscr", [128, T], F32, RC + 45312)
    wt_hi = sb("wt_hi", [128, D], BF16, RC + 33024)
    wt_lo = sb("wt_lo", [128, D], BF16, RC + 35072)
    ones_pbf = sb("ones_pbf", [128, 128], BF16, RC + 53504)
    rh = sb("rh", [128, 512], BF16, RC + 53760)
    rl = sb("rl", [128, 512], BF16, RC + 54784)
    m_hi = [sb(f"mhi{i}", [128, D], BF16, RC + 2048 * i) for i in range(8)]
    m_lo = [sb(f"mlo{i}", [128, D], BF16, RC + 16384 + 2048 * i)
            for i in range(8)]
    fy = sb("fy", [128, D], F32, RC + 32768)
    fln = sb("fln", [128, D], F32, RC + 36864)
    fout = [sb(f"fout{i}", [128, D], F32, RC + 40960 + 4096 * i) for i in range(2)]
    fsq = sb("fsq", [128, D], F32, RC + 49152)

    ps_sc = nc.alloc_psum_tensor("ps_sc", [128, 2048], F32).ap()
    ps_t0 = nc.alloc_psum_tensor("ps_t0", [128, 512], F32).ap()
    ps_t1 = nc.alloc_psum_tensor("ps_t1", [128, 512], F32).ap()
    ps_av = nc.alloc_psum_tensor("ps_av", [128, 512], F32).ap()
    ps_x = nc.alloc_psum_tensor("ps_x", [128, 512], F32).ap()
    ps_t = [ps_t0, ps_t1]

    ld = Sem(nc, "ld")
    sg = Sem(nc, "sg")
    tm = Sem(nc, "tm")
    ve = Sem(nc, "ve")
    sc = Sem(nc, "sc")
    gp = Sem(nc, "gp")

    tens, vect, scal, gps, sync = nc.tensor, nc.vector, nc.scalar, nc.gpsimd, nc.sync

    def tap(name, ap):
        if not debug:
            return
        d = nc.dram_tensor("tap_" + name, list(ap.shape), F32,
                           kind="ExternalOutput").ap()
        gps.dma_start(out=d, in_=ap).then_inc(sg.h, 16)
        sg.n += 16
        gps.wait_ge(sg.h, sg.n)

    gp.inc(gps.memset(vec[7][:, :], HALF_PI))
    gp.inc(gps.memset(vec[8][:, :], EPS))
    gp.inc(gps.memset(ones_row[:, :], 1.0))
    gp.inc(gps.memset(neg1_row[:, :], -1.0))
    gp.inc(gps.memset(neg64_row[:, :], -64.0))
    gp.inc(gps.memset(ones_col[:, :], 1.0))
    gp.inc(gps.memset(ones2[:, :], 0.0))
    gp.inc(gps.memset(ones_pbf[:, :], 1.0))
    gp.inc(gps.iota(w[2][:, 0:128], pattern=[[-1, 128]], base=0,
                    channel_multiplier=1,
                    allow_small_or_imprecise_dtypes=True))
    gp.wait(vect)
    ve.inc(vect.tensor_scalar(ident[:, :], w[2][:, 0:128], 0.0, None,
                              op0=ALU.is_equal))
    vect.wait_ge(ve.h, ve.n)
    gp.inc(gps.memset(maskt[:, :], 0.0))
    gps.wait_ge(gp.h, gp.n)
    gp.inc(gps.affine_select(maskt[:, :], maskt[:, :], pattern=[[1, 128]],
                             compare_op=ALU.is_ge, fill=NEG, base=0,
                             channel_multiplier=-1))
    gp.inc(gps.memset(ones2[0:64, 0:1], 1.0))
    gp.inc(gps.memset(ones2[64:128, 32:33], 1.0))
    gp.wait(vect)
    ve.inc(vect.tensor_copy(ident_bf[:, :], ident[:, :]))
    ve.inc(vect.tensor_copy(maskt_bf[:, :], maskt[:, :]))
    gps.wait_ge(ve.h, ve.n)
    gp.inc(gps.iota(w[2][:, :], pattern=[[1, D]], base=0, channel_multiplier=0,
                    allow_small_or_imprecise_dtypes=True))
    for dc in range(NDC):
        if dc > 0:
            gps.wait_ge(ve.h, ve.n)
        gp.inc(gps.iota(vec[0][:, :], pattern=[[0, 1]], base=dc * 128,
                        channel_multiplier=1,
                        allow_small_or_imprecise_dtypes=True))
        gp.wait(vect)
        if dc > 0:
            vect.wait_ge(sc.h, sc.n)
        def _step(inst):
            ve.inc(inst)
            vect.wait_ge(ve.h, ve.n)
        _step(vect.tensor_scalar(w[3][:, :], w[2][:, :], vec[0][:, :], None,
                                 op0=ALU.mult))
        _step(vect.tensor_copy(c_sb[dc][:, :].bitcast(I32), w[3][:, :]))
        _step(vect.tensor_scalar(c_sb[dc][:, :].bitcast(I32),
                                 c_sb[dc][:, :].bitcast(I32), 1023, None,
                                 op0=ALU.bitwise_and))
        _step(vect.tensor_copy(w[3][:, :], c_sb[dc][:, :].bitcast(I32)))
        _step(vect.tensor_scalar(c_sb[dc][:, :], w[3][:, :], 511.5, None,
                                 op0=ALU.is_gt))
        _step(vect.scalar_tensor_tensor(w[3][:, :], c_sb[dc][:, :], -1024.0,
                                        w[3][:, :], op0=ALU.mult, op1=ALU.add))
        _step(vect.tensor_scalar(s_sb[dc][:, :], w[3][:, :], TPD_HI, None,
                                 op0=ALU.mult))
        ve.inc(vect.scalar_tensor_tensor(w[3][:, :], w[3][:, :], TPD_LO,
                                         s_sb[dc][:, :], op0=ALU.mult, op1=ALU.add))
        ve.wait(scal)
        sc.inc(scal.activation(s_sb[dc][:, :], w[3][:, :], AF.Sin))
        sc.inc(scal.activation(w[3][:, :], w[3][:, :], AF.Abs))
        scal.wait_ge(sc.h, sc.n)
        sc.inc(scal.activation(c_sb[dc][:, :], w[3][:, :], AF.Sin,
                               bias=vec[7][:, :], scale=-1.0))
    nc.all_engine_barrier()
    tap("c0", c_sb[0][:, :]); tap("s0", s_sb[0][:, :])
    tap("ident", ident[:, :]); tap("maskt", maskt[:, :])
    nc.all_engine_barrier()

    sync.dma_start(out=w[2][:, :], in_=wfr_hbm[:, :]).then_inc(ld.h, 16)
    sync.dma_start(out=w[3][:, :], in_=wfi_hbm[:, :]).then_inc(ld.h, 16)
    sync.dma_start(out=coff_s[:, :], in_=coff_hbm[:, :]).then_inc(ld.h, 16)
    ld.n += 48
    tens.wait_ge(ld.h, ld.n)
    p1rec = {}
    for a in range(8):
        pp = ps_t[a % 2]
        if a >= 2:
            tens.wait_ge(ve.h, p1rec[a - 2])
        tm.inc(tens.transpose(pp[:, 0:128], w[2][:, a * 128:(a + 1) * 128],
                              ident[:, :]))
        tm.inc(tens.transpose(pp[:, 128:256], w[3][:, a * 128:(a + 1) * 128],
                              ident[:, :]))
        tm.wait(vect)
        ve.inc(vect.tensor_copy(wrtS[:, a * 128:(a + 1) * 128], pp[:, 0:128]))
        ve.inc(vect.tensor_copy(witS[:, a * 128:(a + 1) * 128],
                                pp[:, 128:256]))
        p1rec[a] = ve.n
    tm.inc(tens.matmul(ps_x[:, 0:1], ones_row[:, :], coff_s[0:1, 0:1],
                       start=True, stop=True))
    tm.wait(vect)
    ve.inc(vect.tensor_copy(vec[1][:, :], ps_x[:, 0:1]))
    nc.all_engine_barrier()
    tap("wrt0", wrtS[:, :]); tap("wit0", witS[:, :])
    nc.all_engine_barrier()

    gp.inc(gps.iota(gfj[:, :], pattern=[[1, 128]], base=0,
                    channel_multiplier=0, allow_small_or_imprecise_dtypes=True))
    gp.inc(gps.iota(mvs[:, :], pattern=[[128, 8]], base=0,
                    channel_multiplier=1, allow_small_or_imprecise_dtypes=True))
    iota_ms = gp.n
    gp.wait(vect)
    vect.wait_ge(ve.h, ve.n)
    ve.inc(vect.tensor_scalar(gfv[:, :], gfj[:, :], vec[1][:, :], None,
                              op0=ALU.add))
    vect.wait_ge(ve.h, ve.n)
    for mc in range(8):
        if mc > 0:
            vect.wait_ge(sc.h, sc.n)

        def _gstep(inst):
            ve.inc(inst)
            vect.wait_ge(ve.h, ve.n)
        _gstep(vect.tensor_scalar(gwk[:, :], gfv[:, :], mvs[:, mc:mc + 1],
                                  None, op0=ALU.mult))
        _gstep(vect.tensor_copy(gtk[:, :].bitcast(I32), gwk[:, :]))
        _gstep(vect.tensor_scalar(gtk[:, :].bitcast(I32),
                                  gtk[:, :].bitcast(I32), 1023, None,
                                  op0=ALU.bitwise_and))
        _gstep(vect.tensor_copy(gwk[:, :], gtk[:, :].bitcast(I32)))
        _gstep(vect.tensor_scalar(gtk[:, :], gwk[:, :], 511.5, None,
                                  op0=ALU.is_gt))
        _gstep(vect.scalar_tensor_tensor(gwk[:, :], gtk[:, :], -1024.0,
                                         gwk[:, :], op0=ALU.mult, op1=ALU.add))
        _gstep(vect.tensor_scalar(gtk[:, :], gwk[:, :], TPD_HI, None,
                                  op0=ALU.mult))
        ve.inc(vect.scalar_tensor_tensor(gwk[:, :], gwk[:, :], TPD_LO,
                                         gtk[:, :], op0=ALU.mult, op1=ALU.add))
        ve.wait(scal)
        sc.inc(scal.activation(sbt[mc][:, :], gwk[:, :], AF.Sin))
        sc.inc(scal.activation(gwk[:, :], gwk[:, :], AF.Abs))
        scal.wait_ge(sc.h, sc.n)
        sc.inc(scal.activation(cbt[mc][:, :], gwk[:, :], AF.Sin,
                               bias=vec[7][:, :], scale=-1.0))
    gen_sc = sc.n

    def stage12s(dst, rhs_a, rhs_b, sign_b):
        tens.wait_ge(ve.h, p1rec[7])
        for jh in range(2):
            pa = ps_sc[:, jh * 1024:jh * 1024 + 512]
            pb = ps_sc[:, jh * 1024 + 512:jh * 1024 + 1024]
            for oc in range(8):
                tens.matmul(pa, wrtS[:, oc * 128:(oc + 1) * 128],
                            rhs_a[oc][:, jh * 512:(jh + 1) * 512],
                            start=(oc == 0), stop=(oc == 7))
            for oc in range(8):
                st = tens.matmul(pb, witS[:, oc * 128:(oc + 1) * 128],
                                 rhs_b[oc][:, jh * 512:(jh + 1) * 512],
                                 start=(oc == 0), stop=(oc == 7))
            tm.inc(st)
            tm.wait(vect)
            ve.inc(vect.tensor_scalar(w[2][:, jh * 512:(jh + 1) * 512], pb,
                                      float(sign_b), None, op0=ALU.mult))
            vect.wait_ge(ve.h, ve.n)
            ve.inc(vect.tensor_tensor(dst[:, jh * 512:(jh + 1) * 512], pa,
                                      w[2][:, jh * 512:(jh + 1) * 512],
                                      op=ALU.add))
            tens.wait_ge(ve.h, ve.n)

    stage12s(tp_sh, c_sb, s_sb, -1)
    stage12s(up_sh, s_sb, c_sb, +1)
    ve.wait(gps)
    gps.wait_ge(sg.h, sg.n)
    gps.dma_start(out=tu_in.ap()[0:128, :], in_=tp_sh[:, :]).then_inc(sg.h, 16)
    gps.dma_start(out=tu_in.ap()[128:256, :], in_=up_sh[:, :]).then_inc(sg.h, 16)
    sg.n += 32
    gps.wait_ge(sg.h, sg.n)
    gps.wait_ge(gp.h, iota_ms)
    gp.inc(gps.collective_compute(
        "AllGather", ALU.bypass, replica_groups=[list(range(8))],
        ins=[tu_in.ap().opt()], outs=[tu_full.ap().opt()]))
    cc1 = gp.n
    nc.all_engine_barrier()
    tap("tp0", tp_sh[:, :]); tap("up0", up_sh[:, :])
    nc.all_engine_barrier()

    sync.wait_ge(ld.h, ld.n)
    sync.wait_ge(gp.h, cc1)
    gps.wait_ge(gp.h, cc1)
    ld_ms, sg_ms, mm_ms = {}, {}, {}

    def _tu_load(mc):
        eng, sem, ms = (sync, ld, ld_ms) if mc % 2 == 0 else (gps, sg, sg_ms)
        if mc >= 2:
            eng.wait_ge(tm.h, mm_ms[mc - 2])
        eng.dma_start(out=tTs[mc % 2][:, :],
                      in_=tu_full.ap()[mc * 256:mc * 256 + 128, :]
                      ).then_inc(sem.h, 16)
        eng.dma_start(out=tUs[mc % 2][:, :],
                      in_=tu_full.ap()[mc * 256 + 128:mc * 256 + 256, :]
                      ).then_inc(sem.h, 16)
        sem.n += 32
        ms[mc] = sem.n

    _tu_load(0)
    _tu_load(1)
    tens.wait_ge(sc.h, gen_sc)
    tens.wait_ge(ve.h, ve.n)
    for mc in range(8):
        sem, ms = (ld, ld_ms) if mc % 2 == 0 else (sg, sg_ms)
        tens.wait_ge(sem.h, ms[mc])
        for jh in range(2):
            tens.matmul(ps_t[jh][:, :], cbt[mc][:, :],
                        tTs[mc % 2][:, jh * 512:(jh + 1) * 512],
                        start=(mc == 0), stop=False)
            st = tens.matmul(ps_t[jh][:, :], sbt[mc][:, :],
                             tUs[mc % 2][:, jh * 512:(jh + 1) * 512],
                             start=False, stop=(mc == 7))
        tm.inc(st)
        mm_ms[mc] = tm.n
        if mc + 2 < 8:
            _tu_load(mc + 2)
    tm.wait(scal)
    for jh in range(2):
        sc.inc(scal.mul(msh[:, jh * 512:(jh + 1) * 512], ps_t[jh][:, :],
                        1.0 / 1024.0))
    sc.wait(gps)
    gps.wait_ge(sg.h, sg.n)
    gps.dma_start(out=m_in.ap()[:, :], in_=msh[:, :]).then_inc(sg.h, 16)
    sg.n += 16
    gps.wait_ge(sg.h, sg.n)
    gp.inc(gps.collective_compute(
        "AllGather", ALU.bypass, replica_groups=[list(range(8))],
        ins=[m_in.ap().opt()], outs=[m_full.ap().opt()]))
    cc2 = gp.n
    gps.wait_ge(gp.h, cc2)
    nc.all_engine_barrier()
    gp.inc(gps.memset(v_hi[:, :], 1.0))
    gp.inc(gps.memset(v_lo[:, :], 0.0))
    vall_ms = gp.n
    gps.wait_ge(gp.h, vall_ms)

    VS_EVEN = (0, 1, 2, 3, 4)
    VS_ODD = (5, 6, 9, 10, 11)

    def layernorm(src, dst, sq, vs=VS_EVEN, pre=None):
        rsum, negmean, sumsq, stdv, rstd = (vec[i] for i in vs)
        if pre is None:
            vect.wait_ge(sc.h, sc.n)
            vect.wait_ge(ve.h, ve.n)
            scal.wait_ge(ve.h, ve.n)
        else:
            for (h_, v_) in pre:
                vect.wait_ge(h_, v_)
                scal.wait_ge(h_, v_)
        ve.inc(vect.tensor_reduce(rsum[:, :], src, axis=AX.X, op=ALU.add))
        rsum_done = ve.n
        scal.wait_ge(ve.h, rsum_done)
        sc.inc(scal.mul(negmean[:, :], rsum[:, :], -1.0 / D))
        scal.wait_ge(sc.h, sc.n)
        sc.inc(scal.add(dst, src, negmean[:, :]))
        scal.wait_ge(sc.h, sc.n)
        sc.inc(scal.activation(sq, dst, AF.Square, accum_out=sumsq[:, :]))
        scal.wait_ge(sc.h, sc.n)
        sc.inc(scal.activation(stdv[:, :], sumsq[:, :], AF.Sqrt,
                               bias=vec[8][:, :], scale=1.0 / D))
        stdv_done = sc.n
        vect.wait_ge(sc.h, stdv_done)
        ve.inc(vect.reciprocal(rstd[:, :], stdv[:, :]))
        vect.wait_ge(ve.h, ve.n)
        ve.inc(vect.tensor_scalar(dst, dst, rstd[:, :], None, op0=ALU.mult))
        return ve.n

    for tt in range(2):
        sync.dma_start(out=w[tt % 2][:, :],
                       in_=x_hbm[tt * 128:(tt + 1) * 128, :]).then_inc(ld.h, 16)
        ld.n += 16
    lnA = [sb("lnA0", [128, D], F32, RC), sb("lnA1", [128, D], F32, RC + 4096)]
    ln_done = {}
    tr_done = {}
    ev_A = {}

    def a_stage2(tt):
        tens.wait_ge(ve.h, ln_done[tt])
        base = (tt % 2) * 1024
        if tt - 2 in ev_A:
            tens.wait_ge(ve.h, ev_A[tt - 2])
        for dc in range(NDC):
            tm.inc(tens.transpose(
                ps_sc[:, base + dc * 128:base + (dc + 1) * 128],
                lnA[tt % 2][:, dc * 128:(dc + 1) * 128], ident[:, :]))
        tm.wait(scal)
        sl = slice(tt * 128, (tt + 1) * 128)
        for dc in range(NDC):
            sc.inc(scal.copy(lnfm_hi[dc][:, sl],
                             ps_sc[:, base + dc * 128:base + (dc + 1) * 128]))
        sc.wait(vect)
        for dc in range(NDC):
            ve.inc(vect.tensor_tensor(
                lnfm_lo[dc][:, sl],
                ps_sc[:, base + dc * 128:base + (dc + 1) * 128],
                lnfm_hi[dc][:, sl], op=ALU.subtract))
        ev_A[tt] = ve.n
        tr_done[tt] = tm.n

    s1_done, s2_first, s2_done, s3_done, x_free = {}, {}, {}, {}, {}

    def ln_s1(t):
        rsum = vec[(VS_EVEN if t % 2 == 0 else VS_ODD)[0]]
        vect.wait_ge(ld.h, ld.n)
        if t - 2 in s2_first:
            vect.wait_ge(sc.h, s2_first[t - 2])
        ve.inc(vect.tensor_reduce(rsum[:, :], w[t % 3][:, :], axis=AX.X,
                                  op=ALU.add))
        s1_done[t] = ve.n

    def ln_s2(t):
        vs = VS_EVEN if t % 2 == 0 else VS_ODD
        rsum, negmean, sumsq, stdv = (vec[vs[i]] for i in range(4))
        scal.wait_ge(ve.h, s1_done[t])
        if t - 2 in s3_done:
            scal.wait_ge(ve.h, s3_done[t - 2])
        if t - 2 in tr_done:
            scal.wait_ge(tm.h, tr_done[t - 2])
        sc.inc(scal.mul(negmean[:, :], rsum[:, :], -1.0 / D))
        s2_first[t] = sc.n
        scal.wait_ge(sc.h, sc.n)
        sc.inc(scal.add(lnA[t % 2][:, :], w[t % 3][:, :], negmean[:, :]))
        x_free[t] = sc.n
        scal.wait_ge(sc.h, sc.n)
        sc.inc(scal.activation(w[3][:, :], lnA[t % 2][:, :], AF.Square,
                               accum_out=sumsq[:, :]))
        scal.wait_ge(sc.h, sc.n)
        sc.inc(scal.activation(stdv[:, :], sumsq[:, :], AF.Sqrt,
                               bias=vec[8][:, :], scale=1.0 / D))
        s2_done[t] = sc.n
        if t + 3 < NTT:
            sync.wait_ge(ld.h, ld.n)
            sync.wait_ge(sc.h, x_free[t])
            sync.wait_ge(ve.h, s1_done[t])
            sync.dma_start(out=w[t % 3][:, :],
                           in_=x_hbm[(t + 3) * 128:(t + 4) * 128, :]
                           ).then_inc(ld.h, 16)
            ld.n += 16

    def ln_s3(t):
        vs = VS_EVEN if t % 2 == 0 else VS_ODD
        stdv, rstd = vec[vs[3]], vec[vs[4]]
        vect.wait_ge(sc.h, s2_done[t])
        ve.inc(vect.reciprocal(rstd[:, :], stdv[:, :]))
        vect.wait_ge(ve.h, ve.n)
        ve.inc(vect.tensor_scalar(lnA[t % 2][:, :], lnA[t % 2][:, :],
                                  rstd[:, :], None, op0=ALU.mult))
        ln_done[t] = ve.n

    sync.dma_start(out=w[2][:, :],
                   in_=x_hbm[2 * 128:3 * 128, :]).then_inc(ld.h, 16)
    ld.n += 16
    for k in range(NTT + 2):
        if k - 2 >= 0:
            ln_s3(k - 2)
        if k < NTT:
            ln_s1(k)
        if k - 1 >= 0 and k - 1 < NTT:
            ln_s2(k - 1)
        if k - 2 >= 0:
            a_stage2(k - 2)
    nc.all_engine_barrier()
    tap("ln1fmh0", lnfm_hi[0][:, :])
    nc.all_engine_barrier()

    def load_pair_weights(p):
        sync.dma_start(out=w[0][:, :],
                       in_=wq_hbm[p * 128:(p + 1) * 128, :]).then_inc(ld.h, 16)
        sync.dma_start(out=w[1][:, :],
                       in_=wk_hbm[p * 128:(p + 1) * 128, :]).then_inc(ld.h, 16)
        sync.dma_start(out=w[2][:, :],
                       in_=wv_hbm[p * 128:(p + 1) * 128, :]).then_inc(ld.h, 16)
        sync.dma_start(
            out=w[3][:, :].rearrange("p (a c) -> p a c", a=8),
            in_=wo_hbm[:, p * 128:(p + 1) * 128].rearrange(
                "(a p) c -> p a c", p=128)).then_inc(ld.h, 16)
        ld.n += 64
        return ld.n

    SB4 = [ps_t0, ps_t1, ps_av, ps_x]
    AVS = {"n": 0, "meta": {}, "sctm": {}, "exp": {}, "msk": {}, "avtm": {},
           "spl": {}, "bfree": {}, "pend": [], "pend2": [], "eva": {},
           "ev": {}, "r": 0, "slotrec": {}, "sbslot": {}}

    def sb4_take():
        slot = AVS["r"] % 4
        AVS["r"] += 1
        rec = AVS["slotrec"].get(slot)
        if rec is not None:
            kind, val = rec
            if kind == "exp":
                tens.wait_ge(sc.h, val)
            else:
                tens.wait_ge(ve.h, val)
        return slot

    def av_scores(i):
        c, qg, hh = AVS["meta"][i]
        hsl = slice(hh * 64, (hh + 1) * 64)
        qsl = slice(qg * 512, (qg + 1) * 512)
        csl = slice(c * 128, (c + 1) * 128)
        slot = sb4_take()
        AVS["sbslot"][i] = slot
        sb_ = SB4[slot][:, 0:512]
        tens.matmul(sb_, k_hi[hsl, csl], q_hi[hsl, qsl], start=True,
                    stop=False)
        tens.matmul(sb_, k_hi[hsl, csl], q_lo[hsl, qsl], start=False,
                    stop=False)
        st = tens.matmul(sb_, k_lo[hsl, csl], q_hi[hsl, qsl], start=False,
                         stop=True)
        tm.inc(st)
        AVS["sctm"][i] = tm.n

    def av_exp(i):
        c, qg, hh = AVS["meta"][i]
        scal.wait_ge(tm.h, AVS["sctm"][i])
        if i - 3 in AVS["spl"]:
            scal.wait_ge(ve.h, AVS["spl"][i - 3])
        sc.inc(scal.activation(pt3[i % 3][:, 0:512],
                               SB4[AVS["sbslot"][i]][:, 0:512],
                               AF.Exp, bias=vec[10 + hh][:, :], scale=1.0))
        AVS["exp"][i] = sc.n
        AVS["slotrec"][AVS["sbslot"][i]] = ("exp", sc.n)
        if c // 4 == qg:
            gps.wait_ge(sc.h, AVS["exp"][i])
            gps.wait_ge(gp.h, gp.n)
            gp.inc(gps.affine_select(
                pt3[i % 3][:, 0:512], pt3[i % 3][:, 0:512],
                pattern=[[1, 512]], compare_op=ALU.is_ge, fill=0.0,
                base=-128 * (c - 4 * qg), channel_multiplier=-1))
            AVS["msk"][i] = gp.n

    def av_split(i):
        scal.wait_ge(sc.h, AVS["exp"][i])
        if i in AVS["msk"]:
            scal.wait_ge(gp.h, AVS["msk"][i])
        if i - 3 in AVS["avtm"]:
            scal.wait_ge(tm.h, AVS["avtm"][i - 3])
        sc.inc(scal.copy(pth[i % 3][:, 0:512], pt3[i % 3][:, 0:512]))
        hi_sc = sc.n
        vect.wait_ge(sc.h, hi_sc)
        if i - 3 in AVS["avtm"]:
            vect.wait_ge(tm.h, AVS["avtm"][i - 3])
        ve.inc(vect.tensor_tensor(ptl[i % 3][:, 0:512], pt3[i % 3][:, 0:512],
                                  pth[i % 3][:, 0:512], op=ALU.subtract))
        AVS["spl"][i] = ve.n

    def av_av(i):
        c, qg, hh = AVS["meta"][i]
        tens.wait_ge(ve.h, AVS["spl"][i])
        if c == 0 and qg in AVS["bfree"]:
            bf_sc, bf_ve = AVS["bfree"][qg]
            tens.wait_ge(sc.h, bf_sc)
            tens.wait_ge(ve.h, bf_ve)
        if hh == 0:
            vh = v_hi[:, c * 196:c * 196 + 65]
            vl = v_lo[:, c * 196:c * 196 + 65]
            np_out = 65
        else:
            vh = v_hi[:, c * 196 + 66:c * 196 + 194]
            vl = v_lo[:, c * 196 + 66:c * 196 + 194]
            np_out = 128
        bank = ps_sc[0:np_out, qg * 512:(qg + 1) * 512]
        pth_, ptl_ = pth[i % 3][:, 0:512], ptl[i % 3][:, 0:512]
        tens.matmul(bank, vh, pth_, start=(c == 0), stop=False)
        tens.matmul(bank, vh, ptl_, start=False, stop=False)
        st = tens.matmul(bank, vl, pth_, start=False, stop=(c == 4 * qg + 3))
        tm.inc(st)
        AVS["avtm"][i] = tm.n
        if c == 4 * qg + 3:
            AVS["pend"].append((i, AVS["n"]))

    def av_evac_a(i):
        c, qg, hh = AVS["meta"][i]
        den_p = 64 if hh == 0 else 32
        fsl = slice(hh * 64, (hh + 1) * 64)
        prev = AVS["ev"]
        qpsl = slice(qg * 512, (qg + 1) * 512)
        scal.wait_ge(tm.h, AVS["avtm"][i])
        if "mult" in prev:
            scal.wait_ge(ve.h, prev["mult"])
        if "rhl" in prev:
            scal.wait_ge(ve.h, prev["rhl"])
        sc.inc(scal.copy(slotA[fsl, 0:512], ps_sc[fsl, qpsl]))
        copy_sc = sc.n
        vect.wait_ge(tm.h, AVS["avtm"][i])
        if "bcast" in prev:
            vect.wait_ge(tm.h, prev["bcast"])
        if "mult" in prev:
            vect.wait_ge(ve.h, prev["mult"])
        ve.inc(vect.reciprocal(slotA[den_p:den_p + 1, 0:512],
                               ps_sc[den_p:den_p + 1, qpsl]))
        vect.wait_ge(ve.h, ve.n)
        ve.inc(vect.tensor_copy(rh[den_p:den_p + 1, 0:512],
                                slotA[den_p:den_p + 1, 0:512]))
        vect.wait_ge(ve.h, ve.n)
        ve.inc(vect.tensor_tensor(rl[den_p:den_p + 1, 0:512],
                                  slotA[den_p:den_p + 1, 0:512],
                                  rh[den_p:den_p + 1, 0:512],
                                  op=ALU.subtract))
        rhl_ve = ve.n
        AVS["bfree"][qg] = (copy_sc, rhl_ve)
        AVS["eva"][i] = (rhl_ve, copy_sc)
        prev["rhl"] = rhl_ve

    def av_evac_b(i):
        c, qg, hh = AVS["meta"][i]
        den_p = 64 if hh == 0 else 32
        fsl = slice(hh * 64, (hh + 1) * 64)
        prev = AVS["ev"]
        qpsl = slice(qg * 512, (qg + 1) * 512)
        rhl_ve, copy_sc = AVS["eva"][i]
        slot = sb4_take()
        tens.wait_ge(ve.h, rhl_ve)
        tens.matmul(SB4[slot][:, 0:512], ones_pbf[den_p:den_p + 1, 0:128],
                    rh[den_p:den_p + 1, 0:512], start=True, stop=False)
        st = tens.matmul(SB4[slot][:, 0:512],
                         ones_pbf[den_p:den_p + 1, 0:128],
                         rl[den_p:den_p + 1, 0:512], start=False, stop=True)
        tm.inc(st)
        bcast_tm = tm.n
        vect.wait_ge(tm.h, bcast_tm)
        vect.wait_ge(sc.h, copy_sc)
        if "hi" in prev:
            vect.wait_ge(sc.h, prev["hi"])
        if "lo" in prev:
            vect.wait_ge(ve.h, prev["lo"])
        ve.inc(vect.tensor_tensor(hof32[fsl, 0:512], slotA[fsl, 0:512],
                                  SB4[slot][fsl, 0:512], op=ALU.mult))
        mult_ve = ve.n
        AVS["slotrec"][slot] = ("mult", mult_ve)
        scal.wait_ge(ve.h, mult_ve)
        sc.inc(scal.copy(hofm_hi[fsl, qpsl], hof32[fsl, 0:512]))
        hi_sc = sc.n
        vect.wait_ge(sc.h, hi_sc)
        ve.inc(vect.tensor_tensor(hofm_lo[fsl, qpsl], hof32[fsl, 0:512],
                                  hofm_hi[fsl, qpsl], op=ALU.subtract))
        prev.update({"bcast": bcast_tm, "mult": mult_ve,
                     "hi": hi_sc, "lo": ve.n})

    def av_drain(force=False):
        if force:
            while AVS["pend2"]:
                i, _ = AVS["pend2"].pop(0)
                av_evac_b(i)
            while AVS["pend"]:
                i, _ = AVS["pend"].pop(0)
                av_evac_a(i)
                av_evac_b(i)
            return
        while AVS["pend"] and AVS["pend"][0][1] <= AVS["n"] - 1:
            i, _ = AVS["pend"].pop(0)
            av_evac_a(i)
            AVS["pend2"].append((i, AVS["n"]))
        while AVS["pend2"] and AVS["pend2"][0][1] <= AVS["n"] - 2:
            i, _ = AVS["pend2"].pop(0)
            av_evac_b(i)

    wrec = load_pair_weights(0)
    accrec = {}
    for p in range(NPAIR):
        tens.wait_ge(ld.h, wrec)

        def transpose_split(nat, dst_hi, dst_lo):
            tens.wait_ge(ve.h, ve.n)
            tens.wait_ge(sc.h, sc.n)
            for half in range(2):
                pp = ps_t[half]
                for a4 in range(4):
                    a = half * 4 + a4
                    tm.inc(tens.transpose(pp[:, a4 * 128:(a4 + 1) * 128],
                                          nat[:, a * 128:(a + 1) * 128],
                                          ident[:, :]))
                tm.wait(scal)
                sl = slice(half * 512, (half + 1) * 512)
                sc.inc(scal.copy(dst_hi[:, sl], pp[:, 0:512]))
                sc.wait(vect)
                ve.inc(vect.tensor_tensor(dst_lo[:, sl], pp[:, 0:512],
                                          dst_hi[:, sl], op=ALU.subtract))

        evq = {}
        for side, nat in enumerate((w[0], w[1])):
            transpose_split(nat, wt_hi, wt_lo)
            ve.wait(tens)
            dst_hi, dst_lo = (q_hi, q_lo) if side == 0 else (k_hi, k_lo)
            for tc in range(4):
                g = side * 4 + tc
                pp = ps_sc[:, (g % 2) * 512:(g % 2) * 512 + 512]
                tsl = slice(tc * 512, (tc + 1) * 512)
                if g >= 2:
                    tens.wait_ge(ve.h, evq[g - 2])
                for dc in range(NDC):
                    dsl = slice(dc * 128, (dc + 1) * 128)
                    tens.matmul(pp, wt_hi[:, dsl], lnfm_hi[dc][:, tsl],
                                start=(dc == 0), stop=False)
                    tens.matmul(pp, wt_hi[:, dsl], lnfm_lo[dc][:, tsl],
                                start=False, stop=False)
                    st = tens.matmul(pp, wt_lo[:, dsl], lnfm_hi[dc][:, tsl],
                                     start=False, stop=(dc == NDC - 1))
                tm.inc(st)
                tm.wait(scal)
                if side == 0:
                    sc.inc(scal.mul(dst_hi[:, tsl], pp, 0.125))
                    sc.wait(vect)
                    ve.inc(vect.scalar_tensor_tensor(
                        dst_lo[:, tsl], pp, 0.125, dst_hi[:, tsl],
                        op0=ALU.mult, op1=ALU.subtract))
                else:
                    sc.inc(scal.copy(dst_hi[:, tsl], pp))
                    sc.wait(vect)
                    ve.inc(vect.tensor_tensor(dst_lo[:, tsl], pp,
                                              dst_hi[:, tsl], op=ALU.subtract))
                evq[g] = ve.n

        transpose_split(w[2], wt_hi, wt_lo)
        ve.wait(tens)
        evv = {}
        evs = {}
        for tt in range(NTT):
            pp = ps_t[tt % 2]
            if tt >= 2:
                tens.wait_ge(ve.h, evv[tt - 2])
                tens.wait_ge(sc.h, evs[tt - 2])
            for dc in range(NDC):
                dsl = slice(dc * 128, (dc + 1) * 128)
                tsl = slice(tt * 128, (tt + 1) * 128)
                tens.matmul(pp[:, 0:128], lnfm_hi[dc][:, tsl], wt_hi[:, dsl],
                            start=(dc == 0), stop=False)
                tens.matmul(pp[:, 0:128], lnfm_hi[dc][:, tsl], wt_lo[:, dsl],
                            start=False, stop=False)
                st = tens.matmul(pp[:, 0:128], lnfm_lo[dc][:, tsl],
                                 wt_hi[:, dsl], start=False,
                                 stop=(dc == NDC - 1))
            tm.inc(st)
            tm.wait(scal)
            tm.wait(vect)
            if p == 0 and tt == 0:
                scal.wait_ge(gp.h, vall_ms)
                vect.wait_ge(gp.h, vall_ms)
            sc.inc(scal.copy(v_hi[:, tt * 196:tt * 196 + 64], pp[:, 0:64]))
            sc.inc(scal.copy(v_hi[:, tt * 196 + 130:tt * 196 + 194],
                             pp[:, 64:128]))
            evs[tt] = sc.n
            sc.wait(vect)
            ve.inc(vect.tensor_tensor(v_lo[:, tt * 196:tt * 196 + 64],
                                      pp[:, 0:64],
                                      v_hi[:, tt * 196:tt * 196 + 64],
                                      op=ALU.subtract))
            ve.inc(vect.tensor_tensor(v_lo[:, tt * 196 + 130:tt * 196 + 194],
                                      pp[:, 64:128],
                                      v_hi[:, tt * 196 + 130:tt * 196 + 194],
                                      op=ALU.subtract))
            evv[tt] = ve.n

        if p == 0:
            scal.wait_ge(sg.h, sg.n)
            vect.wait_ge(sg.h, sg.n)
        transpose_split(w[3], wot_hi, wot_lo)
        if p + 1 < NPAIR:
            sync.wait_ge(tm.h, tm.n)
            wrec = load_pair_weights(p + 1)

        vect.wait_ge(sc.h, sc.n)
        for side, srcb in enumerate((q_hi, k_hi)):
            scal.wait_ge(tm.h, tm.n)
            scal.wait_ge(ve.h, ve.n)
            sc.inc(scal.activation(sqscr[:, :], srcb[:, :], AF.Square))
            tens.wait_ge(sc.h, sc.n)
            tens.wait_ge(ve.h, ve.n)
            last = None
            for g in range(4):
                last = tens.matmul(ps_sc[0:33, g * 512:(g + 1) * 512],
                                   ones2[:, :], sqscr[:, g * 512:(g + 1) * 512],
                                   start=True, stop=True)
            tm.inc(last)
            tm.wait(vect)
            ve.inc(vect.tensor_reduce(bmx[0:33, side:side + 1],
                                      ps_sc[0:33, :], axis=AX.X, op=ALU.max))
            vect.wait_ge(ve.h, ve.n)
        ve.inc(vect.tensor_tensor(bmx[0:33, 0:1], bmx[0:33, 0:1],
                                  bmx[0:33, 1:2], op=ALU.mult))
        ve.wait(scal)
        sc.inc(scal.activation(bmx[0:33, 1:2], bmx[0:33, 0:1], AF.Sqrt))
        gps.wait_ge(tm.h, tm.n)
        gp.inc(gps.memset(sqscr[:, 0:128], -1.0))
        negms = gp.n
        sc.wait(tens)
        tens.wait_ge(gp.h, negms)
        for hh2 in range(2):
            xp = hh2 * 32
            tm.inc(tens.matmul(ps_t1[:, 0:1], sqscr[xp:xp + 1, 0:128],
                               bmx[xp:xp + 1, 1:2], start=True, stop=True))
            tm.wait(vect)
            ve.inc(vect.tensor_copy(vec[10 + hh2][:, :], ps_t1[:, 0:1]))
            tens.wait_ge(ve.h, ve.n)
        nc.all_engine_barrier()
        if debug and p == 0:
            tap("qh0", q_hi[:, :]); tap("ql0", q_lo[:, :])
            tap("kh0", k_hi[:, :]); tap("v00", v_hi[:, :])
            tap("bv0", vec[10][:, :])
            nc.all_engine_barrier()

        pb = AVS["n"]
        for hh in range(2):
            for c in range(16):
                for qg in range(c // 4, 4):
                    i = AVS["n"]
                    AVS["n"] += 1
                    AVS["meta"][i] = (c, qg, hh)
                    if AV_BATCH:
                        if i % 2 == 1:
                            for k in (i - 7, i - 6):
                                if k >= pb:
                                    av_av(k)
                            for k in (i - 5, i - 4):
                                if k >= pb:
                                    av_split(k)
                            for k in (i - 3, i - 2):
                                if k >= pb:
                                    av_exp(k)
                            av_scores(i - 1)
                            av_scores(i)
                            av_drain()
                    else:
                        av_scores(i)
                        if i - 3 >= pb:
                            av_exp(i - 3)
                        if i - 7 >= pb:
                            av_av(i - 7)
                        if i - 5 >= pb:
                            av_split(i - 5)
                        av_drain()
        last = AVS["n"] - 1
        if AV_BATCH:
            av_av(last - 5)
            av_av(last - 4)
            av_split(last - 3)
            av_split(last - 2)
            av_exp(last - 1)
            av_exp(last)
            av_av(last - 3)
            av_av(last - 2)
            av_split(last - 1)
            av_split(last)
            av_av(last - 1)
            av_av(last)
        else:
            av_av(last - 6)
            av_av(last - 5)
            av_split(last - 4)
            av_split(last - 3)
            av_exp(last - 2)
            av_exp(last - 1)
            av_exp(last)
            av_av(last - 4)
            av_av(last - 3)
            av_split(last - 2)
            av_split(last - 1)
            av_av(last - 2)
            av_split(last)
            av_av(last - 1)
            av_av(last)
        av_drain(force=True)
        nc.all_engine_barrier()
        if debug and p == 0:
            tap("hofmh0", hofm_hi[:, :]); tap("hofml0", hofm_lo[:, :])
            nc.all_engine_barrier()

        evrec = {}
        for tt in range(NTT):
            for oc in range(2):
                g = tt * 2 + oc
                pp = (ps_av, ps_x)[oc]
                tsl = slice(tt * 128, (tt + 1) * 128)
                osl = slice(oc * 512, (oc + 1) * 512)
                if g >= 2:
                    tens.wait_ge(ve.h, evrec[g - 2])
                tens.matmul(pp[:, :], hofm_hi[:, tsl], wot_hi[:, osl],
                            start=True, stop=False)
                tens.matmul(pp[:, :], hofm_hi[:, tsl], wot_lo[:, osl],
                            start=False, stop=False)
                st = tens.matmul(pp[:, :], hofm_lo[:, tsl], wot_hi[:, osl],
                                 start=False, stop=True)
                tm.inc(st)
                tm.wait(vect)
                if p > 0:
                    vect.wait_ge(ve.h, accrec[g])
                dst = acc_tm[tt][:, oc * 512:(oc + 1) * 512]
                if p == 0:
                    ve.inc(vect.tensor_copy(dst, pp[:, :]))
                else:
                    ve.inc(vect.tensor_tensor(dst, dst, pp[:, :], op=ALU.add))
                evrec[g] = ve.n
                accrec[g] = ve.n
        nc.all_engine_barrier()

    tap("acc0", acc_tm[0][:, :]); tap("acc15", acc_tm[15][:, :])
    nc.all_engine_barrier()
    dD = [sb("dD0", [128, D], F32, RC + 32768), sb("dD1", [128, D], F32, RC + 36864)]
    dE = [sb("dE0", [128, D], F32, RC + 40960), w[2]]
    lnE_hi = [sb("lnEh0", [128, 1024], BF16, RC + 45056),
              sb("lnEh1", [128, 1024], BF16, RC + 49152)]
    lnE_lo = [sb("lnEl0", [128, 1024], BF16, RC + 47104),
              sb("lnEl1", [128, 1024], BF16, RC + 51200)]
    obuf = [sb(f"obuf{i}", [128, D], F32, RC + 45056 + 4096 * i) for i in range(2)]
    mstage = dD[0]

    sync.wait_ge(sg.h, sg.n)
    sync.wait_ge(gp.h, cc2)
    sync.wait_ge(ld.h, ld.n)
    gps.wait_ge(sg.h, sg.n)
    gps.dma_start(out=bias_bc[0:1, :], in_=bias_hbm[:, :]).then_inc(sg.h, 16)
    sg.n += 16
    bias_ld = sg.n
    mstages = [dD[0], dD[1]]
    gps.wait_ge(sg.h, bias_ld)
    m_ms, spl_rec = {}, {}

    def _m_split(dc):
        eng_sem = ld if dc % 2 == 0 else sg
        scal.wait_ge(eng_sem.h, m_ms[dc])
        sc.inc(scal.copy(m_hi[dc][:, :], mstages[dc % 2][:, :]))
        sc.wait(vect)
        ve.inc(vect.tensor_tensor(m_lo[dc][:, :], mstages[dc % 2][:, :],
                                  m_hi[dc][:, :], op=ALU.subtract))
        spl_rec[dc] = ve.n

    for dc in range(NDC):
        eng, sem = (sync, ld) if dc % 2 == 0 else (gps, sg)
        if dc >= 2:
            eng.wait_ge(ve.h, spl_rec[dc - 2])
        eng.dma_start(out=mstages[dc % 2][:, :],
                      in_=m_full.ap()[dc * 128:(dc + 1) * 128, :]
                      ).then_inc(sem.h, 16)
        sem.n += 16
        m_ms[dc] = sem.n
        if dc >= 1:
            _m_split(dc - 1)
    _m_split(NDC - 1)
    tens.wait_ge(sg.h, bias_ld)
    for oc in range(2):
        tm.inc(tens.matmul(ps_x[:, :], ones_row[:, :],
                           bias_bc[0:1, oc * 512:(oc + 1) * 512],
                           start=True, stop=True))
        tm.wait(vect)
        ve.inc(vect.tensor_copy(bias_bc[:, oc * 512:(oc + 1) * 512], ps_x[:, :]))
        tens.wait_ge(ve.h, ve.n)
    sync.wait_ge(ld.h, ld.n)
    for tt in range(2):
        sync.dma_start(out=w[tt % 2][:, :],
                       in_=x_hbm[tt * 128:(tt + 1) * 128, :]).then_inc(ld.h, 16)
        ld.n += 16

    x1_done = {}
    lnE_done = {}
    out_dma = {}
    fmm_done = {}
    fy_ve = {}
    ev_done = {}
    ob_done = {}

    def def_s12(tt):
        r = tt % 2
        pre = [(ld.h, ld.n)]
        if tt >= 2:
            pre.append((tm.h, fmm_done.get(tt - 2, 0)))
        layernorm(acc_tm[tt][:, :], dD[r][:, :], w[3][:, :],
                  VS_EVEN if r == 0 else VS_ODD, pre=pre)
        vect.wait_ge(ve.h, ve.n)
        ve.inc(vect.tensor_tensor(x1_tm[tt][:, :], dD[r][:, :],
                                  w[tt % 2][:, :], op=ALU.add))
        x1_done[tt] = ve.n
        if tt + 2 < NTT:
            sync.wait_ge(ve.h, x1_done[tt])
            sync.dma_start(out=w[tt % 2][:, :],
                           in_=x_hbm[(tt + 2) * 128:(tt + 3) * 128, :]
                           ).then_inc(ld.h, 16)
            ld.n += 16
        layernorm(x1_tm[tt][:, :], dE[r][:, :], w[3][:, :],
                  VS_EVEN if r == 0 else VS_ODD,
                  pre=[(ve.h, x1_done[tt])])
        lnE_done[tt] = ve.n

    def def_s3a(tt):
        r = tt % 2
        tens.wait_ge(ve.h, lnE_done[tt])
        if tt >= 2:
            scal.wait_ge(sg.h, out_dma[tt - 2])
            vect.wait_ge(sg.h, out_dma[tt - 2])
            tens.wait_ge(ve.h, ob_done[tt - 2])
        for half in range(2):
            pp = ps_t[half]
            if (tt - 1, half) in ev_done:
                tens.wait_ge(ve.h, ev_done[(tt - 1, half)])
            for dc4 in range(4):
                dc = half * 4 + dc4
                tm.inc(tens.transpose(pp[:, dc4 * 128:(dc4 + 1) * 128],
                                      dE[r][:, dc * 128:(dc + 1) * 128],
                                      ident[:, :]))
            tm.wait(scal)
            hsl = slice(half * 512, (half + 1) * 512)
            sc.inc(scal.copy(lnE_hi[r][:, hsl], pp[:, 0:512]))
            sc.wait(vect)
            ve.inc(vect.tensor_tensor(lnE_lo[r][:, hsl], pp[:, 0:512],
                                      lnE_hi[r][:, hsl], op=ALU.subtract))
            ev_done[(tt, half)] = ve.n

    def def_s3b(tt):
        r = tt % 2
        fy_t = dD[r]
        for oc in range(2):
            pp = ps_sc[:, (oc + 2 * (tt % 2)) * 512:
                       (oc + 2 * (tt % 2)) * 512 + 512]
            osl = slice(oc * 512, (oc + 1) * 512)
            tens.wait_ge(ve.h, ev_done[(tt, 1)])
            for dc in range(NDC):
                dsl = slice(dc * 128, (dc + 1) * 128)
                tens.matmul(pp, lnE_hi[r][:, dsl], m_hi[dc][:, osl],
                            start=(dc == 0), stop=False)
                tens.matmul(pp, lnE_hi[r][:, dsl], m_lo[dc][:, osl],
                            start=False, stop=False)
                st = tens.matmul(pp, lnE_lo[r][:, dsl], m_hi[dc][:, osl],
                                 start=False, stop=(dc == 7))
            tm.inc(st)
            tm.wait(vect)
            ve.inc(vect.tensor_tensor(fy_t[:, osl], pp,
                                      bias_bc[:, osl], op=ALU.add))
        fmm_done[tt] = tm.n
        fy_ve[tt] = ve.n

    def def_s3b2(tt):
        r = tt % 2
        fy_t = dD[r]
        layernorm(fy_t[:, :], dE[r][:, :], w[3][:, :],
                  VS_EVEN if r == 0 else VS_ODD, pre=[(ve.h, fy_ve[tt])])
        ob = obuf[r]
        if tt >= 2:
            vect.wait_ge(sg.h, sg.n)
        vect.wait_ge(ve.h, ve.n)
        ve.inc(vect.tensor_tensor(ob[:, :], dE[r][:, :], x1_tm[tt][:, :],
                                  op=ALU.add))
        ob_done[tt] = ve.n
        ve.wait(gps)
        gps.wait_ge(sg.h, sg.n)
        gps.dma_start(out=out_hbm[tt * 128:(tt + 1) * 128, :],
                      in_=ob[:, :]).then_inc(sg.h, 16)
        sg.n += 16
        out_dma[tt] = sg.n

    for tt in range(NTT):
        if tt >= 1:
            def_s3a(tt - 1)
        def_s12(tt)
        if tt >= 1:
            def_s3b(tt - 1)
            def_s3b2(tt - 1)
    def_s3a(NTT - 1)
    def_s3b(NTT - 1)
    def_s3b2(NTT - 1)
    gps.wait_ge(sg.h, sg.n)
    return nc


_NC_CACHE = None


def kernel(L0_attn_x, L0_attn_wq, L0_attn_wk, L0_attn_wv, L0_attn_wo,
           L0_ffn_w_freq_real, L0_ffn_w_freq_imag, L0_ffn_bias):
    global _NC_CACHE
    x = np.ascontiguousarray(np.asarray(L0_attn_x, dtype=np.float32))
    wq = np.ascontiguousarray(np.asarray(L0_attn_wq, np.float32).reshape(H * DH, D))
    wk = np.ascontiguousarray(np.asarray(L0_attn_wk, np.float32).reshape(H * DH, D))
    wv = np.ascontiguousarray(np.asarray(L0_attn_wv, np.float32).reshape(H * DH, D))
    wo = np.ascontiguousarray(np.asarray(L0_attn_wo, np.float32))
    wfr = np.ascontiguousarray(np.asarray(L0_ffn_w_freq_real, np.float32))
    wfi = np.ascontiguousarray(np.asarray(L0_ffn_w_freq_imag, np.float32))
    bias = np.ascontiguousarray(np.asarray(L0_ffn_bias, np.float32).reshape(1, D))
    if _NC_CACHE is None:
        _NC_CACHE = build_nc()
    nc = _NC_CACHE
    in_maps = [{"x": np.ascontiguousarray(x[b]), "wq": wq, "wk": wk, "wv": wv,
                "wo": wo,
                "wfr": np.ascontiguousarray(wfr[b * 128:(b + 1) * 128]),
                "wfi": np.ascontiguousarray(wfi[b * 128:(b + 1) * 128]),
                "coff": np.full((1, 1), b * 128, np.float32),
                "bias": bias}
               for b in range(8)]
    res = run_bass_kernel_spmd(nc, in_maps, core_ids=list(range(8)))
    return np.stack([res.results[b]["out"] for b in range(8)], axis=0)
